# version 03
# baseline (speedup 1.0000x reference)
import numpy as np
import ml_dtypes
import concourse.bass as bass
import concourse.mybir as mybir
from concourse.bass_utils import run_bass_kernel_spmd
from contextlib import ExitStack

F32 = mybir.dt.float32
BF16 = mybir.dt.bfloat16
I32 = mybir.dt.int32
AF = mybir.ActivationFunctionType
ALU = mybir.AluOpType
AX = mybir.AxisListType

S = 2048
D = 4096
II = 11008
NI = II // 128
NCH = D // 128
T = 512
NTT = S // T
NH = 32
NKV = 8
HD = 128
QKVO = D + 2 * NKV * HD
RSCL = float(1.0 / np.sqrt(128.0))
ENGS = ("tensor", "scalar", "vector", "gpsimd", "sync")

PHASES = ("f1", "qkv", "attn", "f2")


class Sem:
    __slots__ = ("h", "c")

    def __init__(self, es, nc, name):
        self.h = es.enter_context(nc.semaphore(name))
        self.c = 0

    def bump(self, n=1):
        self.c += n
        return self.c


def build_nc(phases=PHASES):
    nc = bass.Bass("TRN2", target_bir_lowering=False)
    x_in = nc.dram_tensor("x", [S, D], F32, kind="ExternalInput").ap()
    gT = nc.dram_tensor("gT", [D, II], I32, kind="ExternalInput").ap()
    uT = nc.dram_tensor("uT", [D, II], I32, kind="ExternalInput").ap()
    dT = nc.dram_tensor("dT", [II, D], I32, kind="ExternalInput").ap()
    wqkv = nc.dram_tensor("wqkv", [D, QKVO], F32, kind="ExternalInput").ap()
    wo_in = nc.dram_tensor("wo", [D, D], F32, kind="ExternalInput").ap()
    gs_in = nc.dram_tensor("gs", [128, NI], F32, kind="ExternalInput").ap()
    us_in = nc.dram_tensor("us", [128, NI], F32, kind="ExternalInput").ap()
    ds_in = nc.dram_tensor("ds", [128, NCH], F32, kind="ExternalInput").ap()
    idn_in = nc.dram_tensor("idn", [128, 128], BF16, kind="ExternalInput").ap()
    idnf_in = nc.dram_tensor("idnf", [128, 128], F32, kind="ExternalInput").ap()
    cos_in = nc.dram_tensor("cosq", [64, S], F32, kind="ExternalInput").ap()
    sin_in = nc.dram_tensor("sinq", [64, S], F32, kind="ExternalInput").ap()
    msk_in = nc.dram_tensor("msk", [128, 4 * T], BF16, kind="ExternalInput").ap()
    out = nc.dram_tensor("out", [S, D], F32, kind="ExternalOutput").ap()

    gB = nc.dram_tensor("gB", [NI, 128, D], BF16, kind="Internal").ap()
    uB = nc.dram_tensor("uB", [NI, 128, D], BF16, kind="Internal").ap()
    dB = nc.dram_tensor("dB", [NCH, 128, II], BF16, kind="Internal").ap()
    wqB = nc.dram_tensor("wqB", [D, QKVO], BF16, kind="Internal").ap()
    woB = nc.dram_tensor("woB", [D, D], BF16, kind="Internal").ap()
    xfmI = nc.dram_tensor("xfmI", [NCH, 128, S], F32, kind="Internal").ap()
    xfmA = nc.dram_tensor("xfmA", [NCH, 128, S], F32, kind="Internal").ap()
    xfmB = nc.dram_tensor("xfmB", [NCH, 128, S], F32, kind="Internal").ap()
    qscr = nc.dram_tensor("qscr", [NH, 128, S], BF16, kind="Internal").ap()

    ES = ExitStack()
    E = ES.enter_context

    idn = E(nc.sbuf_tensor("idn_s", [128, 128], BF16))
    idnf = E(nc.sbuf_tensor("idnf_s", [128, 128], F32))
    ones_cb = E(nc.sbuf_tensor("ones_cb", [128, 1], BF16))
    ones_rf = E(nc.sbuf_tensor("ones_rf", [1, 128], F32))
    gsus = E(nc.sbuf_tensor("gsus", [128, NI], F32))
    us_s = E(nc.sbuf_tensor("us_s", [128, NI], F32))
    dsch = E(nc.sbuf_tensor("dsch", [128, NCH], F32))
    negmbc = E(nc.sbuf_tensor("negmbc", [128, NH], F32))
    qn = E(nc.sbuf_tensor("qn", [1, NH], F32))
    kn = E(nc.sbuf_tensor("kn", [1, NKV], F32))
    knr = E(nc.sbuf_tensor("knr", [1, NH], F32))
    sm1 = E(nc.sbuf_tensor("sm1", [1, NH], F32))
    srow = [E(nc.sbuf_tensor(f"srow{i}", [1, T], F32)) for i in range(3)]
    a_bc = E(nc.sbuf_tensor("a_bc", [128, T], F32))
    b_bc = E(nc.sbuf_tensor("b_bc", [128, T], F32))
    cnv = [E(nc.sbuf_tensor(f"cnv{i}", [128, T], BF16)) for i in range(4)]
    lnt = E(nc.sbuf_tensor("lnt", [128, T], F32))
    xfs = [E(nc.sbuf_tensor(f"xf{i}", [128, T], F32)) for i in range(2)]
    ogs = [E(nc.sbuf_tensor(f"og{i}", [128, T], F32)) for i in range(2)]
    xn = E(nc.sbuf_tensor("xn", [128, NCH * T], BF16))

    pp = [E(nc.psum_tensor(f"pp{i}", [128, 512], F32)) for i in range(5)]
    pS1 = E(nc.psum_tensor("pS1", [1, 512], F32))
    pS2 = E(nc.psum_tensor("pS2", [1, 512], F32))
    pT = E(nc.psum_tensor("pT", [128, 512], BF16))

    sem_names = ["cld", "wgl0", "wgl1", "wglg0", "wglg1", "wdlg0", "wdlg1", "wqlg0", "wqlg1",
                 "l1d", "uev", "gud", "wwb", "wdl0", "wdl1", "wdf",
                 "dev", "xfl0", "xfl1", "xnb", "owb", "nsq", "nmm", "nrd",
                 "wql0", "wql1", "wqf", "qev", "qwb0", "qwb1", "tvd", "vtd",
                 "qld", "exd", "mkd", "pvd", "aod", "rcd", "std", "ppd"]
    SM = {n: Sem(ES, nc, n) for n in sem_names}

    prog = {e: [] for e in ENGS}

    def em(eng):
        def deco(f):
            prog[eng].append(f)
        return deco

    def wv(sem, delta=0):
        v = sem.c + delta
        return (sem.h, v) if v >= 1 else None

    def wlist(*sems):
        return tuple(x for x in (wv(s) for s in sems) if x)

    def phase_entry_waits():
        caps = [(s.h, s.c) for s in SM.values() if s.c >= 1]
        for eng in ENGS:
            @em(eng)
            def _(e, caps=tuple(caps)):
                for h, v in caps:
                    e.wait_ge(h, v)

    @em("sync")
    def _(e):
        e.dma_start(out=idn[:], in_=idn_in).then_inc(SM["cld"].h, 16)
        e.dma_start(out=idnf[:], in_=idnf_in).then_inc(SM["cld"].h, 16)
        e.dma_start(out=gsus[:], in_=gs_in).then_inc(SM["cld"].h, 16)
        e.dma_start(out=us_s[:], in_=us_in).then_inc(SM["cld"].h, 16)
        e.dma_start(out=dsch[:], in_=ds_in).then_inc(SM["cld"].h, 16)
    w_cld0 = SM["cld"].bump(5 * 16)

    @em("vector")
    def _(e, w=w_cld0):
        e.memset(ones_cb[:], 1.0)
        e.memset(ones_rf[:], 1.0)
        e.memset(qn[:], 0.0)
        e.memset(kn[:], 0.0)
        e.wait_ge(SM["cld"].h, w)
        e.tensor_mul(gsus[:], gsus[:], us_s[:])
        e.tensor_scalar_mul(dsch[:], dsch[:], 0.5).then_inc(SM["std"].h, 1)
    SM["std"].bump(1)
    w_consts = SM["std"].c

    def emit_prepass():
        xv = x_in.rearrange("(q j i) (c m) -> i j q c m", i=128, j=4, m=128)
        k = 0
        for c in range(NCH):
            for q in range(NTT):
                slot = k % 2

                @em("sync")
                def _(e, c=c, q=q, slot=slot, wfree=wv(SM["nmm"], -1)):
                    if wfree:
                        e.wait_ge(*wfree)
                    e.dma_start(out=xfs[slot].ap().rearrange("i (j m) -> i j m", m=128),
                                in_=xv[:, :, q, c, :]).then_inc(SM[f"xfl{slot}"].h, 16)
                w_ld = (SM[f"xfl{slot}"].h, SM[f"xfl{slot}"].bump(16))

                @em("tensor")
                def _(e, slot=slot, w=w_ld, wst=w_consts, wev=wv(SM["nsq"], -1)):
                    e.wait_ge(*w)
                    e.wait_ge(SM["std"].h, wst)
                    if wev:
                        e.wait_ge(*wev)
                    last = None
                    for j in range(4):
                        last = e.transpose(pp[slot][:, j * 128:(j + 1) * 128],
                                           xfs[slot][:, j * 128:(j + 1) * 128], idnf[:])
                    last.then_inc(SM["nmm"].h, 1)
                w_tr = SM["nmm"].bump(1)

                @em("scalar")
                def _(e, slot=slot, w=w_tr, wwb=wv(SM["owb"], 0)):
                    e.wait_ge(SM["nmm"].h, w)
                    if wwb:
                        e.wait_ge(*wwb)
                    e.copy(ogs[slot][:], pp[slot][:]).then_inc(SM["nsq"].h, 1)
                w_ev = SM["nsq"].bump(1)

                @em("sync")
                def _(e, c=c, q=q, slot=slot, w=w_ev):
                    e.wait_ge(SM["nsq"].h, w)
                    e.dma_start(out=xfmI[c][:, q * T:(q + 1) * T], in_=ogs[slot][:]).then_inc(SM["owb"].h, 16)
                SM["owb"].bump(16)
                k += 1
        return SM["owb"].c

    def ln_tile(src_fm3, tt, xn_guard_sem, tens_guards, src_wait):
        t0 = tt * T
        for c in range(NCH):
            @em("scalar")
            def _(e, c=c, t0=t0, wfree=wv(SM["nmm"], -1), sw=src_wait if c == 0 else None,
                  wdev=wlist(SM["dev"], SM["qev"]) if c == 0 else (),
                  wxb=wv(SM["xnb"], 0) if c < 2 else None):
                if sw:
                    e.wait_ge(*sw)
                for hvv in wdev:
                    e.wait_ge(*hvv)
                if wxb:
                    e.wait_ge(*wxb)
                if wfree:
                    e.wait_ge(*wfree)
                e.dma_start(out=xfs[c % 2][:], in_=src_fm3[c][:, t0:t0 + T]).then_inc(SM[f"xfl{c % 2}"].h, 16)
            w_ld = (SM[f"xfl{c % 2}"].h, SM[f"xfl{c % 2}"].bump(16))

            @em("scalar")
            def _(e, c=c, w=w_ld, wmm=wv(SM["nmm"], -1)):
                e.wait_ge(*w)
                if wmm:
                    e.wait_ge(*wmm)
                e.copy(cnv[2 * (c % 2)][:], xfs[c % 2][:])
                e.activation(cnv[2 * (c % 2) + 1][:], xfs[c % 2][:], AF.Square).then_inc(SM["nsq"].h, 1)
            w_cv = SM["nsq"].bump(1)

            @em("tensor")
            def _(e, c=c, w=w_cv, tg=tens_guards if c == 0 else ()):
                e.wait_ge(SM["nsq"].h, w)
                for hvv in tg:
                    e.wait_ge(*hvv)
                e.matmul(pS1[:], ones_cb[:], cnv[2 * (c % 2)][:], start=(c == 0), stop=(c == NCH - 1))
                e.matmul(pS2[:], ones_cb[:], cnv[2 * (c % 2) + 1][:], start=(c == 0), stop=(c == NCH - 1)).then_inc(SM["nmm"].h, 1)
            SM["nmm"].bump(1)
        w_stat = SM["nmm"].c

        @em("vector")
        def _(e, w=w_stat):
            e.wait_ge(SM["nmm"].h, w)
            m, r, t2 = srow
            e.tensor_scalar_mul(m[:], pS1[0:1, 0:T], 1.0 / D)
            e.tensor_scalar_mul(r[:], pS2[0:1, 0:T], 1.0 / D)
            e.drain()
            e.tensor_tensor(t2[:], m[:], m[:], ALU.mult)
            e.drain()
            e.tensor_sub(r[:], r[:], t2[:])
            e.drain()
            e.tensor_scalar_add(r[:], r[:], 1e-5)
            e.drain()
            e.tensor_copy(t2[:], r[:])
            e.drain()
            e.reciprocal(r[:], r[:]).then_inc(SM["nsq"].h, 1)
        w1 = SM["nsq"].bump(1)

        @em("scalar")
        def _(e, w=w1):
            e.wait_ge(SM["nsq"].h, w)
            e.activation(srow[1][:], srow[1][:], AF.Sqrt).then_inc(SM["nsq"].h, 1)
        w2 = SM["nsq"].bump(1)

        @em("vector")
        def _(e, w=w2):
            e.wait_ge(SM["nsq"].h, w)
            m, r, t2 = srow
            e.tensor_tensor(lnt[0:1, :], r[:], r[:], ALU.mult)
            e.drain()
            e.tensor_tensor(lnt[0:1, :], lnt[0:1, :], t2[:], ALU.mult)
            e.drain()
            e.tensor_scalar(lnt[0:1, :], lnt[0:1, :], -0.5, 1.5, ALU.mult, ALU.add)
            e.drain()
            e.tensor_tensor(r[:], r[:], lnt[0:1, :], ALU.mult)
            e.drain()
            e.scalar_tensor_tensor(m[:], m[:], -1.0, r[:], ALU.mult, ALU.mult).then_inc(SM["nsq"].h, 1)
        w3 = SM["nsq"].bump(1)

        @em("tensor")
        def _(e, w=w3, wab=wv(SM["xnb"], 0), wpo=wv(SM["ppd"], 0)):
            e.wait_ge(SM["nsq"].h, w)
            if wab:
                e.wait_ge(*wab)
            if wpo:
                e.wait_ge(*wpo)
            e.matmul(pp[0][:, 0:T], ones_rf[:], srow[1][:], start=True, stop=True)
            e.matmul(pp[1][:, 0:T], ones_rf[:], srow[0][:], start=True, stop=True).then_inc(SM["nmm"].h, 1)
        w4 = SM["nmm"].bump(1)

        @em("scalar")
        def _(e, w=w4):
            e.wait_ge(SM["nmm"].h, w)
            e.copy(a_bc[:], pp[0][:, 0:T])
            e.copy(b_bc[:], pp[1][:, 0:T]).then_inc(SM["nsq"].h, 1)
        w_abc = SM["nsq"].bump(1)

        xn_w = (xn_guard_sem.h, xn_guard_sem.c) if xn_guard_sem.c >= 1 else None
        for c2 in range(NCH // 2):
            for c in (2 * c2, 2 * c2 + 1):
                @em("scalar")
                def _(e, c=c, t0=t0, wfree=wv(SM["xnb"], -1 if c % 2 == 0 else 0), wab=w_abc):
                    e.wait_ge(SM["nsq"].h, wab)
                    if wfree:
                        e.wait_ge(*wfree)
                    e.dma_start(out=xfs[c % 2][:], in_=src_fm3[c][:, t0:t0 + T]).then_inc(SM[f"xfl{c % 2}"].h, 16)
                SM[f"xfl{c % 2}"].bump(16)
            wld = ((SM["xfl0"].h, SM["xfl0"].c), (SM["xfl1"].h, SM["xfl1"].c))

            @em("vector")
            def _(e, c2=c2, wld=wld, wab=w_abc, xw=xn_w if c2 == 0 else None):
                e.wait_ge(*wld[0])
                e.wait_ge(*wld[1])
                e.wait_ge(SM["nsq"].h, wab)
                if xw:
                    e.wait_ge(*xw)
                ca, cb = 2 * c2, 2 * c2 + 1
                e.tensor_tensor(lnt[:], xfs[0][:], a_bc[:], ALU.mult)
                e.tensor_tensor(ogs[0][:], xfs[1][:], a_bc[:], ALU.mult)
                e.drain()
                e.tensor_add(xn[:, ca * T:(ca + 1) * T], lnt[:], b_bc[:]).then_inc(SM["xnb"].h, 1)
                e.tensor_add(xn[:, cb * T:(cb + 1) * T], ogs[0][:], b_bc[:]).then_inc(SM["xnb"].h, 1)
            SM["xnb"].bump(2)
        return SM["xnb"].c

    def ffn_phase(tag, src_fm3, dst_fm3, dst_tm, first, stk, src_wait):
        gu = stk.enter_context(nc.sbuf_tensor(f"gu_{tag}", [128, NI * T], BF16))
        wg = [stk.enter_context(nc.sbuf_tensor(f"wg{tag}{i}", [128, D], BF16)) for i in range(2)]
        wu = [stk.enter_context(nc.sbuf_tensor(f"wu{tag}{i}", [128, D], BF16)) for i in range(2)]
        wd = [stk.enter_context(nc.sbuf_tensor(f"wd{tag}{i}", [128, II // 2], BF16)) for i in range(2)]
        usb = [stk.enter_context(nc.sbuf_tensor(f"us{tag}{i}", [128, T], BF16)) for i in range(2)]
        otm = stk.enter_context(nc.sbuf_tensor(f"otm{tag}", [128, T], F32)) if dst_tm is not None else None
        gv = gT.rearrange("(c p) (t m) -> p c t m", p=128, m=128)
        uv = uT.rearrange("(c p) (t m) -> p c t m", p=128, m=128)
        dv = dT.rearrange("(j p) (h m) -> p j h m", p=128, m=128)
        if dst_tm is not None:
            ov = dst_tm.rearrange("(q j i) (c m) -> q i j c m", i=128, j=4, m=128)

        for tt in range(NTT):
            t0 = tt * T
            conv = first and tt == 0
            xn_tgt = ln_tile(src_fm3, tt, SM["l1d"],
                             wlist(SM["nrd"], SM["rcd"], SM["gud"], SM["uev"], SM["qev"], SM["exd"], SM["dev"]),
                             src_wait if tt == 0 else None)

            for tau in range(NI):
                slot = tau % 2
                if conv:
                    @em("gpsimd")
                    def _(e, tau=tau, slot=slot, wfree=wv(SM["l1d"], -1), wwb=wv(SM["wwb"], 0)):
                        if wfree:
                            e.wait_ge(*wfree)
                        if wwb:
                            e.wait_ge(*wwb)
                        e.dma_start(out=wg[slot].ap().rearrange("p (c m) -> p c m", m=128),
                                    in_=gv[:, :, tau, :]).then_inc(SM[f"wglg{slot}"].h, 16)
                        e.dma_start(out=wu[slot].ap().rearrange("p (c m) -> p c m", m=128),
                                    in_=uv[:, :, tau, :]).then_inc(SM[f"wglg{slot}"].h, 16)
                    wld = (SM[f"wglg{slot}"].h, SM[f"wglg{slot}"].bump(32))

                    @em("sync")
                    def _(e, tau=tau, slot=slot, wld=wld):
                        e.wait_ge(*wld)
                        e.dma_start(out=gB[tau], in_=wg[slot][:]).then_inc(SM["wwb"].h, 16)
                        e.dma_start(out=uB[tau], in_=wu[slot][:]).then_inc(SM["wwb"].h, 16)
                    SM["wwb"].bump(32)
                else:
                    @em("sync")
                    def _(e, tau=tau, slot=slot, wfree=wv(SM["l1d"], -1),
                          wwb=wv(SM["wwb"], 0) if (tau == 0 and first and tt == 1) else None):
                        if wwb:
                            e.wait_ge(*wwb)
                        if wfree:
                            e.wait_ge(*wfree)
                        e.dma_start(out=wg[slot][:], in_=gB[tau]).then_inc(SM[f"wgl{slot}"].h, 16)
                        e.dma_start(out=wu[slot][:], in_=uB[tau]).then_inc(SM[f"wgl{slot}"].h, 16)
                    wld = (SM[f"wgl{slot}"].h, SM[f"wgl{slot}"].bump(32))

                @em("tensor")
                def _(e, tau=tau, slot=slot, wld=wld, xw=xn_tgt,
                      wgu=wv(SM["gud"], -1), wue=wv(SM["uev"], -1),
                      wpo=wv(SM["ppd"], 0) if tau < 2 else None):
                    e.wait_ge(*wld)
                    if wpo:
                        e.wait_ge(*wpo)
                    if tau == 0:
                        e.wait_ge(SM["xnb"].h, xw)
                    if wgu:
                        e.wait_ge(*wgu)
                    if wue:
                        e.wait_ge(*wue)
                    for c in range(NCH):
                        e.matmul(pp[slot][:], wg[slot][:, c * 128:(c + 1) * 128], xn[:, c * T:(c + 1) * T],
                                 start=(c == 0), stop=(c == NCH - 1))
                    for c in range(NCH):
                        ins = e.matmul(pp[2 + slot][:], wu[slot][:, c * 128:(c + 1) * 128], xn[:, c * T:(c + 1) * T],
                                       start=(c == 0), stop=(c == NCH - 1))
                    ins.then_inc(SM["l1d"].h, 1)
                SM["l1d"].bump(1)

                @em("scalar")
                def _(e, slot=slot, w=SM["l1d"].c):
                    e.wait_ge(SM["l1d"].h, w)
                    e.copy(usb[slot][:], pp[2 + slot][:]).then_inc(SM["uev"].h, 1)
                SM["uev"].bump(1)

                @em("vector")
                def _(e, tau=tau, slot=slot, w=SM["uev"].c):
                    e.wait_ge(SM["uev"].h, w)
                    e.scalar_tensor_tensor(gu[:, tau * T:(tau + 1) * T], pp[slot][:], gsus[:, tau:tau + 1],
                                           usb[slot][:], ALU.mult, ALU.mult).then_inc(SM["gud"].h, 1)
                SM["gud"].bump(1)
            l1_all = SM["gud"].c

            for ht in range(NCH):
                for half in range(2):
                    hslot = (2 * ht + half) % 2
                    if conv:
                        @em("gpsimd")
                        def _(e, ht=ht, half=half, hslot=hslot, wfree=wv(SM["wdf"], -1), wwb=wv(SM["wwb"], 0)):
                            if wfree:
                                e.wait_ge(*wfree)
                            if wwb:
                                e.wait_ge(*wwb)
                            e.dma_start(out=wd[hslot].ap().rearrange("p (j m) -> p j m", m=128),
                                        in_=dv[:, half * 43:(half + 1) * 43, ht, :]).then_inc(SM[f"wdlg{hslot}"].h, 16)
                        wldd = (SM[f"wdlg{hslot}"].h, SM[f"wdlg{hslot}"].bump(16))

                        @em("sync")
                        def _(e, ht=ht, half=half, hslot=hslot, wldd=wldd):
                            e.wait_ge(*wldd)
                            e.dma_start(out=dB[ht].rearrange("p (hh x) -> p hh x", hh=2)[:, half, :],
                                        in_=wd[hslot][:]).then_inc(SM["wwb"].h, 16)
                        SM["wwb"].bump(16)
                    else:
                        @em("sync")
                        def _(e, ht=ht, half=half, hslot=hslot, wfree=wv(SM["wdf"], -1),
                              wwb=wv(SM["wwb"], 0) if (ht == 0 and half == 0 and first and tt == 1) else None):
                            if wwb:
                                e.wait_ge(*wwb)
                            if wfree:
                                e.wait_ge(*wfree)
                            e.dma_start(out=wd[hslot][:],
                                        in_=dB[ht].rearrange("p (hh x) -> p hh x", hh=2)[:, half, :]).then_inc(SM[f"wdl{hslot}"].h, 16)
                        wldd = (SM[f"wdl{hslot}"].h, SM[f"wdl{hslot}"].bump(16))

                    @em("tensor")
                    def _(e, ht=ht, half=half, hslot=hslot, wldd=wldd, wall=l1_all,
                          wde=wv(SM["dev"], 0) if half == 0 else None):
                        e.wait_ge(*wldd)
                        if ht == 0 and half == 0:
                            e.wait_ge(SM["gud"].h, wall)
                        if wde:
                            e.wait_ge(*wde)
                        last = None
                        for jj in range(43):
                            j = half * 43 + jj
                            last = e.matmul(pp[4][:], wd[hslot][:, jj * 128:(jj + 1) * 128],
                                            gu[:, j * T:(j + 1) * T], start=(j == 0), stop=(j == NI - 1))
                        last.then_inc(SM["wdf"].h, 1)
                    SM["wdf"].bump(1)
                w_dmm = SM["wdf"].c

                @em("scalar")
                def _(e, ht=ht, t0=t0, wfree=wv(SM["dev"], -1),
                      wxn=wv(SM["xnb"], 0) if ht == 0 else None):
                    if wfree:
                        e.wait_ge(*wfree)
                    if wxn:
                        e.wait_ge(*wxn)
                    e.dma_start(out=xfs[ht % 2][:], in_=src_fm3[ht][:, t0:t0 + T]).then_inc(SM[f"xfl{ht % 2}"].h, 16)
                wxl = (SM[f"xfl{ht % 2}"].h, SM[f"xfl{ht % 2}"].bump(16))

                @em("vector")
                def _(e, ht=ht, wmm=w_dmm, wxl=wxl, wowb=wv(SM["owb"], 0)):
                    e.wait_ge(SM["wdf"].h, wmm)
                    e.wait_ge(*wxl)
                    if wowb:
                        e.wait_ge(*wowb)
                    e.scalar_tensor_tensor(ogs[ht % 2][:], pp[4][:], dsch[:, ht:ht + 1],
                                           xfs[ht % 2][:], ALU.mult, ALU.add).then_inc(SM["dev"].h, 1)
                SM["dev"].bump(1)
                w_ev = SM["dev"].c

                if dst_fm3 is not None:
                    @em("sync")
                    def _(e, ht=ht, t0=t0, w=w_ev):
                        e.wait_ge(SM["dev"].h, w)
                        e.dma_start(out=dst_fm3[ht][:, t0:t0 + T], in_=ogs[ht % 2][:]).then_inc(SM["owb"].h, 16)
                    SM["owb"].bump(16)
                else:
                    @em("tensor")
                    def _(e, ht=ht, w=w_ev, wpo=wv(SM["ppd"], 0)):
                        e.wait_ge(SM["dev"].h, w)
                        if wpo:
                            e.wait_ge(*wpo)
                        last = None
                        for j in range(4):
                            last = e.transpose(pp[ht % 2][:, j * 128:(j + 1) * 128],
                                               ogs[ht % 2][:, j * 128:(j + 1) * 128], idnf[:])
                        last.then_inc(SM["nmm"].h, 1)
                    w_tr = SM["nmm"].bump(1)

                    @em("scalar")
                    def _(e, ht=ht, w=w_tr, wwb=wv(SM["owb"], 0)):
                        e.wait_ge(SM["nmm"].h, w)
                        if wwb:
                            e.wait_ge(*wwb)
                        e.copy(otm[:], pp[ht % 2][:]).then_inc(SM["ppd"].h, 1)
                    w_ot = SM["ppd"].bump(1)

                    @em("sync")
                    def _(e, ht=ht, tt=tt, w=w_ot):
                        e.wait_ge(SM["ppd"].h, w)
                        e.dma_start(out=ov[tt, :, :, ht, :],
                                    in_=otm.ap().rearrange("i (j m) -> i j m", m=128)).then_inc(SM["owb"].h, 16)
                    SM["owb"].bump(16)

    w_pre = emit_prepass()
    if "f1" in phases:
        stk = ExitStack()
        last_f1 = len(phases) == 1
        ffn_phase("a", xfmI, None if last_f1 else xfmA, out if last_f1 else None,
                  True, stk, (SM["owb"].h, w_pre))
        stk.close()
        phase_entry_waits()
        src_after_f1 = xfmA
    else:
        src_after_f1 = xfmI

    kv_stack = ExitStack()
    if "qkv" in phases:
        src_fm3 = src_after_f1
        kfm = kv_stack.enter_context(nc.sbuf_tensor("kfm", [128, NKV * S], BF16))
        vtm = kv_stack.enter_context(nc.sbuf_tensor("vtm", [128, (S // 128) * NKV * HD], BF16))
        qes = ExitStack()
        cosq = qes.enter_context(nc.sbuf_tensor("cosq_s", [64, S], F32))
        sinq = qes.enter_context(nc.sbuf_tensor("sinq_s", [64, S], F32))
        wqb = [qes.enter_context(nc.sbuf_tensor(f"wqb{i}", [128, 640], BF16)) for i in range(2)]
        qsb = [qes.enter_context(nc.sbuf_tensor(f"qsb{i}", [128, T], BF16)) for i in range(2)]
        rsc = [qes.enter_context(nc.sbuf_tensor(f"rsc{i}", [64, T], F32)) for i in range(4)]
        vst = [qes.enter_context(nc.sbuf_tensor(f"vst{i}", [128, T], BF16)) for i in range(2)]
        sqsc = qes.enter_context(nc.sbuf_tensor("sqsc", [128, T], BF16))
        nrm = qes.enter_context(nc.sbuf_tensor("nrm", [1, 1], F32))

        @em("sync")
        def _(e):
            e.dma_start(out=cosq[:], in_=cos_in).then_inc(SM["cld"].h, 16)
            e.dma_start(out=sinq[:], in_=sin_in).then_inc(SM["cld"].h, 16)
        w_rope = SM["cld"].bump(32)

        nqu = 0
        q_sq = []
        for tt in range(NTT):
            t0 = tt * T
            conv = tt == 0
            xn_tgt = ln_tile(src_fm3, tt, SM["wqf"],
                             wlist(SM["nrd"], SM["rcd"], SM["qev"], SM["exd"]),
                             (SM["owb"].h, SM["owb"].c) if (tt == 0 and "f1" not in phases) else None)

            for b in range(10):
                nu = 5 if b < 9 else 3
                ww = nu * 128
                for c in range(NCH):
                    if conv:
                        @em("gpsimd")
                        def _(e, b=b, c=c, ww=ww, wfree=wv(SM["wqf"], -1), wwb=wv(SM["wwb"], 0)):
                            if wfree:
                                e.wait_ge(*wfree)
                            if wwb:
                                e.wait_ge(*wwb)
                            e.dma_start(out=wqb[c % 2][:, 0:ww],
                                        in_=wqkv[c * 128:(c + 1) * 128, b * 640:b * 640 + ww]).then_inc(SM[f"wqlg{c % 2}"].h, 16)
                        wldq = (SM[f"wqlg{c % 2}"].h, SM[f"wqlg{c % 2}"].bump(16))

                        @em("sync")
                        def _(e, b=b, c=c, ww=ww, wldq=wldq):
                            e.wait_ge(*wldq)
                            e.dma_start(out=wqB[c * 128:(c + 1) * 128, b * 640:b * 640 + ww],
                                        in_=wqb[c % 2][:, 0:ww]).then_inc(SM["wwb"].h, 16)
                        SM["wwb"].bump(16)
                    else:
                        @em("sync")
                        def _(e, b=b, c=c, ww=ww, wfree=wv(SM["wqf"], -1),
                              wwb=wv(SM["wwb"], 0) if (b == 0 and c == 0 and tt == 1) else None):
                            if wwb:
                                e.wait_ge(*wwb)
                            if wfree:
                                e.wait_ge(*wfree)
                            e.dma_start(out=wqb[c % 2][:, 0:ww],
                                        in_=wqB[c * 128:(c + 1) * 128, b * 640:b * 640 + ww]).then_inc(SM[f"wql{c % 2}"].h, 16)
                        wldq = (SM[f"wql{c % 2}"].h, SM[f"wql{c % 2}"].bump(16))

                    @em("tensor")
                    def _(e, b=b, c=c, nu=nu, wldq=wldq, xw=xn_tgt, wev=wv(SM["qev"], 0) if c == 0 else None):
                        e.wait_ge(*wldq)
                        if c == 0:
                            e.wait_ge(SM["xnb"].h, xw)
                            if wev:
                                e.wait_ge(*wev)
                        last = None
                        for u in range(nu):
                            last = e.matmul(pp[u][:], wqb[c % 2][:, u * 128:(u + 1) * 128],
                                            xn[:, c * T:(c + 1) * T], start=(c == 0), stop=(c == NCH - 1))
                        last.then_inc(SM["wqf"].h, 1)
                    SM["wqf"].bump(1)
                w_batch = SM["wqf"].c

                for u in range(nu):
                    ot = b * 5 + u
                    if ot < NH:
                        slot = nqu % 2

                        @em("vector")
                        def _(e, u=u, slot=slot, t0=t0, wmm=w_batch, wrope=w_rope,
                              wqwb=wv(SM[f"qwb{nqu % 2}"], 0),
                              wsq=(q_sq[nqu - 2] if nqu >= 2 else 0)):
                            e.wait_ge(SM["wqf"].h, wmm)
                            e.wait_ge(SM["cld"].h, wrope)
                            if wqwb:
                                e.wait_ge(*wqwb)
                            if wsq >= 1:
                                e.wait_ge(SM["nsq"].h, wsq)
                            psrc = pp[u]
                            cs = cosq[:, t0:t0 + T]
                            sn = sinq[:, t0:t0 + T]
                            pe = psrc[0:64, :]
                            po = psrc[64:128, :]
                            e.tensor_tensor(rsc[0][:], pe, cs, ALU.mult)
                            e.tensor_tensor(rsc[1][:], po, sn, ALU.mult)
                            e.tensor_tensor(rsc[2][:], pe, sn, ALU.mult)
                            e.tensor_tensor(rsc[3][:], po, cs, ALU.mult)
                            e.drain()
                            e.tensor_sub(qsb[slot][0:64, :], rsc[0][:], rsc[1][:])
                            e.tensor_add(qsb[slot][64:128, :], rsc[2][:], rsc[3][:]).then_inc(SM["qev"].h, 1)
                        w_rp = SM["qev"].bump(1)

                        @em("sync")
                        def _(e, ot=ot, slot=slot, t0=t0, w=w_rp):
                            e.wait_ge(SM["qev"].h, w)
                            e.dma_start(out=qscr[ot][:, t0:t0 + T], in_=qsb[slot][:]).then_inc(SM[f"qwb{slot}"].h, 16)
                        SM[f"qwb{slot}"].bump(16)

                        @em("scalar")
                        def _(e, slot=slot, w=w_rp, wn=wv(SM["nmm"], 0)):
                            e.wait_ge(SM["qev"].h, w)
                            if wn:
                                e.wait_ge(*wn)
                            e.activation(sqsc[:], qsb[slot][:], AF.Square).then_inc(SM["nsq"].h, 1)
                        w_sq = SM["nsq"].bump(1)
                        q_sq.append(w_sq)
                        nqu += 1

                        @em("tensor")
                        def _(e, w=w_sq, wr=wv(SM["nrd"], 0)):
                            e.wait_ge(SM["nsq"].h, w)
                            if wr:
                                e.wait_ge(*wr)
                            e.matmul(pS1[:], ones_cb[:], sqsc[:], start=True, stop=True).then_inc(SM["nmm"].h, 1)
                        w_nm = SM["nmm"].bump(1)

                        @em("vector")
                        def _(e, ot=ot, w=w_nm):
                            e.wait_ge(SM["nmm"].h, w)
                            e.reduce_max(nrm[:], pS1[0:1, 0:T], axis=AX.X)
                            e.drain()
                            e.tensor_max(qn[0:1, ot:ot + 1], qn[0:1, ot:ot + 1], nrm[:]).then_inc(SM["nrd"].h, 1)
                        SM["nrd"].bump(1)
                    elif ot < NH + NKV:
                        kh = ot - NH

                        @em("vector")
                        def _(e, kh=kh, u=u, t0=t0, wmm=w_batch, wrope=w_rope):
                            e.wait_ge(SM["wqf"].h, wmm)
                            e.wait_ge(SM["cld"].h, wrope)
                            psrc = pp[u]
                            cs = cosq[:, t0:t0 + T]
                            sn = sinq[:, t0:t0 + T]
                            pe = psrc[0:64, :]
                            po = psrc[64:128, :]
                            e.tensor_tensor(rsc[0][:], pe, cs, ALU.mult)
                            e.tensor_tensor(rsc[1][:], po, sn, ALU.mult)
                            e.tensor_tensor(rsc[2][:], pe, sn, ALU.mult)
                            e.tensor_tensor(rsc[3][:], po, cs, ALU.mult)
                            e.drain()
                            e.tensor_sub(kfm[0:64, kh * S + t0: kh * S + t0 + T], rsc[0][:], rsc[1][:])
                            e.tensor_add(kfm[64:128, kh * S + t0: kh * S + t0 + T], rsc[2][:], rsc[3][:]).then_inc(SM["qev"].h, 1)
                        w_rp = SM["qev"].bump(1)

                        @em("scalar")
                        def _(e, kh=kh, t0=t0, w=w_rp, wn=wv(SM["nmm"], 0)):
                            e.wait_ge(SM["qev"].h, w)
                            if wn:
                                e.wait_ge(*wn)
                            e.activation(sqsc[:], kfm[:, kh * S + t0: kh * S + t0 + T], AF.Square).then_inc(SM["nsq"].h, 1)
                        w_sq = SM["nsq"].bump(1)

                        @em("tensor")
                        def _(e, w=w_sq, wr=wv(SM["nrd"], 0)):
                            e.wait_ge(SM["nsq"].h, w)
                            if wr:
                                e.wait_ge(*wr)
                            e.matmul(pS1[:], ones_cb[:], sqsc[:], start=True, stop=True).then_inc(SM["nmm"].h, 1)
                        w_nm = SM["nmm"].bump(1)

                        @em("vector")
                        def _(e, kh=kh, w=w_nm):
                            e.wait_ge(SM["nmm"].h, w)
                            e.reduce_max(nrm[:], pS1[0:1, 0:T], axis=AX.X)
                            e.drain()
                            e.tensor_max(kn[0:1, kh:kh + 1], kn[0:1, kh:kh + 1], nrm[:]).then_inc(SM["nrd"].h, 1)
                        SM["nrd"].bump(1)
                    else:
                        vh = ot - NH - NKV
                        vslot = vh % 2

                        @em("scalar")
                        def _(e, u=u, vslot=vslot, wmm=w_batch, wtv=wv(SM["tvd"], -1)):
                            e.wait_ge(SM["wqf"].h, wmm)
                            if wtv:
                                e.wait_ge(*wtv)
                            e.copy(vst[vslot][:], pp[u][:]).then_inc(SM["qev"].h, 1)
                        w_vst = SM["qev"].bump(1)

                        @em("tensor")
                        def _(e, vslot=vslot, w=w_vst, wvt=wv(SM["vtd"], 0)):
                            e.wait_ge(SM["qev"].h, w)
                            if wvt:
                                e.wait_ge(*wvt)
                            last = None
                            for j in range(4):
                                last = e.transpose(pT[:, j * 128:(j + 1) * 128], vst[vslot][:, j * 128:(j + 1) * 128], idn[:])
                            last.then_inc(SM["tvd"].h, 1)
                        w_tv = SM["tvd"].bump(1)

                        @em("scalar")
                        def _(e, vh=vh, t0=t0, w=w_tv):
                            e.wait_ge(SM["tvd"].h, w)
                            ins = None
                            for j in range(4):
                                base = (t0 // 128 + j) * (NKV * HD) + vh * HD
                                ins = e.copy(vtm[:, base: base + HD], pT[:, j * 128:(j + 1) * 128])
                            ins.then_inc(SM["vtd"].h, 1)
                        SM["vtd"].bump(1)

        @em("vector")
        def _(e, w=SM["nrd"].c):
            e.wait_ge(SM["nrd"].h, w)
            for k in range(NKV):
                for r in range(4):
                    e.tensor_copy(knr[0:1, 4 * k + r:4 * k + r + 1], kn[0:1, k:k + 1])
            e.drain()
            e.tensor_tensor(sm1[:], qn[:], knr[:], ALU.mult).then_inc(SM["nsq"].h, 1)
        w1 = SM["nsq"].bump(1)

        @em("scalar")
        def _(e, w=w1):
            e.wait_ge(SM["nsq"].h, w)
            e.activation(sm1[:], sm1[:], AF.Sqrt).then_inc(SM["nsq"].h, 1)
        w2 = SM["nsq"].bump(1)

        @em("vector")
        def _(e, w=w2):
            e.wait_ge(SM["nsq"].h, w)
            e.tensor_scalar_mul(sm1[:], sm1[:], -RSCL).then_inc(SM["nsq"].h, 1)
        w3 = SM["nsq"].bump(1)

        @em("tensor")
        def _(e, w=w3):
            e.wait_ge(SM["nsq"].h, w)
            e.matmul(pp[0][:, 0:NH], ones_rf[:], sm1[:], start=True, stop=True).then_inc(SM["nmm"].h, 1)
        w4 = SM["nmm"].bump(1)

        @em("scalar")
        def _(e, w=w4):
            e.wait_ge(SM["nmm"].h, w)
            e.copy(negmbc[:], pp[0][:, 0:NH]).then_inc(SM["nsq"].h, 1)
        SM["nsq"].bump(1)
        qes.close()
        phase_entry_waits()

    if "attn" in phases:
        src_fm3 = src_after_f1
        dst_is_out = "f2" not in phases
        aes = ExitStack()
        qt = aes.enter_context(nc.sbuf_tensor("qt", [128, NH * T], BF16))
        ao = aes.enter_context(nc.sbuf_tensor("ao", [128, NH * T], BF16))
        msk = aes.enter_context(nc.sbuf_tensor("msk_s", [128, 4 * T], BF16))
        wob = [aes.enter_context(nc.sbuf_tensor(f"wob{i}", [128, 640], BF16)) for i in range(2)]
        pex = [aes.enter_context(nc.sbuf_tensor(f"pex{i}", [128, T], BF16)) for i in range(2)]
        rec = aes.enter_context(nc.sbuf_tensor("rec", [1, T], F32))
        rbc = aes.enter_context(nc.sbuf_tensor("rbc", [128, T], F32))
        otm2 = aes.enter_context(nc.sbuf_tensor("otm2", [128, T], F32)) if dst_is_out else None
        if dst_is_out:
            ov2 = out.rearrange("(q j i) (c m) -> q i j c m", i=128, j=4, m=128)

        @em("sync")
        def _(e):
            e.dma_start(out=msk[:], in_=msk_in).then_inc(SM["cld"].h, 16)
        w_msk = SM["cld"].bump(16)

        for tq in range(NTT):
            t0 = tq * T
            conv = tq == 0

            @em("sync")
            def _(e, t0=t0, waod=wv(SM["aod"], 0)):
                if waod:
                    e.wait_ge(*waod)
                e.dma_start(out=qt.ap().rearrange("p (h t) -> p h t", t=T),
                            in_=qscr.rearrange("h p t -> p h t")[:, :, t0:t0 + T]).then_inc(SM["qld"].h, 16)
            w_ql = SM["qld"].bump(16)

            nck = (t0 + T) // 128
            for h in range(NH):
                kvh = h // 4
                for ck in range(nck):
                    diag = ck >= t0 // 128

                    @em("tensor")
                    def _(e, h=h, ck=ck, kvh=kvh, w_ql=w_ql, wexp=wv(SM["exd"], -1),
                          waod=wv(SM["aod"], 0) if ck == 0 else None,
                          wrcd=wv(SM["rcd"], -1) if ck == 0 else None,
                          wqev=wv(SM["qev"], 0) if (h == 0 and ck == 0) else None):
                        if h == 0 and ck == 0:
                            e.wait_ge(SM["qld"].h, w_ql)
                            if wqev:
                                e.wait_ge(*wqev)
                        if ck == 0:
                            if waod:
                                e.wait_ge(*waod)
                            if wrcd:
                                e.wait_ge(*wrcd)
                        if wexp:
                            e.wait_ge(*wexp)
                        e.matmul(pp[ck % 2][:], kfm[:, kvh * S + ck * 128: kvh * S + (ck + 1) * 128],
                                 qt[:, h * T:(h + 1) * T], start=True, stop=True).then_inc(SM["qld"].h, 1)
                    SM["qld"].bump(1)

                    @em("scalar")
                    def _(e, h=h, ck=ck, wm=SM["qld"].c, wpv=wv(SM["pvd"], -1), w_msk=w_msk):
                        e.wait_ge(SM["qld"].h, wm)
                        if wpv:
                            e.wait_ge(*wpv)
                        e.wait_ge(SM["cld"].h, w_msk)
                        e.activation(pex[ck % 2][:], pp[ck % 2][:], AF.Exp,
                                     bias=negmbc[:, h:h + 1], scale=RSCL).then_inc(SM["exd"].h, 1)
                    w_ex = SM["exd"].bump(1)

                    if diag:
                        dd = ck - t0 // 128

                        @em("vector")
                        def _(e, ck=ck, dd=dd, w=w_ex):
                            e.wait_ge(SM["exd"].h, w)
                            e.tensor_tensor(pex[ck % 2][:], pex[ck % 2][:], msk[:, dd * T:(dd + 1) * T],
                                            ALU.mult).then_inc(SM["mkd"].h, 1)
                        SM["mkd"].bump(1)

                    @em("tensor")
                    def _(e, ck=ck, nck=nck, kvh=kvh, diag=diag, wex=w_ex, wmk=SM["mkd"].c):
                        if diag:
                            e.wait_ge(SM["mkd"].h, wmk)
                        else:
                            e.wait_ge(SM["exd"].h, wex)
                        e.matmul(pp[2][:], vtm[:, ck * (NKV * HD) + kvh * HD: ck * (NKV * HD) + (kvh + 1) * HD],
                                 pex[ck % 2][:], start=(ck == 0), stop=(ck == nck - 1))
                        e.matmul(pS1[:], ones_cb[:], pex[ck % 2][:],
                                 start=(ck == 0), stop=(ck == nck - 1)).then_inc(SM["pvd"].h, 1)
                    SM["pvd"].bump(1)
                w_pv = SM["pvd"].c

                @em("vector")
                def _(e, w=w_pv):
                    e.wait_ge(SM["pvd"].h, w)
                    e.reciprocal(rec[:], pS1[0:1, 0:T]).then_inc(SM["rcd"].h, 1)
                w_rc = SM["rcd"].bump(1)

                @em("tensor")
                def _(e, w=w_rc):
                    e.wait_ge(SM["rcd"].h, w)
                    e.matmul(pp[3][:, 0:T], ones_rf[:], rec[:], start=True, stop=True).then_inc(SM["nmm"].h, 1)
                w_bc2 = SM["nmm"].bump(1)

                @em("scalar")
                def _(e, w=w_bc2):
                    e.wait_ge(SM["nmm"].h, w)
                    e.copy(rbc[:], pp[3][:, 0:T]).then_inc(SM["rcd"].h, 1)
                w_rbc = SM["rcd"].bump(1)

                @em("vector")
                def _(e, h=h, w=w_rbc):
                    e.wait_ge(SM["rcd"].h, w)
                    e.tensor_tensor(ao[:, h * T:(h + 1) * T], pp[2][:], rbc[:], ALU.mult).then_inc(SM["aod"].h, 1)
                SM["aod"].bump(1)
            all_heads = SM["aod"].c

            for b in range(7):
                units = list(range(5 if b < 6 else 2))
                ww = 128 * len(units)
                for c in range(NCH):
                    if conv:
                        @em("gpsimd")
                        def _(e, b=b, c=c, ww=ww, wfree=wv(SM["wqf"], -1), wwb=wv(SM["wwb"], 0)):
                            if wfree:
                                e.wait_ge(*wfree)
                            if wwb:
                                e.wait_ge(*wwb)
                            e.dma_start(out=wob[c % 2][:, 0:ww],
                                        in_=wo_in[c * 128:(c + 1) * 128, b * 640:b * 640 + ww]).then_inc(SM[f"wqlg{c % 2}"].h, 16)
                        wldw = (SM[f"wqlg{c % 2}"].h, SM[f"wqlg{c % 2}"].bump(16))

                        @em("sync")
                        def _(e, b=b, c=c, ww=ww, wldw=wldw):
                            e.wait_ge(*wldw)
                            e.dma_start(out=woB[c * 128:(c + 1) * 128, b * 640:b * 640 + ww],
                                        in_=wob[c % 2][:, 0:ww]).then_inc(SM["wwb"].h, 16)
                        SM["wwb"].bump(16)
                    else:
                        @em("sync")
                        def _(e, b=b, c=c, ww=ww, wfree=wv(SM["wqf"], -1),
                              wwb=wv(SM["wwb"], 0) if (b == 0 and c == 0 and tq == 1) else None):
                            if wwb:
                                e.wait_ge(*wwb)
                            if wfree:
                                e.wait_ge(*wfree)
                            e.dma_start(out=wob[c % 2][:, 0:ww],
                                        in_=woB[c * 128:(c + 1) * 128, b * 640:b * 640 + ww]).then_inc(SM[f"wql{c % 2}"].h, 16)
                        wldw = (SM[f"wql{c % 2}"].h, SM[f"wql{c % 2}"].bump(16))

                    @em("tensor")
                    def _(e, c=c, units=tuple(units), wldw=wldw, wah=all_heads,
                          wev=wv(SM["qev"], 0) if c == 0 else None,
                          wex=wv(SM["exd"], 0) if c == 0 else None,
                          wrc=wv(SM["rcd"], 0) if c == 0 else None,
                          wpo=wv(SM["ppd"], 0) if c == 0 else None):
                        e.wait_ge(*wldw)
                        if c == 0:
                            e.wait_ge(SM["aod"].h, wah)
                            if wpo:
                                e.wait_ge(*wpo)
                            if wev:
                                e.wait_ge(*wev)
                            if wex:
                                e.wait_ge(*wex)
                            if wrc:
                                e.wait_ge(*wrc)
                        last = None
                        for u in units:
                            last = e.matmul(pp[u][:], wob[c % 2][:, u * 128:(u + 1) * 128],
                                            ao[:, c * T:(c + 1) * T], start=(c == 0), stop=(c == NCH - 1))
                        last.then_inc(SM["wqf"].h, 1)
                    SM["wqf"].bump(1)
                w_wmm = SM["wqf"].c

                for u in units:
                    ot = b * 5 + u

                    @em("scalar")
                    def _(e, ot=ot, t0=t0, wfree=wv(SM["qev"], -1)):
                        if wfree:
                            e.wait_ge(*wfree)
                        e.dma_start(out=xfs[ot % 2][:], in_=src_fm3[ot][:, t0:t0 + T]).then_inc(SM[f"xfl{ot % 2}"].h, 16)
                    wxl = (SM[f"xfl{ot % 2}"].h, SM[f"xfl{ot % 2}"].bump(16))

                    @em("vector")
                    def _(e, ot=ot, u=u, wmm=w_wmm, wxl=wxl, wowb=wv(SM["owb"], 0)):
                        e.wait_ge(SM["wqf"].h, wmm)
                        e.wait_ge(*wxl)
                        if wowb:
                            e.wait_ge(*wowb)
                        e.tensor_add(ogs[ot % 2][:], pp[u][:], xfs[ot % 2][:]).then_inc(SM["qev"].h, 1)
                    w_ev2 = SM["qev"].bump(1)

                    if not dst_is_out:
                        @em("sync")
                        def _(e, ot=ot, t0=t0, w=w_ev2):
                            e.wait_ge(SM["qev"].h, w)
                            e.dma_start(out=xfmB[ot][:, t0:t0 + T], in_=ogs[ot % 2][:]).then_inc(SM["owb"].h, 16)
                        SM["owb"].bump(16)
                    else:
                        @em("tensor")
                        def _(e, ot=ot, u=u, w=w_ev2, wpo=wv(SM["ppd"], 0)):
                            e.wait_ge(SM["qev"].h, w)
                            if wpo:
                                e.wait_ge(*wpo)
                            last = None
                            for j in range(4):
                                last = e.transpose(pp[u][:, j * 128:(j + 1) * 128],
                                                   ogs[ot % 2][:, j * 128:(j + 1) * 128], idnf[:])
                            last.then_inc(SM["nmm"].h, 1)
                        w_tr = SM["nmm"].bump(1)

                        @em("scalar")
                        def _(e, u=u, w=w_tr, wwb=wv(SM["owb"], 0)):
                            e.wait_ge(SM["nmm"].h, w)
                            if wwb:
                                e.wait_ge(*wwb)
                            e.copy(otm2[:], pp[u][:]).then_inc(SM["ppd"].h, 1)
                        w_ot2 = SM["ppd"].bump(1)

                        @em("sync")
                        def _(e, ot=ot, tq=tq, w=w_ot2):
                            e.wait_ge(SM["ppd"].h, w)
                            e.dma_start(out=ov2[tq, :, :, ot, :],
                                        in_=otm2.ap().rearrange("i (j m) -> i j m", m=128)).then_inc(SM["owb"].h, 16)
                        SM["owb"].bump(16)
        aes.close()
        kv_stack.close()
        phase_entry_waits()

    if "f2" in phases:
        src3 = xfmB if "attn" in phases else xfmI
        stk = ExitStack()
        ffn_phase("b", src3, None, out, "f1" not in phases, stk,
                  (SM["owb"].h, SM["owb"].c) if "attn" not in phases else None)
        stk.close()

    @em("sync")
    def _(e, w=SM["owb"].c):
        e.wait_ge(SM["owb"].h, w)

    with nc.Block() as blk:
        @blk.tensor
        def _(e):
            for f in prog["tensor"]:
                f(e)

        @blk.scalar
        def _(e):
            for f in prog["scalar"]:
                f(e)

        @blk.vector
        def _(e):
            for f in prog["vector"]:
                f(e)

        @blk.gpsimd
        def _(e):
            for f in prog["gpsimd"]:
                f(e)

        @blk.sync
        def _(e):
            for f in prog["sync"]:
                f(e)

    ES.close()
    return nc


def host_inputs(inp):
    wqkv = np.asarray(inp["L0_attn_w_qkv"], dtype=np.float32)
    wo = np.ascontiguousarray(np.asarray(inp["L0_attn_w_o"], dtype=np.float32))
    perm = np.concatenate([np.arange(0, 128, 2), np.arange(1, 128, 2)])
    wq = wqkv.copy()
    for h in range(NH):
        wq[:, h * 128:(h + 1) * 128] = wqkv[:, h * 128 + perm]
    for kh in range(NKV):
        b0 = D + kh * 128
        wq[:, b0:b0 + 128] = wqkv[:, b0 + perm]
    wq = np.ascontiguousarray(wq)
    gT = np.ascontiguousarray(np.asarray(inp["L0_ffn_gate_wq"], dtype=np.int32).T)
    uT = np.ascontiguousarray(np.asarray(inp["L0_ffn_up_wq"], dtype=np.int32).T)
    dT = np.ascontiguousarray(np.asarray(inp["L0_ffn_down_wq"], dtype=np.int32).T)
    gs = np.ascontiguousarray(np.asarray(inp["L0_ffn_gate_scale"], dtype=np.float32).reshape(NI, 128).T)
    us = np.ascontiguousarray(np.asarray(inp["L0_ffn_up_scale"], dtype=np.float32).reshape(NI, 128).T)
    ds = np.ascontiguousarray(np.asarray(inp["L0_ffn_down_scale"], dtype=np.float32).reshape(NCH, 128).T)
    idn = np.eye(128, dtype=ml_dtypes.bfloat16)
    idnf = np.eye(128, dtype=np.float32)
    j = np.arange(64, dtype=np.float64)
    invf = 1.0 / (10000.0 ** (2.0 * j / 128.0))
    t = np.arange(S, dtype=np.float64)
    ang = invf[:, None] * t[None, :]
    cosq = np.cos(ang).astype(np.float32)
    sinq = np.sin(ang).astype(np.float32)
    ii = np.arange(128)[:, None]
    jj = np.arange(T)[None, :]
    msk = np.concatenate([(ii <= jj - dd * 128) for dd in range(4)], axis=1).astype(ml_dtypes.bfloat16)
    return {"gT": gT, "uT": uT, "dT": dT, "wqkv": wq, "wo": wo,
            "gs": gs, "us": us, "ds": ds, "idn": idn, "idnf": idnf,
            "cosq": cosq, "sinq": sinq, "msk": msk}


def kernel(**inp):
    nc = build_nc()
    base = host_inputs(inp)
    xall = np.asarray(inp["L0_attn_x"], dtype=np.float32)
    in_maps = [{**base, "x": np.ascontiguousarray(xall[c])} for c in range(8)]
    res = run_bass_kernel_spmd(nc, in_maps, core_ids=list(range(8)))
    return np.stack([np.asarray(res.results[c]["out"], dtype=np.float32) for c in range(8)], axis=0)


# version 04
# speedup vs baseline: 1.0619x; 1.0619x over previous
import numpy as np
import ml_dtypes
import concourse.bass as bass
import concourse.mybir as mybir
from concourse.bass_utils import run_bass_kernel_spmd
from contextlib import ExitStack

F32 = mybir.dt.float32
BF16 = mybir.dt.bfloat16
I32 = mybir.dt.int32
AF = mybir.ActivationFunctionType
ALU = mybir.AluOpType
AX = mybir.AxisListType

S = 2048
D = 4096
II = 11008
NI = II // 128
NCH = D // 128
T = 512
NTT = S // T
NH = 32
NKV = 8
HD = 128
QKVO = D + 2 * NKV * HD
RSCL = float(1.0 / np.sqrt(128.0))
ENGS = ("tensor", "scalar", "vector", "gpsimd", "sync")

PHASES = ("f1", "qkv", "attn", "f2")


class Sem:
    __slots__ = ("h", "c")

    def __init__(self, es, nc, name):
        self.h = es.enter_context(nc.semaphore(name))
        self.c = 0

    def bump(self, n=1):
        self.c += n
        return self.c


def build_nc(phases=PHASES):
    nc = bass.Bass("TRN2", target_bir_lowering=False)
    x_in = nc.dram_tensor("x", [S, D], F32, kind="ExternalInput").ap()
    gT = nc.dram_tensor("gT", [D, II], I32, kind="ExternalInput").ap()
    uT = nc.dram_tensor("uT", [D, II], I32, kind="ExternalInput").ap()
    dT = nc.dram_tensor("dT", [II, D], I32, kind="ExternalInput").ap()
    wqkv = nc.dram_tensor("wqkv", [D, QKVO], F32, kind="ExternalInput").ap()
    wo_in = nc.dram_tensor("wo", [D, D], F32, kind="ExternalInput").ap()
    gs_in = nc.dram_tensor("gs", [128, NI], F32, kind="ExternalInput").ap()
    us_in = nc.dram_tensor("us", [128, NI], F32, kind="ExternalInput").ap()
    ds_in = nc.dram_tensor("ds", [128, NCH], F32, kind="ExternalInput").ap()
    idn_in = nc.dram_tensor("idn", [128, 128], BF16, kind="ExternalInput").ap()
    idnf_in = nc.dram_tensor("idnf", [128, 128], F32, kind="ExternalInput").ap()
    cos_in = nc.dram_tensor("cosq", [64, S], F32, kind="ExternalInput").ap()
    sin_in = nc.dram_tensor("sinq", [64, S], F32, kind="ExternalInput").ap()
    msk_in = nc.dram_tensor("msk", [128, 4 * T], BF16, kind="ExternalInput").ap()
    out = nc.dram_tensor("out", [S, D], F32, kind="ExternalOutput").ap()

    guB = nc.dram_tensor("guB", [NI, 2, 128, D], BF16, kind="Internal").ap()
    dB = nc.dram_tensor("dB", [NCH, 128, II], BF16, kind="Internal").ap()
    wqB = nc.dram_tensor("wqB", [D, QKVO], BF16, kind="Internal").ap()
    woB = nc.dram_tensor("woB", [D, D], BF16, kind="Internal").ap()
    xfmI = nc.dram_tensor("xfmI", [NCH, 128, S], F32, kind="Internal").ap()
    xfmA = nc.dram_tensor("xfmA", [NCH, 128, S], F32, kind="Internal").ap()
    xfmB = nc.dram_tensor("xfmB", [NCH, 128, S], F32, kind="Internal").ap()
    qscr = nc.dram_tensor("qscr", [NH, 128, S], BF16, kind="Internal").ap()

    ES = ExitStack()
    E = ES.enter_context

    idn = E(nc.sbuf_tensor("idn_s", [128, 128], BF16))
    idnf = E(nc.sbuf_tensor("idnf_s", [128, 128], F32))
    ones_cb = E(nc.sbuf_tensor("ones_cb", [128, 1], BF16))
    ones_rf = E(nc.sbuf_tensor("ones_rf", [1, 128], F32))
    gsus = E(nc.sbuf_tensor("gsus", [128, NI], F32))
    us_s = E(nc.sbuf_tensor("us_s", [128, NI], F32))
    dsch = E(nc.sbuf_tensor("dsch", [128, NCH], F32))
    negmbc = E(nc.sbuf_tensor("negmbc", [128, NH], F32))
    qn = E(nc.sbuf_tensor("qn", [1, NH], F32))
    kn = E(nc.sbuf_tensor("kn", [1, NKV], F32))
    knr = E(nc.sbuf_tensor("knr", [1, NH], F32))
    sm1 = E(nc.sbuf_tensor("sm1", [1, NH], F32))
    srow = [E(nc.sbuf_tensor(f"srow{i}", [1, T], F32)) for i in range(3)]
    a_bc = E(nc.sbuf_tensor("a_bc", [128, T], F32))
    b_bc = E(nc.sbuf_tensor("b_bc", [128, T], F32))
    cnv = [E(nc.sbuf_tensor(f"cnv{i}", [128, T], BF16)) for i in range(4)]
    lnt = E(nc.sbuf_tensor("lnt", [128, T], F32))
    xfs = [E(nc.sbuf_tensor(f"xf{i}", [128, T], F32)) for i in range(2)]
    ogs = [E(nc.sbuf_tensor(f"og{i}", [128, T], F32)) for i in range(2)]

    pp = [E(nc.psum_tensor(f"pp{i}", [128, 512], F32)) for i in range(5)]
    pS1 = E(nc.psum_tensor("pS1", [1, 512], F32))
    pS2 = E(nc.psum_tensor("pS2", [1, 512], F32))
    pT = E(nc.psum_tensor("pT", [128, 512], BF16))

    sem_names = ["cld", "wgl0", "wgl1", "wglg0", "wglg1", "wdlg0", "wdlg1", "wqlg0", "wqlg1",
                 "l1d", "uev", "gud", "wwb", "wdl0", "wdl1", "wdf",
                 "dev", "xfl0", "xfl1", "xnb", "owb", "nsq", "nmm", "nrd",
                 "wql0", "wql1", "wqf", "qev", "qwb0", "qwb1", "tvd", "vtd",
                 "qld", "exd", "mkd", "pvd", "aod", "rcd", "std", "ppd"]
    SM = {n: Sem(ES, nc, n) for n in sem_names}

    prog = {e: [] for e in ENGS}

    def em(eng):
        def deco(f):
            prog[eng].append(f)
        return deco

    def wv(sem, delta=0):
        v = sem.c + delta
        return (sem.h, v) if v >= 1 else None

    def wlist(*sems):
        return tuple(x for x in (wv(s) for s in sems) if x)

    def phase_entry_waits():
        caps = [(s.h, s.c) for s in SM.values() if s.c >= 1]
        for eng in ENGS:
            @em(eng)
            def _(e, caps=tuple(caps)):
                for h, v in caps:
                    e.wait_ge(h, v)

    @em("sync")
    def _(e):
        e.dma_start(out=idn[:], in_=idn_in).then_inc(SM["cld"].h, 16)
        e.dma_start(out=idnf[:], in_=idnf_in).then_inc(SM["cld"].h, 16)
        e.dma_start(out=gsus[:], in_=gs_in).then_inc(SM["cld"].h, 16)
        e.dma_start(out=us_s[:], in_=us_in).then_inc(SM["cld"].h, 16)
        e.dma_start(out=dsch[:], in_=ds_in).then_inc(SM["cld"].h, 16)
    w_cld0 = SM["cld"].bump(5 * 16)

    @em("vector")
    def _(e, w=w_cld0):
        e.memset(ones_cb[:], 1.0)
        e.memset(ones_rf[:], 1.0)
        e.memset(qn[:], 0.0)
        e.memset(kn[:], 0.0)
        e.wait_ge(SM["cld"].h, w)
        e.tensor_mul(gsus[:], gsus[:], us_s[:])
        e.tensor_scalar_mul(dsch[:], dsch[:], 0.5).then_inc(SM["std"].h, 1)
    SM["std"].bump(1)
    w_consts = SM["std"].c

    def emit_prepass():
        xv = x_in.rearrange("(q j i) (c m) -> i j q c m", i=128, j=4, m=128)
        k = 0
        for c in range(NCH):
            for q in range(NTT):
                slot = k % 2

                @em("sync")
                def _(e, c=c, q=q, slot=slot, wfree=wv(SM["nmm"], -1)):
                    if wfree:
                        e.wait_ge(*wfree)
                    e.dma_start(out=xfs[slot].ap().rearrange("i (j m) -> i j m", m=128),
                                in_=xv[:, :, q, c, :]).then_inc(SM[f"xfl{slot}"].h, 16)
                w_ld = (SM[f"xfl{slot}"].h, SM[f"xfl{slot}"].bump(16))

                @em("tensor")
                def _(e, slot=slot, w=w_ld, wst=w_consts, wev=wv(SM["nsq"], -1)):
                    e.wait_ge(*w)
                    e.wait_ge(SM["std"].h, wst)
                    if wev:
                        e.wait_ge(*wev)
                    last = None
                    for j in range(4):
                        last = e.transpose(pp[slot][:, j * 128:(j + 1) * 128],
                                           xfs[slot][:, j * 128:(j + 1) * 128], idnf[:])
                    last.then_inc(SM["nmm"].h, 1)
                w_tr = SM["nmm"].bump(1)

                @em("scalar")
                def _(e, slot=slot, w=w_tr, wwb=wv(SM["owb"], 0)):
                    e.wait_ge(SM["nmm"].h, w)
                    if wwb:
                        e.wait_ge(*wwb)
                    e.copy(ogs[slot][:], pp[slot][:]).then_inc(SM["nsq"].h, 1)
                w_ev = SM["nsq"].bump(1)

                @em("sync")
                def _(e, c=c, q=q, slot=slot, w=w_ev):
                    e.wait_ge(SM["nsq"].h, w)
                    e.dma_start(out=xfmI[c][:, q * T:(q + 1) * T], in_=ogs[slot][:]).then_inc(SM["owb"].h, 16)
                SM["owb"].bump(16)
                k += 1
        return SM["owb"].c

    def ln_tile(src_fm3, tt, xn_guard_sem, tens_guards, src_wait, xn):
        t0 = tt * T
        for c in range(NCH):
            @em("scalar")
            def _(e, c=c, t0=t0, wfree=wv(SM["nmm"], -1), sw=src_wait if c == 0 else None,
                  wdev=wlist(SM["dev"], SM["qev"]) if c == 0 else (),
                  wxb=wv(SM["xnb"], 0) if c < 2 else None):
                if sw:
                    e.wait_ge(*sw)
                for hvv in wdev:
                    e.wait_ge(*hvv)
                if wxb:
                    e.wait_ge(*wxb)
                if wfree:
                    e.wait_ge(*wfree)
                e.dma_start(out=xfs[c % 2][:], in_=src_fm3[c][:, t0:t0 + T]).then_inc(SM[f"xfl{c % 2}"].h, 16)
            w_ld = (SM[f"xfl{c % 2}"].h, SM[f"xfl{c % 2}"].bump(16))

            @em("scalar")
            def _(e, c=c, w=w_ld, wmm=wv(SM["nmm"], -1)):
                e.wait_ge(*w)
                if wmm:
                    e.wait_ge(*wmm)
                e.copy(cnv[2 * (c % 2)][:], xfs[c % 2][:])
                e.activation(cnv[2 * (c % 2) + 1][:], xfs[c % 2][:], AF.Square).then_inc(SM["nsq"].h, 1)
            w_cv = SM["nsq"].bump(1)

            @em("tensor")
            def _(e, c=c, w=w_cv, tg=tens_guards if c == 0 else ()):
                e.wait_ge(SM["nsq"].h, w)
                for hvv in tg:
                    e.wait_ge(*hvv)
                e.matmul(pS1[:], ones_cb[:], cnv[2 * (c % 2)][:], start=(c == 0), stop=(c == NCH - 1))
                e.matmul(pS2[:], ones_cb[:], cnv[2 * (c % 2) + 1][:], start=(c == 0), stop=(c == NCH - 1)).then_inc(SM["nmm"].h, 1)
            SM["nmm"].bump(1)
        w_stat = SM["nmm"].c

        @em("vector")
        def _(e, w=w_stat):
            e.wait_ge(SM["nmm"].h, w)
            m, r, t2 = srow
            e.tensor_scalar_mul(m[:], pS1[0:1, 0:T], 1.0 / D)
            e.tensor_scalar_mul(r[:], pS2[0:1, 0:T], 1.0 / D)
            e.drain()
            e.tensor_tensor(t2[:], m[:], m[:], ALU.mult)
            e.drain()
            e.tensor_sub(r[:], r[:], t2[:])
            e.drain()
            e.tensor_scalar_add(r[:], r[:], 1e-5)
            e.drain()
            e.tensor_copy(t2[:], r[:])
            e.drain()
            e.reciprocal(r[:], r[:]).then_inc(SM["nsq"].h, 1)
        w1 = SM["nsq"].bump(1)

        @em("scalar")
        def _(e, w=w1):
            e.wait_ge(SM["nsq"].h, w)
            e.activation(srow[1][:], srow[1][:], AF.Sqrt).then_inc(SM["nsq"].h, 1)
        w2 = SM["nsq"].bump(1)

        @em("vector")
        def _(e, w=w2):
            e.wait_ge(SM["nsq"].h, w)
            m, r, t2 = srow
            e.tensor_tensor(lnt[0:1, :], r[:], r[:], ALU.mult)
            e.drain()
            e.tensor_tensor(lnt[0:1, :], lnt[0:1, :], t2[:], ALU.mult)
            e.drain()
            e.tensor_scalar(lnt[0:1, :], lnt[0:1, :], -0.5, 1.5, ALU.mult, ALU.add)
            e.drain()
            e.tensor_tensor(r[:], r[:], lnt[0:1, :], ALU.mult)
            e.drain()
            e.scalar_tensor_tensor(m[:], m[:], -1.0, r[:], ALU.mult, ALU.mult).then_inc(SM["nsq"].h, 1)
        w3 = SM["nsq"].bump(1)

        @em("tensor")
        def _(e, w=w3, wab=wv(SM["xnb"], 0), wpo=wv(SM["ppd"], 0)):
            e.wait_ge(SM["nsq"].h, w)
            if wab:
                e.wait_ge(*wab)
            if wpo:
                e.wait_ge(*wpo)
            e.matmul(pp[0][:, 0:T], ones_rf[:], srow[1][:], start=True, stop=True)
            e.matmul(pp[1][:, 0:T], ones_rf[:], srow[0][:], start=True, stop=True).then_inc(SM["nmm"].h, 1)
        w4 = SM["nmm"].bump(1)

        @em("scalar")
        def _(e, w=w4):
            e.wait_ge(SM["nmm"].h, w)
            e.copy(a_bc[:], pp[0][:, 0:T])
            e.copy(b_bc[:], pp[1][:, 0:T]).then_inc(SM["nsq"].h, 1)
        w_abc = SM["nsq"].bump(1)

        xn_w = (xn_guard_sem.h, xn_guard_sem.c) if xn_guard_sem.c >= 1 else None
        for c2 in range(NCH // 2):
            for c in (2 * c2, 2 * c2 + 1):
                @em("scalar")
                def _(e, c=c, t0=t0, wfree=wv(SM["xnb"], -1 if c % 2 == 0 else 0), wab=w_abc):
                    e.wait_ge(SM["nsq"].h, wab)
                    if wfree:
                        e.wait_ge(*wfree)
                    e.dma_start(out=xfs[c % 2][:], in_=src_fm3[c][:, t0:t0 + T]).then_inc(SM[f"xfl{c % 2}"].h, 16)
                SM[f"xfl{c % 2}"].bump(16)
            wld = ((SM["xfl0"].h, SM["xfl0"].c), (SM["xfl1"].h, SM["xfl1"].c))

            @em("vector")
            def _(e, c2=c2, wld=wld, wab=w_abc, xw=xn_w if c2 == 0 else None):
                e.wait_ge(*wld[0])
                e.wait_ge(*wld[1])
                e.wait_ge(SM["nsq"].h, wab)
                if xw:
                    e.wait_ge(*xw)
                ca, cb = 2 * c2, 2 * c2 + 1
                e.tensor_tensor(lnt[:], xfs[0][:], a_bc[:], ALU.mult)
                e.tensor_tensor(ogs[0][:], xfs[1][:], a_bc[:], ALU.mult)
                e.drain()
                e.tensor_add(xn[:, ca * T:(ca + 1) * T], lnt[:], b_bc[:]).then_inc(SM["xnb"].h, 1)
                e.tensor_add(xn[:, cb * T:(cb + 1) * T], ogs[0][:], b_bc[:]).then_inc(SM["xnb"].h, 1)
            SM["xnb"].bump(2)
        return SM["xnb"].c

    def ffn_phase(tag, src_fm3, dst_fm3, dst_tm, first, stk, src_wait):
        xn = stk.enter_context(nc.sbuf_tensor(f"xn{tag}", [128, NCH * T], BF16))
        gu = stk.enter_context(nc.sbuf_tensor(f"gu_{tag}", [128, NI * T], BF16))
        wgu = [stk.enter_context(nc.sbuf_tensor(f"wgu{tag}{i}", [128, 2 * D], BF16)) for i in range(2)]
        wd = [stk.enter_context(nc.sbuf_tensor(f"wd{tag}{i}", [128, II // 2], BF16)) for i in range(2)]
        usb = [stk.enter_context(nc.sbuf_tensor(f"us{tag}{i}", [128, T], BF16)) for i in range(2)]
        otm = stk.enter_context(nc.sbuf_tensor(f"otm{tag}", [128, T], F32)) if dst_tm is not None else None
        gv = gT.rearrange("(c p) (t m) -> p c t m", p=128, m=128)
        uv = uT.rearrange("(c p) (t m) -> p c t m", p=128, m=128)
        dv = dT.rearrange("(j p) (h m) -> p j h m", p=128, m=128)
        if dst_tm is not None:
            ov = dst_tm.rearrange("(q j i) (c m) -> q i j c m", i=128, j=4, m=128)

        for tt in range(NTT):
            t0 = tt * T
            conv = first and tt == 0
            xn_tgt = ln_tile(src_fm3, tt, SM["l1d"],
                             wlist(SM["nrd"], SM["rcd"], SM["gud"], SM["uev"], SM["qev"], SM["exd"], SM["dev"]),
                             src_wait if tt == 0 else None, xn)

            for tau in range(NI):
                slot = tau % 2
                if conv:
                    @em("gpsimd")
                    def _(e, tau=tau, slot=slot, wfree=wv(SM["l1d"], -1), wwb=wv(SM["wwb"], 0)):
                        if wfree:
                            e.wait_ge(*wfree)
                        if wwb:
                            e.wait_ge(*wwb)
                        e.dma_start(out=wgu[slot].ap().rearrange("p (c m) -> p c m", m=128)[:, 0:NCH, :],
                                    in_=gv[:, :, tau, :]).then_inc(SM[f"wglg{slot}"].h, 16)
                        e.dma_start(out=wgu[slot].ap().rearrange("p (c m) -> p c m", m=128)[:, NCH:2 * NCH, :],
                                    in_=uv[:, :, tau, :]).then_inc(SM[f"wglg{slot}"].h, 16)
                    wld = (SM[f"wglg{slot}"].h, SM[f"wglg{slot}"].bump(32))

                    @em("sync")
                    def _(e, tau=tau, slot=slot, wld=wld):
                        e.wait_ge(*wld)
                        e.dma_start(out=guB[tau].rearrange("t p f -> p t f"),
                                    in_=wgu[slot].ap().rearrange("p (t f) -> p t f", f=D)).then_inc(SM["wwb"].h, 16)
                    SM["wwb"].bump(16)
                else:
                    @em("sync")
                    def _(e, tau=tau, slot=slot, wfree=wv(SM["l1d"], -1),
                          wwb=wv(SM["wwb"], 0) if (tau == 0 and first and tt == 1) else None):
                        if wwb:
                            e.wait_ge(*wwb)
                        if wfree:
                            e.wait_ge(*wfree)
                        e.dma_start(out=wgu[slot].ap().rearrange("p (t f) -> p t f", f=D),
                                    in_=guB[tau].rearrange("t p f -> p t f")).then_inc(SM[f"wgl{slot}"].h, 16)
                    wld = (SM[f"wgl{slot}"].h, SM[f"wgl{slot}"].bump(16))

                @em("tensor")
                def _(e, tau=tau, slot=slot, wld=wld, xw=xn_tgt,
                      wgud=wv(SM["gud"], -1), wue=wv(SM["uev"], -1),
                      wpo=wv(SM["ppd"], 0) if tau < 2 else None):
                    e.wait_ge(*wld)
                    if wpo:
                        e.wait_ge(*wpo)
                    if tau == 0:
                        e.wait_ge(SM["xnb"].h, xw)
                    if wgud:
                        e.wait_ge(*wgud)
                    if wue:
                        e.wait_ge(*wue)
                    for c in range(NCH):
                        e.matmul(pp[slot][:], wgu[slot][:, c * 128:(c + 1) * 128], xn[:, c * T:(c + 1) * T],
                                 start=(c == 0), stop=(c == NCH - 1))
                    for c in range(NCH):
                        ins = e.matmul(pp[2 + slot][:], wgu[slot][:, D + c * 128:D + (c + 1) * 128], xn[:, c * T:(c + 1) * T],
                                       start=(c == 0), stop=(c == NCH - 1))
                    ins.then_inc(SM["l1d"].h, 1)
                SM["l1d"].bump(1)

                @em("scalar")
                def _(e, slot=slot, w=SM["l1d"].c):
                    e.wait_ge(SM["l1d"].h, w)
                    e.copy(usb[slot][:], pp[2 + slot][:]).then_inc(SM["uev"].h, 1)
                SM["uev"].bump(1)

                @em("vector")
                def _(e, tau=tau, slot=slot, w=SM["uev"].c):
                    e.wait_ge(SM["uev"].h, w)
                    e.scalar_tensor_tensor(gu[:, tau * T:(tau + 1) * T], pp[slot][:], gsus[:, tau:tau + 1],
                                           usb[slot][:], ALU.mult, ALU.mult).then_inc(SM["gud"].h, 1)
                SM["gud"].bump(1)
            l1_all = SM["gud"].c

            for ht in range(NCH):
                for half in range(2):
                    hslot = (2 * ht + half) % 2
                    if conv:
                        @em("gpsimd")
                        def _(e, ht=ht, half=half, hslot=hslot, wfree=wv(SM["wdf"], -1), wwb=wv(SM["wwb"], 0)):
                            if wfree:
                                e.wait_ge(*wfree)
                            if wwb:
                                e.wait_ge(*wwb)
                            e.dma_start(out=wd[hslot].ap().rearrange("p (j m) -> p j m", m=128),
                                        in_=dv[:, half * 43:(half + 1) * 43, ht, :]).then_inc(SM[f"wdlg{hslot}"].h, 16)
                        wldd = (SM[f"wdlg{hslot}"].h, SM[f"wdlg{hslot}"].bump(16))

                        @em("sync")
                        def _(e, ht=ht, half=half, hslot=hslot, wldd=wldd):
                            e.wait_ge(*wldd)
                            e.dma_start(out=dB[ht].rearrange("p (hh x) -> p hh x", hh=2)[:, half, :],
                                        in_=wd[hslot][:]).then_inc(SM["wwb"].h, 16)
                        SM["wwb"].bump(16)
                    else:
                        @em("sync")
                        def _(e, ht=ht, half=half, hslot=hslot, wfree=wv(SM["wdf"], -1),
                              wwb=wv(SM["wwb"], 0) if (ht == 0 and half == 0 and first and tt == 1) else None):
                            if wwb:
                                e.wait_ge(*wwb)
                            if wfree:
                                e.wait_ge(*wfree)
                            e.dma_start(out=wd[hslot][:],
                                        in_=dB[ht].rearrange("p (hh x) -> p hh x", hh=2)[:, half, :]).then_inc(SM[f"wdl{hslot}"].h, 16)
                        wldd = (SM[f"wdl{hslot}"].h, SM[f"wdl{hslot}"].bump(16))

                    @em("tensor")
                    def _(e, ht=ht, half=half, hslot=hslot, wldd=wldd, wall=l1_all,
                          wde=wv(SM["dev"], 0) if half == 0 else None):
                        e.wait_ge(*wldd)
                        if ht == 0 and half == 0:
                            e.wait_ge(SM["gud"].h, wall)
                        if wde:
                            e.wait_ge(*wde)
                        last = None
                        for jj in range(43):
                            j = half * 43 + jj
                            last = e.matmul(pp[4][:], wd[hslot][:, jj * 128:(jj + 1) * 128],
                                            gu[:, j * T:(j + 1) * T], start=(j == 0), stop=(j == NI - 1))
                        last.then_inc(SM["wdf"].h, 1)
                    SM["wdf"].bump(1)
                w_dmm = SM["wdf"].c

                @em("scalar")
                def _(e, ht=ht, t0=t0, wfree=wv(SM["dev"], -1),
                      wxn=wv(SM["xnb"], 0) if ht == 0 else None):
                    if wfree:
                        e.wait_ge(*wfree)
                    if wxn:
                        e.wait_ge(*wxn)
                    e.dma_start(out=xfs[ht % 2][:], in_=src_fm3[ht][:, t0:t0 + T]).then_inc(SM[f"xfl{ht % 2}"].h, 16)
                wxl = (SM[f"xfl{ht % 2}"].h, SM[f"xfl{ht % 2}"].bump(16))

                @em("vector")
                def _(e, ht=ht, wmm=w_dmm, wxl=wxl, wowb=wv(SM["owb"], 0)):
                    e.wait_ge(SM["wdf"].h, wmm)
                    e.wait_ge(*wxl)
                    if wowb:
                        e.wait_ge(*wowb)
                    e.scalar_tensor_tensor(ogs[ht % 2][:], pp[4][:], dsch[:, ht:ht + 1],
                                           xfs[ht % 2][:], ALU.mult, ALU.add).then_inc(SM["dev"].h, 1)
                SM["dev"].bump(1)
                w_ev = SM["dev"].c

                if dst_fm3 is not None:
                    @em("sync")
                    def _(e, ht=ht, t0=t0, w=w_ev):
                        e.wait_ge(SM["dev"].h, w)
                        e.dma_start(out=dst_fm3[ht][:, t0:t0 + T], in_=ogs[ht % 2][:]).then_inc(SM["owb"].h, 16)
                    SM["owb"].bump(16)
                else:
                    @em("tensor")
                    def _(e, ht=ht, w=w_ev, wpo=wv(SM["ppd"], 0)):
                        e.wait_ge(SM["dev"].h, w)
                        if wpo:
                            e.wait_ge(*wpo)
                        last = None
                        for j in range(4):
                            last = e.transpose(pp[ht % 2][:, j * 128:(j + 1) * 128],
                                               ogs[ht % 2][:, j * 128:(j + 1) * 128], idnf[:])
                        last.then_inc(SM["nmm"].h, 1)
                    w_tr = SM["nmm"].bump(1)

                    @em("scalar")
                    def _(e, ht=ht, w=w_tr, wwb=wv(SM["owb"], 0)):
                        e.wait_ge(SM["nmm"].h, w)
                        if wwb:
                            e.wait_ge(*wwb)
                        e.copy(otm[:], pp[ht % 2][:]).then_inc(SM["ppd"].h, 1)
                    w_ot = SM["ppd"].bump(1)

                    @em("sync")
                    def _(e, ht=ht, tt=tt, w=w_ot):
                        e.wait_ge(SM["ppd"].h, w)
                        e.dma_start(out=ov[tt, :, :, ht, :],
                                    in_=otm.ap().rearrange("i (j m) -> i j m", m=128)).then_inc(SM["owb"].h, 16)
                    SM["owb"].bump(16)

    w_pre = emit_prepass()
    if "f1" in phases:
        stk = ExitStack()
        last_f1 = len(phases) == 1
        ffn_phase("a", xfmI, None if last_f1 else xfmA, out if last_f1 else None,
                  True, stk, (SM["owb"].h, w_pre))
        stk.close()
        phase_entry_waits()
        src_after_f1 = xfmA
    else:
        src_after_f1 = xfmI

    kv_stack = ExitStack()
    if "qkv" in phases:
        src_fm3 = src_after_f1
        kfm = kv_stack.enter_context(nc.sbuf_tensor("kfm", [128, NKV * S], BF16))
        vtm = kv_stack.enter_context(nc.sbuf_tensor("vtm", [128, (S // 128) * NKV * HD], BF16))
        qes = ExitStack()
        xn = qes.enter_context(nc.sbuf_tensor("xnq", [128, NCH * T], BF16))
        cosq = qes.enter_context(nc.sbuf_tensor("cosq_s", [64, S], F32))
        sinq = qes.enter_context(nc.sbuf_tensor("sinq_s", [64, S], F32))
        wqh = [qes.enter_context(nc.sbuf_tensor(f"wqh{i}", [128, 16 * 640], BF16)) for i in range(2)]
        qsb = [qes.enter_context(nc.sbuf_tensor(f"qsb{i}", [128, T], BF16)) for i in range(2)]
        rsc = [qes.enter_context(nc.sbuf_tensor(f"rsc{i}", [64, T], F32)) for i in range(4)]
        vst = [qes.enter_context(nc.sbuf_tensor(f"vst{i}", [128, T], BF16)) for i in range(2)]
        sqsc = qes.enter_context(nc.sbuf_tensor("sqsc", [128, T], BF16))
        nrm = qes.enter_context(nc.sbuf_tensor("nrm", [1, 1], F32))

        @em("sync")
        def _(e):
            e.dma_start(out=cosq[:], in_=cos_in).then_inc(SM["cld"].h, 16)
            e.dma_start(out=sinq[:], in_=sin_in).then_inc(SM["cld"].h, 16)
        w_rope = SM["cld"].bump(32)

        wqv4 = wqkv.rearrange("(cc ci p) col -> p cc ci col", p=128, ci=16)
        wqB4 = wqB.rearrange("(cc ci p) col -> p cc ci col", p=128, ci=16)
        qhalf_done = [0, 0]
        nqu = 0
        q_sq = []
        for tt in range(NTT):
            t0 = tt * T
            conv = tt == 0
            xn_tgt = ln_tile(src_fm3, tt, SM["wqf"],
                             wlist(SM["nrd"], SM["rcd"], SM["qev"], SM["exd"]),
                             (SM["owb"].h, SM["owb"].c) if (tt == 0 and "f1" not in phases) else None, xn)

            for b in range(10):
                nu = 5 if b < 9 else 3
                wq_base = SM["wqf"].c
                half_waits = []
                for hh in range(2):
                    par = (2 * b + hh) % 2
                    if conv:
                        @em("gpsimd")
                        def _(e, b=b, hh=hh, par=par, nu=nu, wfree=qhalf_done[par], wwb=wv(SM["wwb"], 0)):
                            if wfree >= 1:
                                e.wait_ge(SM["wqf"].h, wfree)
                            if wwb:
                                e.wait_ge(*wwb)
                            nc_ = 128 * nu
                            e.dma_start(out=wqh[par].ap().rearrange("p (ci col) -> p ci col", col=640)[:, :, 0:nc_],
                                        in_=wqv4[:, hh, :, b * 640:b * 640 + nc_]).then_inc(SM[f"wqlg{par}"].h, 16)
                        wldq = (SM[f"wqlg{par}"].h, SM[f"wqlg{par}"].bump(16))

                        @em("sync")
                        def _(e, b=b, hh=hh, par=par, nu=nu, wldq=wldq):
                            e.wait_ge(*wldq)
                            nc_ = 128 * nu
                            e.dma_start(out=wqB4[:, hh, :, b * 640:b * 640 + nc_],
                                        in_=wqh[par].ap().rearrange("p (ci col) -> p ci col", col=640)[:, :, 0:nc_]).then_inc(SM["wwb"].h, 16)
                        SM["wwb"].bump(16)
                    else:
                        @em("sync")
                        def _(e, b=b, hh=hh, par=par, nu=nu, wfree=qhalf_done[par],
                              wwb=wv(SM["wwb"], 0) if (b == 0 and hh == 0 and tt == 1) else None):
                            if wwb:
                                e.wait_ge(*wwb)
                            if wfree >= 1:
                                e.wait_ge(SM["wqf"].h, wfree)
                            nc_ = 128 * nu
                            e.dma_start(out=wqh[par].ap().rearrange("p (ci col) -> p ci col", col=640)[:, :, 0:nc_],
                                        in_=wqB4[:, hh, :, b * 640:b * 640 + nc_]).then_inc(SM[f"wql{par}"].h, 16)
                        wldq = (SM[f"wql{par}"].h, SM[f"wql{par}"].bump(16))
                    half_waits.append(wldq)
                    qhalf_done[par] = wq_base + 16 * (hh + 1)

                for c in range(NCH):
                    par = (2 * b + c // 16) % 2
                    @em("tensor")
                    def _(e, b=b, c=c, nu=nu, par=par,
                          wldq=half_waits[c // 16] if c % 16 == 0 else None,
                          xw=xn_tgt, wev=wv(SM["qev"], 0) if c == 0 else None):
                        if wldq:
                            e.wait_ge(*wldq)
                        if c == 0:
                            e.wait_ge(SM["xnb"].h, xw)
                            if wev:
                                e.wait_ge(*wev)
                        last = None
                        for u in range(nu):
                            last = e.matmul(pp[u][:], wqh[par][:, (c % 16) * 640 + u * 128:(c % 16) * 640 + (u + 1) * 128],
                                            xn[:, c * T:(c + 1) * T], start=(c == 0), stop=(c == NCH - 1))
                        last.then_inc(SM["wqf"].h, 1)
                    SM["wqf"].bump(1)
                w_batch = SM["wqf"].c

                for u in range(nu):
                    ot = b * 5 + u
                    if ot < NH:
                        slot = nqu % 2

                        @em("vector")
                        def _(e, u=u, slot=slot, t0=t0, wmm=w_batch, wrope=w_rope,
                              wqwb=wv(SM[f"qwb{nqu % 2}"], 0),
                              wsq=(q_sq[nqu - 2] if nqu >= 2 else 0)):
                            e.wait_ge(SM["wqf"].h, wmm)
                            e.wait_ge(SM["cld"].h, wrope)
                            if wqwb:
                                e.wait_ge(*wqwb)
                            if wsq >= 1:
                                e.wait_ge(SM["nsq"].h, wsq)
                            psrc = pp[u]
                            cs = cosq[:, t0:t0 + T]
                            sn = sinq[:, t0:t0 + T]
                            pe = psrc[0:64, :]
                            po = psrc[64:128, :]
                            e.tensor_tensor(rsc[0][:], pe, cs, ALU.mult)
                            e.tensor_tensor(rsc[1][:], po, sn, ALU.mult)
                            e.tensor_tensor(rsc[2][:], pe, sn, ALU.mult)
                            e.tensor_tensor(rsc[3][:], po, cs, ALU.mult)
                            e.drain()
                            e.tensor_sub(qsb[slot][0:64, :], rsc[0][:], rsc[1][:])
                            e.tensor_add(qsb[slot][64:128, :], rsc[2][:], rsc[3][:]).then_inc(SM["qev"].h, 1)
                        w_rp = SM["qev"].bump(1)

                        @em("sync")
                        def _(e, ot=ot, slot=slot, t0=t0, w=w_rp):
                            e.wait_ge(SM["qev"].h, w)
                            e.dma_start(out=qscr[ot][:, t0:t0 + T], in_=qsb[slot][:]).then_inc(SM[f"qwb{slot}"].h, 16)
                        SM[f"qwb{slot}"].bump(16)

                        @em("scalar")
                        def _(e, slot=slot, w=w_rp, wn=wv(SM["nmm"], 0)):
                            e.wait_ge(SM["qev"].h, w)
                            if wn:
                                e.wait_ge(*wn)
                            e.activation(sqsc[:], qsb[slot][:], AF.Square).then_inc(SM["nsq"].h, 1)
                        w_sq = SM["nsq"].bump(1)
                        q_sq.append(w_sq)
                        nqu += 1

                        @em("tensor")
                        def _(e, w=w_sq, wr=wv(SM["nrd"], 0)):
                            e.wait_ge(SM["nsq"].h, w)
                            if wr:
                                e.wait_ge(*wr)
                            e.matmul(pS1[:], ones_cb[:], sqsc[:], start=True, stop=True).then_inc(SM["nmm"].h, 1)
                        w_nm = SM["nmm"].bump(1)

                        @em("vector")
                        def _(e, ot=ot, w=w_nm):
                            e.wait_ge(SM["nmm"].h, w)
                            e.reduce_max(nrm[:], pS1[0:1, 0:T], axis=AX.X)
                            e.drain()
                            e.tensor_max(qn[0:1, ot:ot + 1], qn[0:1, ot:ot + 1], nrm[:]).then_inc(SM["nrd"].h, 1)
                        SM["nrd"].bump(1)
                    elif ot < NH + NKV:
                        kh = ot - NH

                        @em("vector")
                        def _(e, kh=kh, u=u, t0=t0, wmm=w_batch, wrope=w_rope):
                            e.wait_ge(SM["wqf"].h, wmm)
                            e.wait_ge(SM["cld"].h, wrope)
                            psrc = pp[u]
                            cs = cosq[:, t0:t0 + T]
                            sn = sinq[:, t0:t0 + T]
                            pe = psrc[0:64, :]
                            po = psrc[64:128, :]
                            e.tensor_tensor(rsc[0][:], pe, cs, ALU.mult)
                            e.tensor_tensor(rsc[1][:], po, sn, ALU.mult)
                            e.tensor_tensor(rsc[2][:], pe, sn, ALU.mult)
                            e.tensor_tensor(rsc[3][:], po, cs, ALU.mult)
                            e.drain()
                            e.tensor_sub(kfm[0:64, kh * S + t0: kh * S + t0 + T], rsc[0][:], rsc[1][:])
                            e.tensor_add(kfm[64:128, kh * S + t0: kh * S + t0 + T], rsc[2][:], rsc[3][:]).then_inc(SM["qev"].h, 1)
                        w_rp = SM["qev"].bump(1)

                        @em("scalar")
                        def _(e, kh=kh, t0=t0, w=w_rp, wn=wv(SM["nmm"], 0)):
                            e.wait_ge(SM["qev"].h, w)
                            if wn:
                                e.wait_ge(*wn)
                            e.activation(sqsc[:], kfm[:, kh * S + t0: kh * S + t0 + T], AF.Square).then_inc(SM["nsq"].h, 1)
                        w_sq = SM["nsq"].bump(1)

                        @em("tensor")
                        def _(e, w=w_sq, wr=wv(SM["nrd"], 0)):
                            e.wait_ge(SM["nsq"].h, w)
                            if wr:
                                e.wait_ge(*wr)
                            e.matmul(pS1[:], ones_cb[:], sqsc[:], start=True, stop=True).then_inc(SM["nmm"].h, 1)
                        w_nm = SM["nmm"].bump(1)

                        @em("vector")
                        def _(e, kh=kh, w=w_nm):
                            e.wait_ge(SM["nmm"].h, w)
                            e.reduce_max(nrm[:], pS1[0:1, 0:T], axis=AX.X)
                            e.drain()
                            e.tensor_max(kn[0:1, kh:kh + 1], kn[0:1, kh:kh + 1], nrm[:]).then_inc(SM["nrd"].h, 1)
                        SM["nrd"].bump(1)
                    else:
                        vh = ot - NH - NKV
                        vslot = vh % 2

                        @em("scalar")
                        def _(e, u=u, vslot=vslot, wmm=w_batch, wtv=wv(SM["tvd"], -1)):
                            e.wait_ge(SM["wqf"].h, wmm)
                            if wtv:
                                e.wait_ge(*wtv)
                            e.copy(vst[vslot][:], pp[u][:]).then_inc(SM["qev"].h, 1)
                        w_vst = SM["qev"].bump(1)

                        @em("tensor")
                        def _(e, vslot=vslot, w=w_vst, wvt=wv(SM["vtd"], 0)):
                            e.wait_ge(SM["qev"].h, w)
                            if wvt:
                                e.wait_ge(*wvt)
                            last = None
                            for j in range(4):
                                last = e.transpose(pT[:, j * 128:(j + 1) * 128], vst[vslot][:, j * 128:(j + 1) * 128], idn[:])
                            last.then_inc(SM["tvd"].h, 1)
                        w_tv = SM["tvd"].bump(1)

                        @em("scalar")
                        def _(e, vh=vh, t0=t0, w=w_tv):
                            e.wait_ge(SM["tvd"].h, w)
                            ins = None
                            for j in range(4):
                                base = (t0 // 128 + j) * (NKV * HD) + vh * HD
                                ins = e.copy(vtm[:, base: base + HD], pT[:, j * 128:(j + 1) * 128])
                            ins.then_inc(SM["vtd"].h, 1)
                        SM["vtd"].bump(1)

        @em("vector")
        def _(e, w=SM["nrd"].c):
            e.wait_ge(SM["nrd"].h, w)
            for k in range(NKV):
                for r in range(4):
                    e.tensor_copy(knr[0:1, 4 * k + r:4 * k + r + 1], kn[0:1, k:k + 1])
            e.drain()
            e.tensor_tensor(sm1[:], qn[:], knr[:], ALU.mult).then_inc(SM["nsq"].h, 1)
        w1 = SM["nsq"].bump(1)

        @em("scalar")
        def _(e, w=w1):
            e.wait_ge(SM["nsq"].h, w)
            e.activation(sm1[:], sm1[:], AF.Sqrt).then_inc(SM["nsq"].h, 1)
        w2 = SM["nsq"].bump(1)

        @em("vector")
        def _(e, w=w2):
            e.wait_ge(SM["nsq"].h, w)
            e.tensor_scalar_mul(sm1[:], sm1[:], -RSCL).then_inc(SM["nsq"].h, 1)
        w3 = SM["nsq"].bump(1)

        @em("tensor")
        def _(e, w=w3):
            e.wait_ge(SM["nsq"].h, w)
            e.matmul(pp[0][:, 0:NH], ones_rf[:], sm1[:], start=True, stop=True).then_inc(SM["nmm"].h, 1)
        w4 = SM["nmm"].bump(1)

        @em("scalar")
        def _(e, w=w4):
            e.wait_ge(SM["nmm"].h, w)
            e.copy(negmbc[:], pp[0][:, 0:NH]).then_inc(SM["nsq"].h, 1)
        SM["nsq"].bump(1)
        qes.close()
        phase_entry_waits()

    if "attn" in phases:
        src_fm3 = src_after_f1
        dst_is_out = "f2" not in phases
        aes = ExitStack()
        qt = aes.enter_context(nc.sbuf_tensor("qt", [128, NH * T], BF16))
        ao = aes.enter_context(nc.sbuf_tensor("ao", [128, NH * T], BF16))
        msk = aes.enter_context(nc.sbuf_tensor("msk_s", [128, 4 * T], BF16))
        woh = [aes.enter_context(nc.sbuf_tensor(f"woh{i}", [128, 16 * 640], BF16)) for i in range(2)]
        pex = [aes.enter_context(nc.sbuf_tensor(f"pex{i}", [128, T], BF16)) for i in range(2)]
        rec = aes.enter_context(nc.sbuf_tensor("rec", [1, T], F32))
        rbc = aes.enter_context(nc.sbuf_tensor("rbc", [128, T], F32))
        otm2 = aes.enter_context(nc.sbuf_tensor("otm2", [128, T], F32)) if dst_is_out else None
        if dst_is_out:
            ov2 = out.rearrange("(q j i) (c m) -> q i j c m", i=128, j=4, m=128)

        @em("sync")
        def _(e):
            e.dma_start(out=msk[:], in_=msk_in).then_inc(SM["cld"].h, 16)
        w_msk = SM["cld"].bump(16)
        wov4 = wo_in.rearrange("(cc ci p) col -> p cc ci col", p=128, ci=16)
        woB4 = woB.rearrange("(cc ci p) col -> p cc ci col", p=128, ci=16)
        whalf_done = [0, 0]

        for tq in range(NTT):
            t0 = tq * T
            conv = tq == 0

            @em("sync")
            def _(e, t0=t0, waod=wv(SM["aod"], 0)):
                if waod:
                    e.wait_ge(*waod)
                e.dma_start(out=qt.ap().rearrange("p (h t) -> p h t", t=T),
                            in_=qscr.rearrange("h p t -> p h t")[:, :, t0:t0 + T]).then_inc(SM["qld"].h, 16)
            w_ql = SM["qld"].bump(16)

            nck = (t0 + T) // 128
            for h in range(NH):
                kvh = h // 4
                for ck in range(nck):
                    diag = ck >= t0 // 128

                    @em("tensor")
                    def _(e, h=h, ck=ck, kvh=kvh, w_ql=w_ql, wexp=wv(SM["exd"], -1),
                          waod=wv(SM["aod"], 0) if ck == 0 else None,
                          wrcd=wv(SM["rcd"], -1) if ck == 0 else None,
                          wqev=wv(SM["qev"], 0) if (h == 0 and ck == 0) else None):
                        if h == 0 and ck == 0:
                            e.wait_ge(SM["qld"].h, w_ql)
                            if wqev:
                                e.wait_ge(*wqev)
                        if ck == 0:
                            if waod:
                                e.wait_ge(*waod)
                            if wrcd:
                                e.wait_ge(*wrcd)
                        if wexp:
                            e.wait_ge(*wexp)
                        e.matmul(pp[ck % 2][:], kfm[:, kvh * S + ck * 128: kvh * S + (ck + 1) * 128],
                                 qt[:, h * T:(h + 1) * T], start=True, stop=True).then_inc(SM["qld"].h, 1)
                    SM["qld"].bump(1)

                    @em("scalar")
                    def _(e, h=h, ck=ck, wm=SM["qld"].c, wpv=wv(SM["pvd"], -1), w_msk=w_msk):
                        e.wait_ge(SM["qld"].h, wm)
                        if wpv:
                            e.wait_ge(*wpv)
                        e.wait_ge(SM["cld"].h, w_msk)
                        e.activation(pex[ck % 2][:], pp[ck % 2][:], AF.Exp,
                                     bias=negmbc[:, h:h + 1], scale=RSCL).then_inc(SM["exd"].h, 1)
                    w_ex = SM["exd"].bump(1)

                    if diag:
                        dd = ck - t0 // 128

                        @em("vector")
                        def _(e, ck=ck, dd=dd, w=w_ex):
                            e.wait_ge(SM["exd"].h, w)
                            e.tensor_tensor(pex[ck % 2][:], pex[ck % 2][:], msk[:, dd * T:(dd + 1) * T],
                                            ALU.mult).then_inc(SM["mkd"].h, 1)
                        SM["mkd"].bump(1)

                    @em("tensor")
                    def _(e, ck=ck, nck=nck, kvh=kvh, diag=diag, wex=w_ex, wmk=SM["mkd"].c):
                        if diag:
                            e.wait_ge(SM["mkd"].h, wmk)
                        else:
                            e.wait_ge(SM["exd"].h, wex)
                        e.matmul(pp[2][:], vtm[:, ck * (NKV * HD) + kvh * HD: ck * (NKV * HD) + (kvh + 1) * HD],
                                 pex[ck % 2][:], start=(ck == 0), stop=(ck == nck - 1))
                        e.matmul(pS1[:], ones_cb[:], pex[ck % 2][:],
                                 start=(ck == 0), stop=(ck == nck - 1)).then_inc(SM["pvd"].h, 1)
                    SM["pvd"].bump(1)
                w_pv = SM["pvd"].c

                @em("vector")
                def _(e, w=w_pv):
                    e.wait_ge(SM["pvd"].h, w)
                    e.reciprocal(rec[:], pS1[0:1, 0:T]).then_inc(SM["rcd"].h, 1)
                w_rc = SM["rcd"].bump(1)

                @em("tensor")
                def _(e, w=w_rc):
                    e.wait_ge(SM["rcd"].h, w)
                    e.matmul(pp[3][:, 0:T], ones_rf[:], rec[:], start=True, stop=True).then_inc(SM["nmm"].h, 1)
                w_bc2 = SM["nmm"].bump(1)

                @em("scalar")
                def _(e, w=w_bc2):
                    e.wait_ge(SM["nmm"].h, w)
                    e.copy(rbc[:], pp[3][:, 0:T]).then_inc(SM["rcd"].h, 1)
                w_rbc = SM["rcd"].bump(1)

                @em("vector")
                def _(e, h=h, w=w_rbc):
                    e.wait_ge(SM["rcd"].h, w)
                    e.tensor_tensor(ao[:, h * T:(h + 1) * T], pp[2][:], rbc[:], ALU.mult).then_inc(SM["aod"].h, 1)
                SM["aod"].bump(1)
            all_heads = SM["aod"].c

            for b in range(7):
                units = list(range(5 if b < 6 else 2))
                wo_base = SM["wqf"].c
                half_waits = []
                for hh in range(2):
                    par = (2 * b + hh) % 2
                    if conv:
                        @em("gpsimd")
                        def _(e, b=b, hh=hh, par=par, nu=len(units), wfree=whalf_done[par], wwb=wv(SM["wwb"], 0)):
                            if wfree >= 1:
                                e.wait_ge(SM["wqf"].h, wfree)
                            if wwb:
                                e.wait_ge(*wwb)
                            nc_ = 128 * nu
                            e.dma_start(out=woh[par].ap().rearrange("p (ci col) -> p ci col", col=640)[:, :, 0:nc_],
                                        in_=wov4[:, hh, :, b * 640:b * 640 + nc_]).then_inc(SM[f"wqlg{par}"].h, 16)
                        wldw = (SM[f"wqlg{par}"].h, SM[f"wqlg{par}"].bump(16))

                        @em("sync")
                        def _(e, b=b, hh=hh, par=par, nu=len(units), wldw=wldw):
                            e.wait_ge(*wldw)
                            nc_ = 128 * nu
                            e.dma_start(out=woB4[:, hh, :, b * 640:b * 640 + nc_],
                                        in_=woh[par].ap().rearrange("p (ci col) -> p ci col", col=640)[:, :, 0:nc_]).then_inc(SM["wwb"].h, 16)
                        SM["wwb"].bump(16)
                    else:
                        @em("sync")
                        def _(e, b=b, hh=hh, par=par, nu=len(units), wfree=whalf_done[par],
                              wwb=wv(SM["wwb"], 0) if (b == 0 and hh == 0 and tq == 1) else None):
                            if wwb:
                                e.wait_ge(*wwb)
                            if wfree >= 1:
                                e.wait_ge(SM["wqf"].h, wfree)
                            nc_ = 128 * nu
                            e.dma_start(out=woh[par].ap().rearrange("p (ci col) -> p ci col", col=640)[:, :, 0:nc_],
                                        in_=woB4[:, hh, :, b * 640:b * 640 + nc_]).then_inc(SM[f"wql{par}"].h, 16)
                        wldw = (SM[f"wql{par}"].h, SM[f"wql{par}"].bump(16))
                    half_waits.append(wldw)
                    whalf_done[par] = wo_base + 16 * (hh + 1)

                for c in range(NCH):
                    par = (2 * b + c // 16) % 2
                    @em("tensor")
                    def _(e, c=c, par=par, units=tuple(units),
                          wldw=half_waits[c // 16] if c % 16 == 0 else None, wah=all_heads,
                          wev=wv(SM["qev"], 0) if c == 0 else None,
                          wex=wv(SM["exd"], 0) if c == 0 else None,
                          wrc=wv(SM["rcd"], 0) if c == 0 else None,
                          wpo=wv(SM["ppd"], 0) if c == 0 else None):
                        if wldw:
                            e.wait_ge(*wldw)
                        if c == 0:
                            e.wait_ge(SM["aod"].h, wah)
                            if wpo:
                                e.wait_ge(*wpo)
                            if wev:
                                e.wait_ge(*wev)
                            if wex:
                                e.wait_ge(*wex)
                            if wrc:
                                e.wait_ge(*wrc)
                        last = None
                        for u in units:
                            last = e.matmul(pp[u][:], woh[par][:, (c % 16) * 640 + u * 128:(c % 16) * 640 + (u + 1) * 128],
                                            ao[:, c * T:(c + 1) * T], start=(c == 0), stop=(c == NCH - 1))
                        last.then_inc(SM["wqf"].h, 1)
                    SM["wqf"].bump(1)
                w_wmm = SM["wqf"].c

                for u in units:
                    ot = b * 5 + u

                    @em("scalar")
                    def _(e, ot=ot, t0=t0, wfree=wv(SM["qev"], -1)):
                        if wfree:
                            e.wait_ge(*wfree)
                        e.dma_start(out=xfs[ot % 2][:], in_=src_fm3[ot][:, t0:t0 + T]).then_inc(SM[f"xfl{ot % 2}"].h, 16)
                    wxl = (SM[f"xfl{ot % 2}"].h, SM[f"xfl{ot % 2}"].bump(16))

                    @em("vector")
                    def _(e, ot=ot, u=u, wmm=w_wmm, wxl=wxl, wowb=wv(SM["owb"], 0)):
                        e.wait_ge(SM["wqf"].h, wmm)
                        e.wait_ge(*wxl)
                        if wowb:
                            e.wait_ge(*wowb)
                        e.tensor_add(ogs[ot % 2][:], pp[u][:], xfs[ot % 2][:]).then_inc(SM["qev"].h, 1)
                    w_ev2 = SM["qev"].bump(1)

                    if not dst_is_out:
                        @em("sync")
                        def _(e, ot=ot, t0=t0, w=w_ev2):
                            e.wait_ge(SM["qev"].h, w)
                            e.dma_start(out=xfmB[ot][:, t0:t0 + T], in_=ogs[ot % 2][:]).then_inc(SM["owb"].h, 16)
                        SM["owb"].bump(16)
                    else:
                        @em("tensor")
                        def _(e, ot=ot, u=u, w=w_ev2, wpo=wv(SM["ppd"], 0)):
                            e.wait_ge(SM["qev"].h, w)
                            if wpo:
                                e.wait_ge(*wpo)
                            last = None
                            for j in range(4):
                                last = e.transpose(pp[u][:, j * 128:(j + 1) * 128],
                                                   ogs[ot % 2][:, j * 128:(j + 1) * 128], idnf[:])
                            last.then_inc(SM["nmm"].h, 1)
                        w_tr = SM["nmm"].bump(1)

                        @em("scalar")
                        def _(e, u=u, w=w_tr, wwb=wv(SM["owb"], 0)):
                            e.wait_ge(SM["nmm"].h, w)
                            if wwb:
                                e.wait_ge(*wwb)
                            e.copy(otm2[:], pp[u][:]).then_inc(SM["ppd"].h, 1)
                        w_ot2 = SM["ppd"].bump(1)

                        @em("sync")
                        def _(e, ot=ot, tq=tq, w=w_ot2):
                            e.wait_ge(SM["ppd"].h, w)
                            e.dma_start(out=ov2[tq, :, :, ot, :],
                                        in_=otm2.ap().rearrange("i (j m) -> i j m", m=128)).then_inc(SM["owb"].h, 16)
                        SM["owb"].bump(16)
        aes.close()
        kv_stack.close()
        phase_entry_waits()

    if "f2" in phases:
        src3 = xfmB if "attn" in phases else xfmI
        stk = ExitStack()
        ffn_phase("b", src3, None, out, "f1" not in phases, stk,
                  (SM["owb"].h, SM["owb"].c) if "attn" not in phases else None)
        stk.close()

    @em("sync")
    def _(e, w=SM["owb"].c):
        e.wait_ge(SM["owb"].h, w)

    with nc.Block() as blk:
        @blk.tensor
        def _(e):
            for f in prog["tensor"]:
                f(e)

        @blk.scalar
        def _(e):
            for f in prog["scalar"]:
                f(e)

        @blk.vector
        def _(e):
            for f in prog["vector"]:
                f(e)

        @blk.gpsimd
        def _(e):
            for f in prog["gpsimd"]:
                f(e)

        @blk.sync
        def _(e):
            for f in prog["sync"]:
                f(e)

    ES.close()
    return nc


def host_inputs(inp):
    wqkv = np.asarray(inp["L0_attn_w_qkv"], dtype=np.float32)
    wo = np.ascontiguousarray(np.asarray(inp["L0_attn_w_o"], dtype=np.float32))
    perm = np.concatenate([np.arange(0, 128, 2), np.arange(1, 128, 2)])
    wq = wqkv.copy()
    for h in range(NH):
        wq[:, h * 128:(h + 1) * 128] = wqkv[:, h * 128 + perm]
    for kh in range(NKV):
        b0 = D + kh * 128
        wq[:, b0:b0 + 128] = wqkv[:, b0 + perm]
    wq = np.ascontiguousarray(wq)
    gT = np.ascontiguousarray(np.asarray(inp["L0_ffn_gate_wq"], dtype=np.int32).T)
    uT = np.ascontiguousarray(np.asarray(inp["L0_ffn_up_wq"], dtype=np.int32).T)
    dT = np.ascontiguousarray(np.asarray(inp["L0_ffn_down_wq"], dtype=np.int32).T)
    gs = np.ascontiguousarray(np.asarray(inp["L0_ffn_gate_scale"], dtype=np.float32).reshape(NI, 128).T)
    us = np.ascontiguousarray(np.asarray(inp["L0_ffn_up_scale"], dtype=np.float32).reshape(NI, 128).T)
    ds = np.ascontiguousarray(np.asarray(inp["L0_ffn_down_scale"], dtype=np.float32).reshape(NCH, 128).T)
    idn = np.eye(128, dtype=ml_dtypes.bfloat16)
    idnf = np.eye(128, dtype=np.float32)
    j = np.arange(64, dtype=np.float64)
    invf = 1.0 / (10000.0 ** (2.0 * j / 128.0))
    t = np.arange(S, dtype=np.float64)
    ang = invf[:, None] * t[None, :]
    cosq = np.cos(ang).astype(np.float32)
    sinq = np.sin(ang).astype(np.float32)
    ii = np.arange(128)[:, None]
    jj = np.arange(T)[None, :]
    msk = np.concatenate([(ii <= jj - dd * 128) for dd in range(4)], axis=1).astype(ml_dtypes.bfloat16)
    return {"gT": gT, "uT": uT, "dT": dT, "wqkv": wq, "wo": wo,
            "gs": gs, "us": us, "ds": ds, "idn": idn, "idnf": idnf,
            "cosq": cosq, "sinq": sinq, "msk": msk}


def kernel(**inp):
    nc = build_nc()
    base = host_inputs(inp)
    xall = np.asarray(inp["L0_attn_x"], dtype=np.float32)
    in_maps = [{**base, "x": np.ascontiguousarray(xall[c])} for c in range(8)]
    res = run_bass_kernel_spmd(nc, in_maps, core_ids=list(range(8)))
    return np.stack([np.asarray(res.results[c]["out"], dtype=np.float32) for c in range(8)], axis=0)


# version 05
# speedup vs baseline: 1.0934x; 1.0297x over previous
import numpy as np
import ml_dtypes
import concourse.bass as bass
import concourse.mybir as mybir
from concourse.bass_utils import run_bass_kernel_spmd
from contextlib import ExitStack

F32 = mybir.dt.float32
BF16 = mybir.dt.bfloat16
I32 = mybir.dt.int32
AF = mybir.ActivationFunctionType
ALU = mybir.AluOpType
AX = mybir.AxisListType

S = 2048
D = 4096
II = 11008
NI = II // 128
NCH = D // 128
T = 512
NTT = S // T
NH = 32
NKV = 8
HD = 128
QKVO = D + 2 * NKV * HD
RSCL = float(1.0 / np.sqrt(128.0))
ENGS = ("tensor", "scalar", "vector", "gpsimd", "sync")

PHASES = ("f1", "qkv", "attn", "f2")


class Sem:
    __slots__ = ("h", "c")

    def __init__(self, es, nc, name):
        self.h = es.enter_context(nc.semaphore(name))
        self.c = 0

    def bump(self, n=1):
        self.c += n
        return self.c


def build_nc(phases=PHASES):
    nc = bass.Bass("TRN2", target_bir_lowering=False)
    x_in = nc.dram_tensor("x", [S, D], F32, kind="ExternalInput").ap()
    gT = nc.dram_tensor("gT", [D, II], I32, kind="ExternalInput").ap()
    uT = nc.dram_tensor("uT", [D, II], I32, kind="ExternalInput").ap()
    dT = nc.dram_tensor("dT", [II, D], I32, kind="ExternalInput").ap()
    wqkv = nc.dram_tensor("wqkv", [D, QKVO], F32, kind="ExternalInput").ap()
    wo_in = nc.dram_tensor("wo", [D, D], F32, kind="ExternalInput").ap()
    gs_in = nc.dram_tensor("gs", [128, NI], F32, kind="ExternalInput").ap()
    us_in = nc.dram_tensor("us", [128, NI], F32, kind="ExternalInput").ap()
    ds_in = nc.dram_tensor("ds", [128, NCH], F32, kind="ExternalInput").ap()
    idn_in = nc.dram_tensor("idn", [128, 128], BF16, kind="ExternalInput").ap()
    idnf_in = nc.dram_tensor("idnf", [128, 128], F32, kind="ExternalInput").ap()
    cos_in = nc.dram_tensor("cosq", [64, S], F32, kind="ExternalInput").ap()
    sin_in = nc.dram_tensor("sinq", [64, S], F32, kind="ExternalInput").ap()
    msk_in = nc.dram_tensor("msk", [128, 4 * T], BF16, kind="ExternalInput").ap()
    out = nc.dram_tensor("out", [S, D], F32, kind="ExternalOutput").ap()

    guB = nc.dram_tensor("guB", [NI, 2, 128, D], BF16, kind="Internal").ap()
    dB = nc.dram_tensor("dB", [NCH, 128, II], BF16, kind="Internal").ap()
    wqB = nc.dram_tensor("wqB", [D, QKVO], BF16, kind="Internal").ap()
    woB = nc.dram_tensor("woB", [D, D], BF16, kind="Internal").ap()
    xfmI = nc.dram_tensor("xfmI", [NCH, 128, S], F32, kind="Internal").ap()
    xfmA = nc.dram_tensor("xfmA", [NCH, 128, S], F32, kind="Internal").ap()
    xfmB = nc.dram_tensor("xfmB", [NCH, 128, S], F32, kind="Internal").ap()
    qscr = nc.dram_tensor("qscr", [NH, 128, S], BF16, kind="Internal").ap()

    ES = ExitStack()
    E = ES.enter_context

    idn = E(nc.sbuf_tensor("idn_s", [128, 128], BF16))
    idnf = E(nc.sbuf_tensor("idnf_s", [128, 128], F32))
    ones_cb = E(nc.sbuf_tensor("ones_cb", [128, 1], BF16))
    ones_rf = E(nc.sbuf_tensor("ones_rf", [1, 128], F32))
    gsus = E(nc.sbuf_tensor("gsus", [128, NI], F32))
    us_s = E(nc.sbuf_tensor("us_s", [128, NI], F32))
    dsch = E(nc.sbuf_tensor("dsch", [128, NCH], F32))
    negmbc = E(nc.sbuf_tensor("negmbc", [128, NH], F32))
    qn = E(nc.sbuf_tensor("qn", [1, NH], F32))
    kn = E(nc.sbuf_tensor("kn", [1, NKV], F32))
    knr = E(nc.sbuf_tensor("knr", [1, NH], F32))
    sm1 = E(nc.sbuf_tensor("sm1", [1, NH], F32))
    srow = [E(nc.sbuf_tensor(f"srow{i}", [1, T], F32)) for i in range(3)]
    a_bc = E(nc.sbuf_tensor("a_bc", [128, T], F32))
    b_bc = E(nc.sbuf_tensor("b_bc", [128, T], F32))
    cnv = [E(nc.sbuf_tensor(f"cnv{i}", [128, T], BF16)) for i in range(4)]
    lnt = E(nc.sbuf_tensor("lnt", [128, T], F32))
    xfs = [E(nc.sbuf_tensor(f"xf{i}", [128, T], F32)) for i in range(2)]
    sfs = [E(nc.sbuf_tensor(f"sf{i}", [128, T], F32)) for i in range(2)]
    ogs = [E(nc.sbuf_tensor(f"og{i}", [128, T], F32)) for i in range(2)]

    pp = [E(nc.psum_tensor(f"pp{i}", [128, 512], F32)) for i in range(5)]
    pS1 = E(nc.psum_tensor("pS1", [1, 512], F32))
    pS2 = E(nc.psum_tensor("pS2", [1, 512], F32))
    pT = E(nc.psum_tensor("pT", [128, 512], BF16))

    sem_names = ["cld", "wgl0", "wgl1", "wglg0", "wglg1", "wdlg0", "wdlg1", "wqlg0", "wqlg1",
                 "l1d", "uev", "gud", "wwb", "wdl0", "wdl1", "wdf",
                 "dev", "xfl0", "xfl1", "sfl0", "sfl1", "xnb", "owb", "nsq", "nmm", "nrd",
                 "wql0", "wql1", "wqf", "qev", "qwb0", "qwb1", "tvd", "vtd",
                 "qld", "exd", "mkd", "pvd", "aod", "rcd", "std", "ppd"]
    SM = {n: Sem(ES, nc, n) for n in sem_names}

    prog = {e: [] for e in ENGS}

    def em(eng):
        def deco(f):
            prog[eng].append(f)
        return deco

    def wv(sem, delta=0):
        v = sem.c + delta
        return (sem.h, v) if v >= 1 else None

    def wlist(*sems):
        return tuple(x for x in (wv(s) for s in sems) if x)

    def phase_entry_waits():
        caps = [(s.h, s.c) for s in SM.values() if s.c >= 1]
        for eng in ENGS:
            @em(eng)
            def _(e, caps=tuple(caps)):
                for h, v in caps:
                    e.wait_ge(h, v)

    @em("sync")
    def _(e):
        e.dma_start(out=idn[:], in_=idn_in).then_inc(SM["cld"].h, 16)
        e.dma_start(out=idnf[:], in_=idnf_in).then_inc(SM["cld"].h, 16)
        e.dma_start(out=gsus[:], in_=gs_in).then_inc(SM["cld"].h, 16)
        e.dma_start(out=us_s[:], in_=us_in).then_inc(SM["cld"].h, 16)
        e.dma_start(out=dsch[:], in_=ds_in).then_inc(SM["cld"].h, 16)
    w_cld0 = SM["cld"].bump(5 * 16)

    @em("vector")
    def _(e, w=w_cld0):
        e.memset(ones_cb[:], 1.0)
        e.memset(ones_rf[:], 1.0)
        e.memset(qn[:], 0.0)
        e.memset(kn[:], 0.0)
        e.wait_ge(SM["cld"].h, w)
        e.tensor_mul(gsus[:], gsus[:], us_s[:])
        e.tensor_scalar_mul(dsch[:], dsch[:], 0.5).then_inc(SM["std"].h, 1)
    SM["std"].bump(1)
    w_consts = SM["std"].c

    def emit_prepass():
        xv = x_in.rearrange("(q j i) (c m) -> i j q c m", i=128, j=4, m=128)
        k = 0
        for c in range(NCH):
            for q in range(NTT):
                slot = k % 2

                @em("sync")
                def _(e, c=c, q=q, slot=slot, wfree=wv(SM["nmm"], -1)):
                    if wfree:
                        e.wait_ge(*wfree)
                    e.dma_start(out=xfs[slot].ap().rearrange("i (j m) -> i j m", m=128),
                                in_=xv[:, :, q, c, :]).then_inc(SM[f"xfl{slot}"].h, 16)
                w_ld = (SM[f"xfl{slot}"].h, SM[f"xfl{slot}"].bump(16))

                @em("tensor")
                def _(e, slot=slot, w=w_ld, wst=w_consts, wev=wv(SM["nsq"], -1)):
                    e.wait_ge(*w)
                    e.wait_ge(SM["std"].h, wst)
                    if wev:
                        e.wait_ge(*wev)
                    last = None
                    for j in range(4):
                        last = e.transpose(pp[slot][:, j * 128:(j + 1) * 128],
                                           xfs[slot][:, j * 128:(j + 1) * 128], idnf[:])
                    last.then_inc(SM["nmm"].h, 1)
                w_tr = SM["nmm"].bump(1)

                @em("scalar")
                def _(e, slot=slot, w=w_tr, wwb=wv(SM["owb"], 0)):
                    e.wait_ge(SM["nmm"].h, w)
                    if wwb:
                        e.wait_ge(*wwb)
                    e.copy(ogs[slot][:], pp[slot][:]).then_inc(SM["nsq"].h, 1)
                w_ev = SM["nsq"].bump(1)

                @em("sync")
                def _(e, c=c, q=q, slot=slot, w=w_ev):
                    e.wait_ge(SM["nsq"].h, w)
                    e.dma_start(out=xfmI[c][:, q * T:(q + 1) * T], in_=ogs[slot][:]).then_inc(SM["owb"].h, 16)
                SM["owb"].bump(16)
                k += 1
        return SM["owb"].c

    def ln_tile(src_fm3, tt, xn_guard_sem, tens_guards, src_wait, xn):
        t0 = tt * T
        for c in range(NCH):
            @em("scalar")
            def _(e, c=c, t0=t0, wfree=wv(SM["nmm"], -1), sw=src_wait if c == 0 else None):
                if sw:
                    e.wait_ge(*sw)
                if wfree:
                    e.wait_ge(*wfree)
                e.dma_start(out=sfs[c % 2][:], in_=src_fm3[c][:, t0:t0 + T]).then_inc(SM[f"sfl{c % 2}"].h, 16)
            w_ld = (SM[f"sfl{c % 2}"].h, SM[f"sfl{c % 2}"].bump(16))

            @em("scalar")
            def _(e, c=c, w=w_ld, wmm=wv(SM["nmm"], -1)):
                e.wait_ge(*w)
                if wmm:
                    e.wait_ge(*wmm)
                e.copy(cnv[2 * (c % 2)][:], sfs[c % 2][:])
                e.activation(cnv[2 * (c % 2) + 1][:], sfs[c % 2][:], AF.Square).then_inc(SM["nsq"].h, 1)
            w_cv = SM["nsq"].bump(1)

            @em("tensor")
            def _(e, c=c, w=w_cv, tg=tens_guards if c == 0 else ()):
                e.wait_ge(SM["nsq"].h, w)
                for hvv in tg:
                    e.wait_ge(*hvv)
                e.matmul(pS1[:], ones_cb[:], cnv[2 * (c % 2)][:], start=(c == 0), stop=(c == NCH - 1))
                e.matmul(pS2[:], ones_cb[:], cnv[2 * (c % 2) + 1][:], start=(c == 0), stop=(c == NCH - 1)).then_inc(SM["nmm"].h, 1)
            SM["nmm"].bump(1)
        w_stat = SM["nmm"].c

        @em("vector")
        def _(e, w=w_stat):
            e.wait_ge(SM["nmm"].h, w)
            m, r, t2 = srow
            e.tensor_scalar_mul(m[:], pS1[0:1, 0:T], 1.0 / D)
            e.tensor_scalar_mul(r[:], pS2[0:1, 0:T], 1.0 / D)
            e.drain()
            e.tensor_tensor(t2[:], m[:], m[:], ALU.mult)
            e.drain()
            e.tensor_sub(r[:], r[:], t2[:])
            e.drain()
            e.tensor_scalar_add(r[:], r[:], 1e-5)
            e.drain()
            e.tensor_copy(t2[:], r[:])
            e.drain()
            e.reciprocal(r[:], r[:]).then_inc(SM["nsq"].h, 1)
        w1 = SM["nsq"].bump(1)

        @em("scalar")
        def _(e, w=w1):
            e.wait_ge(SM["nsq"].h, w)
            e.activation(srow[1][:], srow[1][:], AF.Sqrt).then_inc(SM["nsq"].h, 1)
        w2 = SM["nsq"].bump(1)

        @em("vector")
        def _(e, w=w2):
            e.wait_ge(SM["nsq"].h, w)
            m, r, t2 = srow
            e.tensor_tensor(lnt[0:1, :], r[:], r[:], ALU.mult)
            e.drain()
            e.tensor_tensor(lnt[0:1, :], lnt[0:1, :], t2[:], ALU.mult)
            e.drain()
            e.tensor_scalar(lnt[0:1, :], lnt[0:1, :], -0.5, 1.5, ALU.mult, ALU.add)
            e.drain()
            e.tensor_tensor(r[:], r[:], lnt[0:1, :], ALU.mult)
            e.drain()
            e.scalar_tensor_tensor(m[:], m[:], -1.0, r[:], ALU.mult, ALU.mult).then_inc(SM["nsq"].h, 1)
        w3 = SM["nsq"].bump(1)

        @em("tensor")
        def _(e, w=w3, wab=wv(SM["xnb"], 0), wpo=wv(SM["ppd"], 0)):
            e.wait_ge(SM["nsq"].h, w)
            if wab:
                e.wait_ge(*wab)
            if wpo:
                e.wait_ge(*wpo)
            e.matmul(pp[0][:, 0:T], ones_rf[:], srow[1][:], start=True, stop=True)
            e.matmul(pp[1][:, 0:T], ones_rf[:], srow[0][:], start=True, stop=True).then_inc(SM["nmm"].h, 1)
        w4 = SM["nmm"].bump(1)

        @em("scalar")
        def _(e, w=w4):
            e.wait_ge(SM["nmm"].h, w)
            e.copy(a_bc[:], pp[0][:, 0:T])
            e.copy(b_bc[:], pp[1][:, 0:T]).then_inc(SM["nsq"].h, 1)
        w_abc = SM["nsq"].bump(1)

        xn_w = (xn_guard_sem.h, xn_guard_sem.c) if xn_guard_sem.c >= 1 else None
        for c2 in range(NCH // 2):
            for c in (2 * c2, 2 * c2 + 1):
                @em("scalar")
                def _(e, c=c, t0=t0, wfree=wv(SM["xnb"], -1 if c % 2 == 0 else 0), wab=w_abc,
                      wdev=wlist(SM["dev"], SM["qev"], SM["nmm"]) if c < 2 else ()):
                    e.wait_ge(SM["nsq"].h, wab)
                    for hvv in wdev:
                        e.wait_ge(*hvv)
                    if wfree:
                        e.wait_ge(*wfree)
                    e.dma_start(out=xfs[c % 2][:], in_=src_fm3[c][:, t0:t0 + T]).then_inc(SM[f"xfl{c % 2}"].h, 16)
                SM[f"xfl{c % 2}"].bump(16)
            wld = ((SM["xfl0"].h, SM["xfl0"].c), (SM["xfl1"].h, SM["xfl1"].c))

            @em("vector")
            def _(e, c2=c2, wld=wld, wab=w_abc, xw=xn_w if c2 == 0 else None):
                e.wait_ge(*wld[0])
                e.wait_ge(*wld[1])
                e.wait_ge(SM["nsq"].h, wab)
                if xw:
                    e.wait_ge(*xw)
                ca, cb = 2 * c2, 2 * c2 + 1
                e.tensor_tensor(lnt[:], xfs[0][:], a_bc[:], ALU.mult)
                e.tensor_tensor(ogs[0][:], xfs[1][:], a_bc[:], ALU.mult)
                e.drain()
                e.tensor_add(xn[:, ca * T:(ca + 1) * T], lnt[:], b_bc[:]).then_inc(SM["xnb"].h, 1)
                e.tensor_add(xn[:, cb * T:(cb + 1) * T], ogs[0][:], b_bc[:]).then_inc(SM["xnb"].h, 1)
            SM["xnb"].bump(2)
        return SM["xnb"].c

    def ffn_phase(tag, src_fm3, dst_fm3, dst_tm, first, stk, src_wait):
        xn = stk.enter_context(nc.sbuf_tensor(f"xn{tag}", [128, NCH * T], BF16))
        gu = stk.enter_context(nc.sbuf_tensor(f"gu_{tag}", [128, NI * T], BF16))
        wgu = [stk.enter_context(nc.sbuf_tensor(f"wgu{tag}{i}", [128, 2 * D], BF16)) for i in range(2)]
        wd = [stk.enter_context(nc.sbuf_tensor(f"wd{tag}{i}", [128, II // 2], BF16)) for i in range(2)]
        usb = [stk.enter_context(nc.sbuf_tensor(f"us{tag}{i}", [128, T], BF16)) for i in range(2)]
        otm = stk.enter_context(nc.sbuf_tensor(f"otm{tag}", [128, T], F32)) if dst_tm is not None else None
        gv = gT.rearrange("(c p) (t m) -> p c t m", p=128, m=128)
        uv = uT.rearrange("(c p) (t m) -> p c t m", p=128, m=128)
        dv = dT.rearrange("(j p) (h m) -> p j h m", p=128, m=128)
        if dst_tm is not None:
            ov = dst_tm.rearrange("(q j i) (c m) -> q i j c m", i=128, j=4, m=128)

        for tt in range(NTT):
            t0 = tt * T
            conv = first and tt == 0
            xn_tgt = ln_tile(src_fm3, tt, SM["l1d"],
                             wlist(SM["nrd"], SM["rcd"], SM["gud"], SM["uev"], SM["qev"], SM["exd"], SM["dev"]),
                             src_wait if tt == 0 else None, xn)

            for tau in range(NI):
                slot = tau % 2
                if conv:
                    @em("gpsimd")
                    def _(e, tau=tau, slot=slot, wfree=wv(SM["l1d"], -1), wwb=wv(SM["wwb"], 0)):
                        if wfree:
                            e.wait_ge(*wfree)
                        if wwb:
                            e.wait_ge(*wwb)
                        e.dma_start(out=wgu[slot].ap().rearrange("p (c m) -> p c m", m=128)[:, 0:NCH, :],
                                    in_=gv[:, :, tau, :]).then_inc(SM[f"wglg{slot}"].h, 16)
                        e.dma_start(out=wgu[slot].ap().rearrange("p (c m) -> p c m", m=128)[:, NCH:2 * NCH, :],
                                    in_=uv[:, :, tau, :]).then_inc(SM[f"wglg{slot}"].h, 16)
                    wld = (SM[f"wglg{slot}"].h, SM[f"wglg{slot}"].bump(32))

                    @em("sync")
                    def _(e, tau=tau, slot=slot, wld=wld):
                        e.wait_ge(*wld)
                        e.dma_start(out=guB[tau].rearrange("t p f -> p t f"),
                                    in_=wgu[slot].ap().rearrange("p (t f) -> p t f", f=D)).then_inc(SM["wwb"].h, 16)
                    SM["wwb"].bump(16)
                else:
                    @em("sync")
                    def _(e, tau=tau, slot=slot, wfree=wv(SM["l1d"], -1),
                          wwb=wv(SM["wwb"], 0) if (tau == 0 and first and tt == 1) else None):
                        if wwb:
                            e.wait_ge(*wwb)
                        if wfree:
                            e.wait_ge(*wfree)
                        e.dma_start(out=wgu[slot].ap().rearrange("p (t f) -> p t f", f=D),
                                    in_=guB[tau].rearrange("t p f -> p t f")).then_inc(SM[f"wgl{slot}"].h, 16)
                    wld = (SM[f"wgl{slot}"].h, SM[f"wgl{slot}"].bump(16))

                @em("tensor")
                def _(e, tau=tau, slot=slot, wld=wld, xw=xn_tgt,
                      wgud=wv(SM["gud"], -1), wue=wv(SM["uev"], -1),
                      wpo=wv(SM["ppd"], 0) if tau < 2 else None):
                    e.wait_ge(*wld)
                    if wpo:
                        e.wait_ge(*wpo)
                    if tau == 0:
                        e.wait_ge(SM["xnb"].h, xw)
                    if wgud:
                        e.wait_ge(*wgud)
                    if wue:
                        e.wait_ge(*wue)
                    for c in range(NCH):
                        e.matmul(pp[slot][:], wgu[slot][:, c * 128:(c + 1) * 128], xn[:, c * T:(c + 1) * T],
                                 start=(c == 0), stop=(c == NCH - 1))
                    for c in range(NCH):
                        ins = e.matmul(pp[2 + slot][:], wgu[slot][:, D + c * 128:D + (c + 1) * 128], xn[:, c * T:(c + 1) * T],
                                       start=(c == 0), stop=(c == NCH - 1))
                    ins.then_inc(SM["l1d"].h, 1)
                SM["l1d"].bump(1)

                @em("scalar")
                def _(e, slot=slot, w=SM["l1d"].c):
                    e.wait_ge(SM["l1d"].h, w)
                    e.copy(usb[slot][:], pp[2 + slot][:]).then_inc(SM["uev"].h, 1)
                SM["uev"].bump(1)

                @em("vector")
                def _(e, tau=tau, slot=slot, w=SM["uev"].c):
                    e.wait_ge(SM["uev"].h, w)
                    e.scalar_tensor_tensor(gu[:, tau * T:(tau + 1) * T], pp[slot][:], gsus[:, tau:tau + 1],
                                           usb[slot][:], ALU.mult, ALU.mult).then_inc(SM["gud"].h, 1)
                SM["gud"].bump(1)
            l1_all = SM["gud"].c

            for ht in range(NCH):
                for half in range(2):
                    hslot = (2 * ht + half) % 2
                    if conv:
                        @em("gpsimd")
                        def _(e, ht=ht, half=half, hslot=hslot, wfree=wv(SM["wdf"], -1), wwb=wv(SM["wwb"], 0)):
                            if wfree:
                                e.wait_ge(*wfree)
                            if wwb:
                                e.wait_ge(*wwb)
                            e.dma_start(out=wd[hslot].ap().rearrange("p (j m) -> p j m", m=128),
                                        in_=dv[:, half * 43:(half + 1) * 43, ht, :]).then_inc(SM[f"wdlg{hslot}"].h, 16)
                        wldd = (SM[f"wdlg{hslot}"].h, SM[f"wdlg{hslot}"].bump(16))

                        @em("sync")
                        def _(e, ht=ht, half=half, hslot=hslot, wldd=wldd):
                            e.wait_ge(*wldd)
                            e.dma_start(out=dB[ht].rearrange("p (hh x) -> p hh x", hh=2)[:, half, :],
                                        in_=wd[hslot][:]).then_inc(SM["wwb"].h, 16)
                        SM["wwb"].bump(16)
                    else:
                        @em("sync")
                        def _(e, ht=ht, half=half, hslot=hslot, wfree=wv(SM["wdf"], -1),
                              wwb=wv(SM["wwb"], 0) if (ht == 0 and half == 0 and first and tt == 1) else None):
                            if wwb:
                                e.wait_ge(*wwb)
                            if wfree:
                                e.wait_ge(*wfree)
                            e.dma_start(out=wd[hslot][:],
                                        in_=dB[ht].rearrange("p (hh x) -> p hh x", hh=2)[:, half, :]).then_inc(SM[f"wdl{hslot}"].h, 16)
                        wldd = (SM[f"wdl{hslot}"].h, SM[f"wdl{hslot}"].bump(16))

                    @em("tensor")
                    def _(e, ht=ht, half=half, hslot=hslot, wldd=wldd, wall=l1_all,
                          wde=wv(SM["dev"], 0) if half == 0 else None):
                        e.wait_ge(*wldd)
                        if ht == 0 and half == 0:
                            e.wait_ge(SM["gud"].h, wall)
                        if wde:
                            e.wait_ge(*wde)
                        last = None
                        for jj in range(43):
                            j = half * 43 + jj
                            last = e.matmul(pp[4][:], wd[hslot][:, jj * 128:(jj + 1) * 128],
                                            gu[:, j * T:(j + 1) * T], start=(j == 0), stop=(j == NI - 1))
                        last.then_inc(SM["wdf"].h, 1)
                    SM["wdf"].bump(1)
                w_dmm = SM["wdf"].c

                @em("scalar")
                def _(e, ht=ht, t0=t0, wfree=wv(SM["dev"], -1),
                      wxn=wv(SM["xnb"], 0) if ht == 0 else None):
                    if wfree:
                        e.wait_ge(*wfree)
                    if wxn:
                        e.wait_ge(*wxn)
                    e.dma_start(out=xfs[ht % 2][:], in_=src_fm3[ht][:, t0:t0 + T]).then_inc(SM[f"xfl{ht % 2}"].h, 16)
                wxl = (SM[f"xfl{ht % 2}"].h, SM[f"xfl{ht % 2}"].bump(16))

                @em("vector")
                def _(e, ht=ht, wmm=w_dmm, wxl=wxl, wowb=wv(SM["owb"], 0)):
                    e.wait_ge(SM["wdf"].h, wmm)
                    e.wait_ge(*wxl)
                    if wowb:
                        e.wait_ge(*wowb)
                    e.scalar_tensor_tensor(ogs[ht % 2][:], pp[4][:], dsch[:, ht:ht + 1],
                                           xfs[ht % 2][:], ALU.mult, ALU.add).then_inc(SM["dev"].h, 1)
                SM["dev"].bump(1)
                w_ev = SM["dev"].c

                if dst_fm3 is not None:
                    @em("sync")
                    def _(e, ht=ht, t0=t0, w=w_ev):
                        e.wait_ge(SM["dev"].h, w)
                        e.dma_start(out=dst_fm3[ht][:, t0:t0 + T], in_=ogs[ht % 2][:]).then_inc(SM["owb"].h, 16)
                    SM["owb"].bump(16)
                else:
                    @em("tensor")
                    def _(e, ht=ht, w=w_ev, wpo=wv(SM["ppd"], 0)):
                        e.wait_ge(SM["dev"].h, w)
                        if wpo:
                            e.wait_ge(*wpo)
                        last = None
                        for j in range(4):
                            last = e.transpose(pp[ht % 2][:, j * 128:(j + 1) * 128],
                                               ogs[ht % 2][:, j * 128:(j + 1) * 128], idnf[:])
                        last.then_inc(SM["nmm"].h, 1)
                    w_tr = SM["nmm"].bump(1)

                    @em("scalar")
                    def _(e, ht=ht, w=w_tr, wwb=wv(SM["owb"], 0)):
                        e.wait_ge(SM["nmm"].h, w)
                        if wwb:
                            e.wait_ge(*wwb)
                        e.copy(otm[:], pp[ht % 2][:]).then_inc(SM["ppd"].h, 1)
                    w_ot = SM["ppd"].bump(1)

                    @em("sync")
                    def _(e, ht=ht, tt=tt, w=w_ot):
                        e.wait_ge(SM["ppd"].h, w)
                        e.dma_start(out=ov[tt, :, :, ht, :],
                                    in_=otm.ap().rearrange("i (j m) -> i j m", m=128)).then_inc(SM["owb"].h, 16)
                    SM["owb"].bump(16)

    w_pre = emit_prepass()
    if "f1" in phases:
        stk = ExitStack()
        last_f1 = len(phases) == 1
        ffn_phase("a", xfmI, None if last_f1 else xfmA, out if last_f1 else None,
                  True, stk, (SM["owb"].h, w_pre))
        stk.close()
        phase_entry_waits()
        src_after_f1 = xfmA
    else:
        src_after_f1 = xfmI

    kv_stack = ExitStack()
    if "qkv" in phases:
        src_fm3 = src_after_f1
        kfm = kv_stack.enter_context(nc.sbuf_tensor("kfm", [128, NKV * S], BF16))
        vtm = kv_stack.enter_context(nc.sbuf_tensor("vtm", [128, (S // 128) * NKV * HD], BF16))
        qes = ExitStack()
        xn = qes.enter_context(nc.sbuf_tensor("xnq", [128, NCH * T], BF16))
        cosq = qes.enter_context(nc.sbuf_tensor("cosq_s", [64, S], F32))
        sinq = qes.enter_context(nc.sbuf_tensor("sinq_s", [64, S], F32))
        wqh = [qes.enter_context(nc.sbuf_tensor(f"wqh{i}", [128, 16 * 640], BF16)) for i in range(2)]
        qsb = [qes.enter_context(nc.sbuf_tensor(f"qsb{i}", [128, T], BF16)) for i in range(2)]
        rsc = [qes.enter_context(nc.sbuf_tensor(f"rsc{i}", [64, T], F32)) for i in range(4)]
        vst = [qes.enter_context(nc.sbuf_tensor(f"vst{i}", [128, T], BF16)) for i in range(2)]
        sqsc = qes.enter_context(nc.sbuf_tensor("sqsc", [128, T], BF16))
        nrm = qes.enter_context(nc.sbuf_tensor("nrm", [1, 1], F32))

        @em("sync")
        def _(e):
            e.dma_start(out=cosq[:], in_=cos_in).then_inc(SM["cld"].h, 16)
            e.dma_start(out=sinq[:], in_=sin_in).then_inc(SM["cld"].h, 16)
        w_rope = SM["cld"].bump(32)

        wqv4 = wqkv.rearrange("(cc ci p) col -> p cc ci col", p=128, ci=16)
        wqB4 = wqB.rearrange("(cc ci p) col -> p cc ci col", p=128, ci=16)
        qhalf_done = [0, 0]
        nqu = 0
        q_sq = []
        for tt in range(NTT):
            t0 = tt * T
            conv = tt == 0
            xn_tgt = ln_tile(src_fm3, tt, SM["wqf"],
                             wlist(SM["nrd"], SM["rcd"], SM["qev"], SM["exd"]),
                             (SM["owb"].h, SM["owb"].c) if (tt == 0 and "f1" not in phases) else None, xn)

            for b in range(10):
                nu = 5 if b < 9 else 3
                wq_base = SM["wqf"].c
                half_waits = []
                for hh in range(2):
                    par = (2 * b + hh) % 2
                    if conv:
                        @em("gpsimd")
                        def _(e, b=b, hh=hh, par=par, nu=nu, wfree=qhalf_done[par], wwb=wv(SM["wwb"], 0)):
                            if wfree >= 1:
                                e.wait_ge(SM["wqf"].h, wfree)
                            if wwb:
                                e.wait_ge(*wwb)
                            nc_ = 128 * nu
                            e.dma_start(out=wqh[par].ap().rearrange("p (ci col) -> p ci col", col=640)[:, :, 0:nc_],
                                        in_=wqv4[:, hh, :, b * 640:b * 640 + nc_]).then_inc(SM[f"wqlg{par}"].h, 16)
                        wldq = (SM[f"wqlg{par}"].h, SM[f"wqlg{par}"].bump(16))

                        @em("sync")
                        def _(e, b=b, hh=hh, par=par, nu=nu, wldq=wldq):
                            e.wait_ge(*wldq)
                            nc_ = 128 * nu
                            e.dma_start(out=wqB4[:, hh, :, b * 640:b * 640 + nc_],
                                        in_=wqh[par].ap().rearrange("p (ci col) -> p ci col", col=640)[:, :, 0:nc_]).then_inc(SM["wwb"].h, 16)
                        SM["wwb"].bump(16)
                    else:
                        @em("sync")
                        def _(e, b=b, hh=hh, par=par, nu=nu, wfree=qhalf_done[par],
                              wwb=wv(SM["wwb"], 0) if (b == 0 and hh == 0 and tt == 1) else None):
                            if wwb:
                                e.wait_ge(*wwb)
                            if wfree >= 1:
                                e.wait_ge(SM["wqf"].h, wfree)
                            nc_ = 128 * nu
                            e.dma_start(out=wqh[par].ap().rearrange("p (ci col) -> p ci col", col=640)[:, :, 0:nc_],
                                        in_=wqB4[:, hh, :, b * 640:b * 640 + nc_]).then_inc(SM[f"wql{par}"].h, 16)
                        wldq = (SM[f"wql{par}"].h, SM[f"wql{par}"].bump(16))
                    half_waits.append(wldq)
                    qhalf_done[par] = wq_base + 16 * (hh + 1)

                for c in range(NCH):
                    par = (2 * b + c // 16) % 2
                    @em("tensor")
                    def _(e, b=b, c=c, nu=nu, par=par,
                          wldq=half_waits[c // 16] if c % 16 == 0 else None,
                          xw=xn_tgt, wev=wv(SM["qev"], 0) if c == 0 else None):
                        if wldq:
                            e.wait_ge(*wldq)
                        if c == 0:
                            e.wait_ge(SM["xnb"].h, xw)
                            if wev:
                                e.wait_ge(*wev)
                        last = None
                        for u in range(nu):
                            last = e.matmul(pp[u][:], wqh[par][:, (c % 16) * 640 + u * 128:(c % 16) * 640 + (u + 1) * 128],
                                            xn[:, c * T:(c + 1) * T], start=(c == 0), stop=(c == NCH - 1))
                        last.then_inc(SM["wqf"].h, 1)
                    SM["wqf"].bump(1)
                w_batch = SM["wqf"].c

                for u in range(nu):
                    ot = b * 5 + u
                    if ot < NH:
                        slot = nqu % 2

                        @em("vector")
                        def _(e, u=u, slot=slot, t0=t0, wmm=w_batch, wrope=w_rope,
                              wqwb=wv(SM[f"qwb{nqu % 2}"], 0),
                              wsq=(q_sq[nqu - 2] if nqu >= 2 else 0)):
                            e.wait_ge(SM["wqf"].h, wmm)
                            e.wait_ge(SM["cld"].h, wrope)
                            if wqwb:
                                e.wait_ge(*wqwb)
                            if wsq >= 1:
                                e.wait_ge(SM["nsq"].h, wsq)
                            psrc = pp[u]
                            cs = cosq[:, t0:t0 + T]
                            sn = sinq[:, t0:t0 + T]
                            pe = psrc[0:64, :]
                            po = psrc[64:128, :]
                            e.tensor_tensor(rsc[0][:], pe, cs, ALU.mult)
                            e.tensor_tensor(rsc[1][:], po, sn, ALU.mult)
                            e.tensor_tensor(rsc[2][:], pe, sn, ALU.mult)
                            e.tensor_tensor(rsc[3][:], po, cs, ALU.mult)
                            e.drain()
                            e.tensor_sub(qsb[slot][0:64, :], rsc[0][:], rsc[1][:])
                            e.tensor_add(qsb[slot][64:128, :], rsc[2][:], rsc[3][:]).then_inc(SM["qev"].h, 1)
                        w_rp = SM["qev"].bump(1)

                        @em("sync")
                        def _(e, ot=ot, slot=slot, t0=t0, w=w_rp):
                            e.wait_ge(SM["qev"].h, w)
                            e.dma_start(out=qscr[ot][:, t0:t0 + T], in_=qsb[slot][:]).then_inc(SM[f"qwb{slot}"].h, 16)
                        SM[f"qwb{slot}"].bump(16)

                        @em("scalar")
                        def _(e, slot=slot, w=w_rp, wn=wv(SM["nmm"], 0)):
                            e.wait_ge(SM["qev"].h, w)
                            if wn:
                                e.wait_ge(*wn)
                            e.activation(sqsc[:], qsb[slot][:], AF.Square).then_inc(SM["nsq"].h, 1)
                        w_sq = SM["nsq"].bump(1)
                        q_sq.append(w_sq)
                        nqu += 1

                        @em("tensor")
                        def _(e, w=w_sq, wr=wv(SM["nrd"], 0)):
                            e.wait_ge(SM["nsq"].h, w)
                            if wr:
                                e.wait_ge(*wr)
                            e.matmul(pS1[:], ones_cb[:], sqsc[:], start=True, stop=True).then_inc(SM["nmm"].h, 1)
                        w_nm = SM["nmm"].bump(1)

                        @em("vector")
                        def _(e, ot=ot, w=w_nm):
                            e.wait_ge(SM["nmm"].h, w)
                            e.reduce_max(nrm[:], pS1[0:1, 0:T], axis=AX.X)
                            e.drain()
                            e.tensor_max(qn[0:1, ot:ot + 1], qn[0:1, ot:ot + 1], nrm[:]).then_inc(SM["nrd"].h, 1)
                        SM["nrd"].bump(1)
                    elif ot < NH + NKV:
                        kh = ot - NH

                        @em("vector")
                        def _(e, kh=kh, u=u, t0=t0, wmm=w_batch, wrope=w_rope):
                            e.wait_ge(SM["wqf"].h, wmm)
                            e.wait_ge(SM["cld"].h, wrope)
                            psrc = pp[u]
                            cs = cosq[:, t0:t0 + T]
                            sn = sinq[:, t0:t0 + T]
                            pe = psrc[0:64, :]
                            po = psrc[64:128, :]
                            e.tensor_tensor(rsc[0][:], pe, cs, ALU.mult)
                            e.tensor_tensor(rsc[1][:], po, sn, ALU.mult)
                            e.tensor_tensor(rsc[2][:], pe, sn, ALU.mult)
                            e.tensor_tensor(rsc[3][:], po, cs, ALU.mult)
                            e.drain()
                            e.tensor_sub(kfm[0:64, kh * S + t0: kh * S + t0 + T], rsc[0][:], rsc[1][:])
                            e.tensor_add(kfm[64:128, kh * S + t0: kh * S + t0 + T], rsc[2][:], rsc[3][:]).then_inc(SM["qev"].h, 1)
                        w_rp = SM["qev"].bump(1)

                        @em("scalar")
                        def _(e, kh=kh, t0=t0, w=w_rp, wn=wv(SM["nmm"], 0)):
                            e.wait_ge(SM["qev"].h, w)
                            if wn:
                                e.wait_ge(*wn)
                            e.activation(sqsc[:], kfm[:, kh * S + t0: kh * S + t0 + T], AF.Square).then_inc(SM["nsq"].h, 1)
                        w_sq = SM["nsq"].bump(1)

                        @em("tensor")
                        def _(e, w=w_sq, wr=wv(SM["nrd"], 0)):
                            e.wait_ge(SM["nsq"].h, w)
                            if wr:
                                e.wait_ge(*wr)
                            e.matmul(pS1[:], ones_cb[:], sqsc[:], start=True, stop=True).then_inc(SM["nmm"].h, 1)
                        w_nm = SM["nmm"].bump(1)

                        @em("vector")
                        def _(e, kh=kh, w=w_nm):
                            e.wait_ge(SM["nmm"].h, w)
                            e.reduce_max(nrm[:], pS1[0:1, 0:T], axis=AX.X)
                            e.drain()
                            e.tensor_max(kn[0:1, kh:kh + 1], kn[0:1, kh:kh + 1], nrm[:]).then_inc(SM["nrd"].h, 1)
                        SM["nrd"].bump(1)
                    else:
                        vh = ot - NH - NKV
                        vslot = vh % 2

                        @em("scalar")
                        def _(e, u=u, vslot=vslot, wmm=w_batch, wtv=wv(SM["tvd"], -1)):
                            e.wait_ge(SM["wqf"].h, wmm)
                            if wtv:
                                e.wait_ge(*wtv)
                            e.copy(vst[vslot][:], pp[u][:]).then_inc(SM["qev"].h, 1)
                        w_vst = SM["qev"].bump(1)

                        @em("tensor")
                        def _(e, vslot=vslot, w=w_vst, wvt=wv(SM["vtd"], 0)):
                            e.wait_ge(SM["qev"].h, w)
                            if wvt:
                                e.wait_ge(*wvt)
                            last = None
                            for j in range(4):
                                last = e.transpose(pT[:, j * 128:(j + 1) * 128], vst[vslot][:, j * 128:(j + 1) * 128], idn[:])
                            last.then_inc(SM["tvd"].h, 1)
                        w_tv = SM["tvd"].bump(1)

                        @em("scalar")
                        def _(e, vh=vh, t0=t0, w=w_tv):
                            e.wait_ge(SM["tvd"].h, w)
                            ins = None
                            for j in range(4):
                                base = (t0 // 128 + j) * (NKV * HD) + vh * HD
                                ins = e.copy(vtm[:, base: base + HD], pT[:, j * 128:(j + 1) * 128])
                            ins.then_inc(SM["vtd"].h, 1)
                        SM["vtd"].bump(1)

        @em("vector")
        def _(e, w=SM["nrd"].c):
            e.wait_ge(SM["nrd"].h, w)
            for k in range(NKV):
                for r in range(4):
                    e.tensor_copy(knr[0:1, 4 * k + r:4 * k + r + 1], kn[0:1, k:k + 1])
            e.drain()
            e.tensor_tensor(sm1[:], qn[:], knr[:], ALU.mult).then_inc(SM["nsq"].h, 1)
        w1 = SM["nsq"].bump(1)

        @em("scalar")
        def _(e, w=w1):
            e.wait_ge(SM["nsq"].h, w)
            e.activation(sm1[:], sm1[:], AF.Sqrt).then_inc(SM["nsq"].h, 1)
        w2 = SM["nsq"].bump(1)

        @em("vector")
        def _(e, w=w2):
            e.wait_ge(SM["nsq"].h, w)
            e.tensor_scalar_mul(sm1[:], sm1[:], -RSCL).then_inc(SM["nsq"].h, 1)
        w3 = SM["nsq"].bump(1)

        @em("tensor")
        def _(e, w=w3):
            e.wait_ge(SM["nsq"].h, w)
            e.matmul(pp[0][:, 0:NH], ones_rf[:], sm1[:], start=True, stop=True).then_inc(SM["nmm"].h, 1)
        w4 = SM["nmm"].bump(1)

        @em("scalar")
        def _(e, w=w4):
            e.wait_ge(SM["nmm"].h, w)
            e.copy(negmbc[:], pp[0][:, 0:NH]).then_inc(SM["nsq"].h, 1)
        SM["nsq"].bump(1)
        qes.close()
        phase_entry_waits()

    if "attn" in phases:
        src_fm3 = src_after_f1
        dst_is_out = "f2" not in phases
        aes = ExitStack()
        qt = aes.enter_context(nc.sbuf_tensor("qt", [128, NH * T], BF16))
        ao = aes.enter_context(nc.sbuf_tensor("ao", [128, NH * T], BF16))
        msk = aes.enter_context(nc.sbuf_tensor("msk_s", [128, 4 * T], BF16))
        woh = [aes.enter_context(nc.sbuf_tensor(f"woh{i}", [128, 16 * 640], BF16)) for i in range(2)]
        pex = [aes.enter_context(nc.sbuf_tensor(f"pex{i}", [128, T], BF16)) for i in range(2)]
        rec = srow[0]
        rbc = a_bc
        otm2 = aes.enter_context(nc.sbuf_tensor("otm2", [128, T], F32)) if dst_is_out else None
        if dst_is_out:
            ov2 = out.rearrange("(q j i) (c m) -> q i j c m", i=128, j=4, m=128)

        @em("sync")
        def _(e):
            e.dma_start(out=msk[:], in_=msk_in).then_inc(SM["cld"].h, 16)
        w_msk = SM["cld"].bump(16)
        wov4 = wo_in.rearrange("(cc ci p) col -> p cc ci col", p=128, ci=16)
        woB4 = woB.rearrange("(cc ci p) col -> p cc ci col", p=128, ci=16)
        whalf_done = [0, 0]

        for tq in range(NTT):
            t0 = tq * T
            conv = tq == 0

            @em("sync")
            def _(e, t0=t0, waod=wv(SM["aod"], 0)):
                if waod:
                    e.wait_ge(*waod)
                e.dma_start(out=qt.ap().rearrange("p (h t) -> p h t", t=T),
                            in_=qscr.rearrange("h p t -> p h t")[:, :, t0:t0 + T]).then_inc(SM["qld"].h, 16)
            w_ql = SM["qld"].bump(16)

            nck = (t0 + T) // 128
            for h in range(NH):
                kvh = h // 4
                po = pp[2 + h % 2]
                psr = pS1 if h % 2 == 0 else pS2
                for ck in range(nck):
                    diag = ck >= t0 // 128

                    @em("tensor")
                    def _(e, h=h, ck=ck, kvh=kvh, w_ql=w_ql, wexp=wv(SM["exd"], -1),
                          wqev=wv(SM["qev"], 0) if (h == 0 and ck == 0) else None):
                        if h == 0 and ck == 0:
                            e.wait_ge(SM["qld"].h, w_ql)
                            if wqev:
                                e.wait_ge(*wqev)
                        if wexp:
                            e.wait_ge(*wexp)
                        e.matmul(pp[ck % 2][:], kfm[:, kvh * S + ck * 128: kvh * S + (ck + 1) * 128],
                                 qt[:, h * T:(h + 1) * T], start=True, stop=True).then_inc(SM["qld"].h, 1)
                    SM["qld"].bump(1)

                    @em("scalar")
                    def _(e, h=h, ck=ck, wm=SM["qld"].c, wpv=wv(SM["pvd"], -1), w_msk=w_msk):
                        e.wait_ge(SM["qld"].h, wm)
                        if wpv:
                            e.wait_ge(*wpv)
                        e.wait_ge(SM["cld"].h, w_msk)
                        e.activation(pex[ck % 2][:], pp[ck % 2][:], AF.Exp,
                                     bias=negmbc[:, h:h + 1], scale=RSCL).then_inc(SM["exd"].h, 1)
                    w_ex = SM["exd"].bump(1)

                    if diag:
                        dd = ck - t0 // 128

                        @em("vector")
                        def _(e, ck=ck, dd=dd, w=w_ex):
                            e.wait_ge(SM["exd"].h, w)
                            e.tensor_tensor(pex[ck % 2][:], pex[ck % 2][:], msk[:, dd * T:(dd + 1) * T],
                                            ALU.mult).then_inc(SM["mkd"].h, 1)
                        SM["mkd"].bump(1)

                    @em("tensor")
                    def _(e, ck=ck, nck=nck, kvh=kvh, diag=diag, wex=w_ex, wmk=SM["mkd"].c,
                          po=po, psr=psr,
                          waod=wv(SM["aod"], -1) if ck == 0 else None,
                          wrcd=wv(SM["rcd"], -3) if ck == 0 else None):
                        if diag:
                            e.wait_ge(SM["mkd"].h, wmk)
                        else:
                            e.wait_ge(SM["exd"].h, wex)
                        if ck == 0:
                            if waod:
                                e.wait_ge(*waod)
                            if wrcd:
                                e.wait_ge(*wrcd)
                        e.matmul(po[:], vtm[:, ck * (NKV * HD) + kvh * HD: ck * (NKV * HD) + (kvh + 1) * HD],
                                 pex[ck % 2][:], start=(ck == 0), stop=(ck == nck - 1))
                        e.matmul(psr[:], ones_cb[:], pex[ck % 2][:],
                                 start=(ck == 0), stop=(ck == nck - 1)).then_inc(SM["pvd"].h, 1)
                    SM["pvd"].bump(1)
                w_pv = SM["pvd"].c

                @em("vector")
                def _(e, w=w_pv, psr=psr, wrb=wv(SM["rcd"], -1)):
                    e.wait_ge(SM["pvd"].h, w)
                    if wrb:
                        e.wait_ge(*wrb)
                    e.reciprocal(rec[:], psr[0:1, 0:T]).then_inc(SM["rcd"].h, 1)
                w_rc = SM["rcd"].bump(1)

                @em("tensor")
                def _(e, w=w_rc, wnm=wv(SM["rcd"], -1)):
                    e.wait_ge(SM["rcd"].h, w)
                    if wnm:
                        e.wait_ge(*wnm)
                    e.matmul(pp[4][:, 0:T], ones_rf[:], rec[:], start=True, stop=True).then_inc(SM["nmm"].h, 1)
                w_bc2 = SM["nmm"].bump(1)

                @em("scalar")
                def _(e, w=w_bc2, wao=wv(SM["aod"], 0)):
                    e.wait_ge(SM["nmm"].h, w)
                    if wao:
                        e.wait_ge(*wao)
                    e.copy(rbc[:], pp[4][:, 0:T]).then_inc(SM["rcd"].h, 1)
                w_rbc = SM["rcd"].bump(1)

                @em("vector")
                def _(e, h=h, w=w_rbc, po=po):
                    e.wait_ge(SM["rcd"].h, w)
                    e.tensor_tensor(ao[:, h * T:(h + 1) * T], po[:], rbc[:], ALU.mult).then_inc(SM["aod"].h, 1)
                SM["aod"].bump(1)
            all_heads = SM["aod"].c

            for b in range(7):
                units = list(range(5 if b < 6 else 2))
                wo_base = SM["wqf"].c
                half_waits = []
                for hh in range(2):
                    par = (2 * b + hh) % 2
                    if conv:
                        @em("gpsimd")
                        def _(e, b=b, hh=hh, par=par, nu=len(units), wfree=whalf_done[par], wwb=wv(SM["wwb"], 0)):
                            if wfree >= 1:
                                e.wait_ge(SM["wqf"].h, wfree)
                            if wwb:
                                e.wait_ge(*wwb)
                            nc_ = 128 * nu
                            e.dma_start(out=woh[par].ap().rearrange("p (ci col) -> p ci col", col=640)[:, :, 0:nc_],
                                        in_=wov4[:, hh, :, b * 640:b * 640 + nc_]).then_inc(SM[f"wqlg{par}"].h, 16)
                        wldw = (SM[f"wqlg{par}"].h, SM[f"wqlg{par}"].bump(16))

                        @em("sync")
                        def _(e, b=b, hh=hh, par=par, nu=len(units), wldw=wldw):
                            e.wait_ge(*wldw)
                            nc_ = 128 * nu
                            e.dma_start(out=woB4[:, hh, :, b * 640:b * 640 + nc_],
                                        in_=woh[par].ap().rearrange("p (ci col) -> p ci col", col=640)[:, :, 0:nc_]).then_inc(SM["wwb"].h, 16)
                        SM["wwb"].bump(16)
                    else:
                        @em("sync")
                        def _(e, b=b, hh=hh, par=par, nu=len(units), wfree=whalf_done[par],
                              wwb=wv(SM["wwb"], 0) if (b == 0 and hh == 0 and tq == 1) else None):
                            if wwb:
                                e.wait_ge(*wwb)
                            if wfree >= 1:
                                e.wait_ge(SM["wqf"].h, wfree)
                            nc_ = 128 * nu
                            e.dma_start(out=woh[par].ap().rearrange("p (ci col) -> p ci col", col=640)[:, :, 0:nc_],
                                        in_=woB4[:, hh, :, b * 640:b * 640 + nc_]).then_inc(SM[f"wql{par}"].h, 16)
                        wldw = (SM[f"wql{par}"].h, SM[f"wql{par}"].bump(16))
                    half_waits.append(wldw)
                    whalf_done[par] = wo_base + 16 * (hh + 1)

                for c in range(NCH):
                    par = (2 * b + c // 16) % 2
                    @em("tensor")
                    def _(e, c=c, par=par, units=tuple(units),
                          wldw=half_waits[c // 16] if c % 16 == 0 else None, wah=all_heads,
                          wev=wv(SM["qev"], 0) if c == 0 else None,
                          wex=wv(SM["exd"], 0) if c == 0 else None,
                          wrc=wv(SM["rcd"], 0) if c == 0 else None,
                          wpo=wv(SM["ppd"], 0) if c == 0 else None):
                        if wldw:
                            e.wait_ge(*wldw)
                        if c == 0:
                            e.wait_ge(SM["aod"].h, wah)
                            if wpo:
                                e.wait_ge(*wpo)
                            if wev:
                                e.wait_ge(*wev)
                            if wex:
                                e.wait_ge(*wex)
                            if wrc:
                                e.wait_ge(*wrc)
                        last = None
                        for u in units:
                            last = e.matmul(pp[u][:], woh[par][:, (c % 16) * 640 + u * 128:(c % 16) * 640 + (u + 1) * 128],
                                            ao[:, c * T:(c + 1) * T], start=(c == 0), stop=(c == NCH - 1))
                        last.then_inc(SM["wqf"].h, 1)
                    SM["wqf"].bump(1)
                w_wmm = SM["wqf"].c

                for u in units:
                    ot = b * 5 + u

                    @em("scalar")
                    def _(e, ot=ot, t0=t0, wfree=wv(SM["qev"], -1)):
                        if wfree:
                            e.wait_ge(*wfree)
                        e.dma_start(out=xfs[ot % 2][:], in_=src_fm3[ot][:, t0:t0 + T]).then_inc(SM[f"xfl{ot % 2}"].h, 16)
                    wxl = (SM[f"xfl{ot % 2}"].h, SM[f"xfl{ot % 2}"].bump(16))

                    @em("vector")
                    def _(e, ot=ot, u=u, wmm=w_wmm, wxl=wxl, wowb=wv(SM["owb"], 0)):
                        e.wait_ge(SM["wqf"].h, wmm)
                        e.wait_ge(*wxl)
                        if wowb:
                            e.wait_ge(*wowb)
                        e.tensor_add(ogs[ot % 2][:], pp[u][:], xfs[ot % 2][:]).then_inc(SM["qev"].h, 1)
                    w_ev2 = SM["qev"].bump(1)

                    if not dst_is_out:
                        @em("sync")
                        def _(e, ot=ot, t0=t0, w=w_ev2):
                            e.wait_ge(SM["qev"].h, w)
                            e.dma_start(out=xfmB[ot][:, t0:t0 + T], in_=ogs[ot % 2][:]).then_inc(SM["owb"].h, 16)
                        SM["owb"].bump(16)
                    else:
                        @em("tensor")
                        def _(e, ot=ot, u=u, w=w_ev2, wpo=wv(SM["ppd"], 0)):
                            e.wait_ge(SM["qev"].h, w)
                            if wpo:
                                e.wait_ge(*wpo)
                            last = None
                            for j in range(4):
                                last = e.transpose(pp[u][:, j * 128:(j + 1) * 128],
                                                   ogs[ot % 2][:, j * 128:(j + 1) * 128], idnf[:])
                            last.then_inc(SM["nmm"].h, 1)
                        w_tr = SM["nmm"].bump(1)

                        @em("scalar")
                        def _(e, u=u, w=w_tr, wwb=wv(SM["owb"], 0)):
                            e.wait_ge(SM["nmm"].h, w)
                            if wwb:
                                e.wait_ge(*wwb)
                            e.copy(otm2[:], pp[u][:]).then_inc(SM["ppd"].h, 1)
                        w_ot2 = SM["ppd"].bump(1)

                        @em("sync")
                        def _(e, ot=ot, tq=tq, w=w_ot2):
                            e.wait_ge(SM["ppd"].h, w)
                            e.dma_start(out=ov2[tq, :, :, ot, :],
                                        in_=otm2.ap().rearrange("i (j m) -> i j m", m=128)).then_inc(SM["owb"].h, 16)
                        SM["owb"].bump(16)
        aes.close()
        kv_stack.close()
        phase_entry_waits()

    if "f2" in phases:
        src3 = xfmB if "attn" in phases else xfmI
        stk = ExitStack()
        ffn_phase("b", src3, None, out, "f1" not in phases, stk,
                  (SM["owb"].h, SM["owb"].c) if "attn" not in phases else None)
        stk.close()

    @em("sync")
    def _(e, w=SM["owb"].c):
        e.wait_ge(SM["owb"].h, w)

    with nc.Block() as blk:
        @blk.tensor
        def _(e):
            for f in prog["tensor"]:
                f(e)

        @blk.scalar
        def _(e):
            for f in prog["scalar"]:
                f(e)

        @blk.vector
        def _(e):
            for f in prog["vector"]:
                f(e)

        @blk.gpsimd
        def _(e):
            for f in prog["gpsimd"]:
                f(e)

        @blk.sync
        def _(e):
            for f in prog["sync"]:
                f(e)

    ES.close()
    return nc


def host_inputs(inp):
    wqkv = np.asarray(inp["L0_attn_w_qkv"], dtype=np.float32)
    wo = np.ascontiguousarray(np.asarray(inp["L0_attn_w_o"], dtype=np.float32))
    perm = np.concatenate([np.arange(0, 128, 2), np.arange(1, 128, 2)])
    wq = wqkv.copy()
    for h in range(NH):
        wq[:, h * 128:(h + 1) * 128] = wqkv[:, h * 128 + perm]
    for kh in range(NKV):
        b0 = D + kh * 128
        wq[:, b0:b0 + 128] = wqkv[:, b0 + perm]
    wq = np.ascontiguousarray(wq)
    gT = np.ascontiguousarray(np.asarray(inp["L0_ffn_gate_wq"], dtype=np.int32).T)
    uT = np.ascontiguousarray(np.asarray(inp["L0_ffn_up_wq"], dtype=np.int32).T)
    dT = np.ascontiguousarray(np.asarray(inp["L0_ffn_down_wq"], dtype=np.int32).T)
    gs = np.ascontiguousarray(np.asarray(inp["L0_ffn_gate_scale"], dtype=np.float32).reshape(NI, 128).T)
    us = np.ascontiguousarray(np.asarray(inp["L0_ffn_up_scale"], dtype=np.float32).reshape(NI, 128).T)
    ds = np.ascontiguousarray(np.asarray(inp["L0_ffn_down_scale"], dtype=np.float32).reshape(NCH, 128).T)
    idn = np.eye(128, dtype=ml_dtypes.bfloat16)
    idnf = np.eye(128, dtype=np.float32)
    j = np.arange(64, dtype=np.float64)
    invf = 1.0 / (10000.0 ** (2.0 * j / 128.0))
    t = np.arange(S, dtype=np.float64)
    ang = invf[:, None] * t[None, :]
    cosq = np.cos(ang).astype(np.float32)
    sinq = np.sin(ang).astype(np.float32)
    ii = np.arange(128)[:, None]
    jj = np.arange(T)[None, :]
    msk = np.concatenate([(ii <= jj - dd * 128) for dd in range(4)], axis=1).astype(ml_dtypes.bfloat16)
    return {"gT": gT, "uT": uT, "dT": dT, "wqkv": wq, "wo": wo,
            "gs": gs, "us": us, "ds": ds, "idn": idn, "idnf": idnf,
            "cosq": cosq, "sinq": sinq, "msk": msk}


def kernel(**inp):
    nc = build_nc()
    base = host_inputs(inp)
    xall = np.asarray(inp["L0_attn_x"], dtype=np.float32)
    in_maps = [{**base, "x": np.ascontiguousarray(xall[c])} for c in range(8)]
    res = run_bass_kernel_spmd(nc, in_maps, core_ids=list(range(8)))
    return np.stack([np.asarray(res.results[c]["out"], dtype=np.float32) for c in range(8)], axis=0)


# version 06 (final)
# speedup vs baseline: 1.1750x; 1.0746x over previous
import numpy as np
import ml_dtypes
import concourse.bass as bass
import concourse.mybir as mybir
from concourse.bass_utils import run_bass_kernel_spmd
from contextlib import ExitStack

F32 = mybir.dt.float32
BF16 = mybir.dt.bfloat16
I32 = mybir.dt.int32
AF = mybir.ActivationFunctionType
ALU = mybir.AluOpType
AX = mybir.AxisListType

S = 2048
D = 4096
II = 11008
NI = II // 128
NCH = D // 128
T = 512
NTT = S // T
NH = 32
NKV = 8
HD = 128
QKVO = D + 2 * NKV * HD
RSCL = float(1.0 / np.sqrt(128.0))
ENGS = ("tensor", "scalar", "vector", "gpsimd", "sync")

PHASES = ("f1", "qkv", "attn", "f2")


class Sem:
    __slots__ = ("h", "c")

    def __init__(self, es, nc, name):
        self.h = es.enter_context(nc.semaphore(name))
        self.c = 0

    def bump(self, n=1):
        self.c += n
        return self.c


def build_nc(phases=PHASES):
    nc = bass.Bass("TRN2", target_bir_lowering=False)
    x_in = nc.dram_tensor("x", [S, D], F32, kind="ExternalInput").ap()
    gT = nc.dram_tensor("gT", [D, II], I32, kind="ExternalInput").ap()
    uT = nc.dram_tensor("uT", [D, II], I32, kind="ExternalInput").ap()
    dT = nc.dram_tensor("dT", [II, D], I32, kind="ExternalInput").ap()
    wqkv = nc.dram_tensor("wqkv", [D, QKVO], F32, kind="ExternalInput").ap()
    wo_in = nc.dram_tensor("wo", [D, D], F32, kind="ExternalInput").ap()
    gs_in = nc.dram_tensor("gs", [128, NI], F32, kind="ExternalInput").ap()
    us_in = nc.dram_tensor("us", [128, NI], F32, kind="ExternalInput").ap()
    ds_in = nc.dram_tensor("ds", [128, NCH], F32, kind="ExternalInput").ap()
    idn_in = nc.dram_tensor("idn", [128, 128], BF16, kind="ExternalInput").ap()
    idnf_in = nc.dram_tensor("idnf", [128, 128], F32, kind="ExternalInput").ap()
    cos_in = nc.dram_tensor("cosq", [64, S], F32, kind="ExternalInput").ap()
    sin_in = nc.dram_tensor("sinq", [64, S], F32, kind="ExternalInput").ap()
    msk_in = nc.dram_tensor("msk", [128, 4 * T], BF16, kind="ExternalInput").ap()
    out = nc.dram_tensor("out", [S, D], F32, kind="ExternalOutput").ap()

    guB = nc.dram_tensor("guB", [NI, 2, 128, D], BF16, kind="Internal").ap()
    dB = nc.dram_tensor("dB", [NCH, 128, II], BF16, kind="Internal").ap()
    wqB = nc.dram_tensor("wqB", [D, QKVO], BF16, kind="Internal").ap()
    woB = nc.dram_tensor("woB", [D, D], BF16, kind="Internal").ap()
    xfmI = nc.dram_tensor("xfmI", [NCH, 128, S], F32, kind="Internal").ap()
    xfmA = nc.dram_tensor("xfmA", [NCH, 128, S], F32, kind="Internal").ap()
    xfmB = nc.dram_tensor("xfmB", [NCH, 128, S], F32, kind="Internal").ap()
    qscr = nc.dram_tensor("qscr", [NH, 128, S], BF16, kind="Internal").ap()

    ES = ExitStack()
    E = ES.enter_context

    idn = E(nc.sbuf_tensor("idn_s", [128, 128], BF16))
    idnf = E(nc.sbuf_tensor("idnf_s", [128, 128], F32))
    ones_cb = E(nc.sbuf_tensor("ones_cb", [128, 1], BF16))
    ones_rf = E(nc.sbuf_tensor("ones_rf", [1, 128], F32))
    gsus = E(nc.sbuf_tensor("gsus", [128, NI], F32))
    us_s = E(nc.sbuf_tensor("us_s", [128, NI], F32))
    dsch = E(nc.sbuf_tensor("dsch", [128, NCH], F32))
    negmbc = E(nc.sbuf_tensor("negmbc", [128, NH], F32))
    qn = E(nc.sbuf_tensor("qn", [1, NH], F32))
    kn = E(nc.sbuf_tensor("kn", [1, NKV], F32))
    knr = E(nc.sbuf_tensor("knr", [1, NH], F32))
    sm1 = E(nc.sbuf_tensor("sm1", [1, NH], F32))
    srow = [E(nc.sbuf_tensor(f"srow{i}", [1, T], F32)) for i in range(3)]
    a_bc = E(nc.sbuf_tensor("a_bc", [128, T], F32))
    b_bc = E(nc.sbuf_tensor("b_bc", [128, T], F32))
    cnv = [E(nc.sbuf_tensor(f"cnv{i}", [128, T], BF16)) for i in range(4)]
    lnt = E(nc.sbuf_tensor("lnt", [128, T], F32))
    xfs = [E(nc.sbuf_tensor(f"xf{i}", [128, T], F32)) for i in range(2)]
    sfs = [E(nc.sbuf_tensor(f"sf{i}", [128, T], F32)) for i in range(2)]
    ogs = [E(nc.sbuf_tensor(f"og{i}", [128, T], F32)) for i in range(2)]

    pp = [E(nc.psum_tensor(f"pp{i}", [128, 512], F32)) for i in range(5)]
    pS1 = E(nc.psum_tensor("pS1", [1, 512], F32))
    pS2 = E(nc.psum_tensor("pS2", [1, 512], F32))
    pT = E(nc.psum_tensor("pT", [128, 512], BF16))

    sem_names = ["cld", "wgl0", "wgl1", "wglg0", "wglg1", "wdlg0", "wdlg1", "wqlg0", "wqlg1",
                 "l1d", "uev", "gud", "wwb", "wdl0", "wdl1", "wdf",
                 "dev", "xfl0", "xfl1", "sfl0", "sfl1", "xnb", "owb", "nsq", "nmm", "nrd",
                 "wql0", "wql1", "wqf", "qev", "qwb0", "qwb1", "tvd", "vtd",
                 "qld", "smm", "exd", "mkd", "pvd", "aod", "rcd", "std", "ppd"]
    SM = {n: Sem(ES, nc, n) for n in sem_names}

    prog = {e: [] for e in ENGS}

    def em(eng):
        def deco(f):
            prog[eng].append(f)
        return deco

    def wv(sem, delta=0):
        v = sem.c + delta
        return (sem.h, v) if v >= 1 else None

    def wlist(*sems):
        return tuple(x for x in (wv(s) for s in sems) if x)

    def phase_entry_waits():
        caps = [(s.h, s.c) for s in SM.values() if s.c >= 1]
        for eng in ENGS:
            @em(eng)
            def _(e, caps=tuple(caps)):
                for h, v in caps:
                    e.wait_ge(h, v)

    @em("sync")
    def _(e):
        e.dma_start(out=idn[:], in_=idn_in).then_inc(SM["cld"].h, 16)
        e.dma_start(out=idnf[:], in_=idnf_in).then_inc(SM["cld"].h, 16)
        e.dma_start(out=gsus[:], in_=gs_in).then_inc(SM["cld"].h, 16)
        e.dma_start(out=us_s[:], in_=us_in).then_inc(SM["cld"].h, 16)
        e.dma_start(out=dsch[:], in_=ds_in).then_inc(SM["cld"].h, 16)
    w_cld0 = SM["cld"].bump(5 * 16)

    @em("vector")
    def _(e, w=w_cld0):
        e.memset(ones_cb[:], 1.0)
        e.memset(ones_rf[:], 1.0)
        e.memset(qn[:], 0.0)
        e.memset(kn[:], 0.0)
        e.wait_ge(SM["cld"].h, w)
        e.tensor_mul(gsus[:], gsus[:], us_s[:])
        e.tensor_scalar_mul(dsch[:], dsch[:], 0.5).then_inc(SM["std"].h, 1)
    SM["std"].bump(1)
    w_consts = SM["std"].c

    def emit_prepass():
        xv = x_in.rearrange("(q j i) (c m) -> i j q c m", i=128, j=4, m=128)
        k = 0
        for c in range(NCH):
            for q in range(NTT):
                slot = k % 2

                @em("sync")
                def _(e, c=c, q=q, slot=slot, wfree=wv(SM["nmm"], -1)):
                    if wfree:
                        e.wait_ge(*wfree)
                    e.dma_start(out=xfs[slot].ap().rearrange("i (j m) -> i j m", m=128),
                                in_=xv[:, :, q, c, :]).then_inc(SM[f"xfl{slot}"].h, 16)
                w_ld = (SM[f"xfl{slot}"].h, SM[f"xfl{slot}"].bump(16))

                @em("tensor")
                def _(e, slot=slot, w=w_ld, wst=w_consts, wev=wv(SM["nsq"], -1)):
                    e.wait_ge(*w)
                    e.wait_ge(SM["std"].h, wst)
                    if wev:
                        e.wait_ge(*wev)
                    last = None
                    for j in range(4):
                        last = e.transpose(pp[slot][:, j * 128:(j + 1) * 128],
                                           xfs[slot][:, j * 128:(j + 1) * 128], idnf[:])
                    last.then_inc(SM["nmm"].h, 1)
                w_tr = SM["nmm"].bump(1)

                @em("scalar")
                def _(e, slot=slot, w=w_tr, wwb=wv(SM["owb"], 0)):
                    e.wait_ge(SM["nmm"].h, w)
                    if wwb:
                        e.wait_ge(*wwb)
                    e.copy(ogs[slot][:], pp[slot][:]).then_inc(SM["nsq"].h, 1)
                w_ev = SM["nsq"].bump(1)

                @em("gpsimd")
                def _(e, c=c, q=q, slot=slot, w=w_ev):
                    e.wait_ge(SM["nsq"].h, w)
                    e.dma_start(out=xfmI[c][:, q * T:(q + 1) * T], in_=ogs[slot][:]).then_inc(SM["owb"].h, 16)
                SM["owb"].bump(16)
                k += 1
        return SM["owb"].c

    def ln_tile(src_fm3, tt, xn_guard_sem, tens_guards, src_wait, xn):
        t0 = tt * T
        for c in range(NCH):
            @em("scalar")
            def _(e, c=c, t0=t0, wfree=wv(SM["nmm"], -1), sw=src_wait if c == 0 else None):
                if sw:
                    e.wait_ge(*sw)
                if wfree:
                    e.wait_ge(*wfree)
                e.dma_start(out=sfs[c % 2][:], in_=src_fm3[c][:, t0:t0 + T]).then_inc(SM[f"sfl{c % 2}"].h, 16)
            w_ld = (SM[f"sfl{c % 2}"].h, SM[f"sfl{c % 2}"].bump(16))

            @em("scalar")
            def _(e, c=c, w=w_ld, wmm=wv(SM["nmm"], -1)):
                e.wait_ge(*w)
                if wmm:
                    e.wait_ge(*wmm)
                e.copy(cnv[2 * (c % 2)][:], sfs[c % 2][:])
                e.activation(cnv[2 * (c % 2) + 1][:], sfs[c % 2][:], AF.Square).then_inc(SM["nsq"].h, 1)
            w_cv = SM["nsq"].bump(1)

            @em("tensor")
            def _(e, c=c, w=w_cv, tg=tens_guards if c == 0 else ()):
                e.wait_ge(SM["nsq"].h, w)
                for hvv in tg:
                    e.wait_ge(*hvv)
                e.matmul(pS1[:], ones_cb[:], cnv[2 * (c % 2)][:], start=(c == 0), stop=(c == NCH - 1))
                e.matmul(pS2[:], ones_cb[:], cnv[2 * (c % 2) + 1][:], start=(c == 0), stop=(c == NCH - 1)).then_inc(SM["nmm"].h, 1)
            SM["nmm"].bump(1)
        w_stat = SM["nmm"].c

        @em("vector")
        def _(e, w=w_stat):
            e.wait_ge(SM["nmm"].h, w)
            m, r, t2 = srow
            e.tensor_scalar_mul(m[:], pS1[0:1, 0:T], 1.0 / D)
            e.tensor_scalar_mul(r[:], pS2[0:1, 0:T], 1.0 / D)
            e.drain()
            e.tensor_tensor(t2[:], m[:], m[:], ALU.mult)
            e.drain()
            e.tensor_sub(r[:], r[:], t2[:])
            e.drain()
            e.tensor_scalar_add(r[:], r[:], 1e-5)
            e.drain()
            e.tensor_copy(t2[:], r[:])
            e.drain()
            e.reciprocal(r[:], r[:]).then_inc(SM["nsq"].h, 1)
        w1 = SM["nsq"].bump(1)

        @em("scalar")
        def _(e, w=w1):
            e.wait_ge(SM["nsq"].h, w)
            e.activation(srow[1][:], srow[1][:], AF.Sqrt).then_inc(SM["nsq"].h, 1)
        w2 = SM["nsq"].bump(1)

        @em("vector")
        def _(e, w=w2):
            e.wait_ge(SM["nsq"].h, w)
            m, r, t2 = srow
            e.tensor_tensor(lnt[0:1, :], r[:], r[:], ALU.mult)
            e.drain()
            e.tensor_tensor(lnt[0:1, :], lnt[0:1, :], t2[:], ALU.mult)
            e.drain()
            e.tensor_scalar(lnt[0:1, :], lnt[0:1, :], -0.5, 1.5, ALU.mult, ALU.add)
            e.drain()
            e.tensor_tensor(r[:], r[:], lnt[0:1, :], ALU.mult)
            e.drain()
            e.scalar_tensor_tensor(m[:], m[:], -1.0, r[:], ALU.mult, ALU.mult).then_inc(SM["nsq"].h, 1)
        w3 = SM["nsq"].bump(1)

        @em("tensor")
        def _(e, w=w3, wab=wv(SM["xnb"], 0), wpo=wv(SM["ppd"], 0)):
            e.wait_ge(SM["nsq"].h, w)
            if wab:
                e.wait_ge(*wab)
            if wpo:
                e.wait_ge(*wpo)
            e.matmul(pp[0][:, 0:T], ones_rf[:], srow[1][:], start=True, stop=True)
            e.matmul(pp[1][:, 0:T], ones_rf[:], srow[0][:], start=True, stop=True).then_inc(SM["nmm"].h, 1)
        w4 = SM["nmm"].bump(1)

        @em("scalar")
        def _(e, w=w4):
            e.wait_ge(SM["nmm"].h, w)
            e.copy(a_bc[:], pp[0][:, 0:T])
            e.copy(b_bc[:], pp[1][:, 0:T]).then_inc(SM["nsq"].h, 1)
        w_abc = SM["nsq"].bump(1)

        xn_w = (xn_guard_sem.h, xn_guard_sem.c) if xn_guard_sem.c >= 1 else None
        for c2 in range(NCH // 2):
            for c in (2 * c2, 2 * c2 + 1):
                @em("scalar")
                def _(e, c=c, t0=t0, wfree=wv(SM["xnb"], -1 if c % 2 == 0 else 0), wab=w_abc,
                      wdev=wlist(SM["dev"], SM["qev"], SM["nmm"]) if c < 2 else ()):
                    e.wait_ge(SM["nsq"].h, wab)
                    for hvv in wdev:
                        e.wait_ge(*hvv)
                    if wfree:
                        e.wait_ge(*wfree)
                    e.dma_start(out=xfs[c % 2][:], in_=src_fm3[c][:, t0:t0 + T]).then_inc(SM[f"xfl{c % 2}"].h, 16)
                SM[f"xfl{c % 2}"].bump(16)
            wld = ((SM["xfl0"].h, SM["xfl0"].c), (SM["xfl1"].h, SM["xfl1"].c))

            @em("vector")
            def _(e, c2=c2, wld=wld, wab=w_abc, xw=xn_w if c2 == 0 else None):
                e.wait_ge(*wld[0])
                e.wait_ge(*wld[1])
                e.wait_ge(SM["nsq"].h, wab)
                if xw:
                    e.wait_ge(*xw)
                ca, cb = 2 * c2, 2 * c2 + 1
                e.tensor_tensor(lnt[:], xfs[0][:], a_bc[:], ALU.mult)
                e.tensor_tensor(ogs[0][:], xfs[1][:], a_bc[:], ALU.mult)
                e.drain()
                e.tensor_add(xn[:, ca * T:(ca + 1) * T], lnt[:], b_bc[:]).then_inc(SM["xnb"].h, 1)
                e.tensor_add(xn[:, cb * T:(cb + 1) * T], ogs[0][:], b_bc[:]).then_inc(SM["xnb"].h, 1)
            SM["xnb"].bump(2)
        return SM["xnb"].c

    def ffn_phase(tag, src_fm3, dst_fm3, dst_tm, first, stk, src_wait):
        xn = stk.enter_context(nc.sbuf_tensor(f"xn{tag}", [128, NCH * T], BF16))
        gu = stk.enter_context(nc.sbuf_tensor(f"gu_{tag}", [128, NI * T], BF16))
        wgu = [stk.enter_context(nc.sbuf_tensor(f"wgu{tag}{i}", [128, 2 * D], BF16)) for i in range(2)]
        wd = [stk.enter_context(nc.sbuf_tensor(f"wd{tag}{i}", [128, II // 2], BF16)) for i in range(2)]
        usb = [stk.enter_context(nc.sbuf_tensor(f"us{tag}{i}", [128, T], BF16)) for i in range(2)]
        otm = stk.enter_context(nc.sbuf_tensor(f"otm{tag}", [128, T], F32)) if dst_tm is not None else None
        gv = gT.rearrange("(c p) (t m) -> p c t m", p=128, m=128)
        uv = uT.rearrange("(c p) (t m) -> p c t m", p=128, m=128)
        dv = dT.rearrange("(j p) (h m) -> p j h m", p=128, m=128)
        if dst_tm is not None:
            ov = dst_tm.rearrange("(q j i) (c m) -> q i j c m", i=128, j=4, m=128)

        for tt in range(NTT):
            t0 = tt * T
            conv = first and tt == 0
            xn_tgt = ln_tile(src_fm3, tt, SM["l1d"],
                             wlist(SM["nrd"], SM["rcd"], SM["gud"], SM["uev"], SM["qev"], SM["exd"], SM["dev"]),
                             src_wait if tt == 0 else None, xn)

            for tau in range(NI):
                slot = tau % 2
                if conv:
                    @em("gpsimd")
                    def _(e, tau=tau, slot=slot, wfree=wv(SM["l1d"], -1), wwb=wv(SM["wwb"], 0)):
                        if wfree:
                            e.wait_ge(*wfree)
                        if wwb:
                            e.wait_ge(*wwb)
                        e.dma_start(out=wgu[slot].ap().rearrange("p (c m) -> p c m", m=128)[:, 0:NCH, :],
                                    in_=gv[:, :, tau, :]).then_inc(SM[f"wglg{slot}"].h, 16)
                        e.dma_start(out=wgu[slot].ap().rearrange("p (c m) -> p c m", m=128)[:, NCH:2 * NCH, :],
                                    in_=uv[:, :, tau, :]).then_inc(SM[f"wglg{slot}"].h, 16)
                    wld = (SM[f"wglg{slot}"].h, SM[f"wglg{slot}"].bump(32))

                    @em("sync")
                    def _(e, tau=tau, slot=slot, wld=wld):
                        e.wait_ge(*wld)
                        e.dma_start(out=guB[tau].rearrange("t p f -> p t f"),
                                    in_=wgu[slot].ap().rearrange("p (t f) -> p t f", f=D)).then_inc(SM["wwb"].h, 16)
                    SM["wwb"].bump(16)
                else:
                    @em("sync")
                    def _(e, tau=tau, slot=slot, wfree=wv(SM["l1d"], -1),
                          wwb=wv(SM["wwb"], 0) if (tau == 0 and first and tt == 1) else None):
                        if wwb:
                            e.wait_ge(*wwb)
                        if wfree:
                            e.wait_ge(*wfree)
                        e.dma_start(out=wgu[slot].ap().rearrange("p (t f) -> p t f", f=D),
                                    in_=guB[tau].rearrange("t p f -> p t f")).then_inc(SM[f"wgl{slot}"].h, 16)
                    wld = (SM[f"wgl{slot}"].h, SM[f"wgl{slot}"].bump(16))

                @em("tensor")
                def _(e, tau=tau, slot=slot, wld=wld, xw=xn_tgt,
                      wgud=wv(SM["gud"], -1), wue=wv(SM["uev"], -1),
                      wpo=wv(SM["ppd"], 0) if tau < 2 else None):
                    e.wait_ge(*wld)
                    if wpo:
                        e.wait_ge(*wpo)
                    if wgud:
                        e.wait_ge(*wgud)
                    if wue:
                        e.wait_ge(*wue)
                    for c in range(NCH):
                        if tau == 0:
                            e.wait_ge(SM["xnb"].h, xw - NCH + c + 1)
                        e.matmul(pp[slot][:], wgu[slot][:, c * 128:(c + 1) * 128], xn[:, c * T:(c + 1) * T],
                                 start=(c == 0), stop=(c == NCH - 1))
                    for c in range(NCH):
                        ins = e.matmul(pp[2 + slot][:], wgu[slot][:, D + c * 128:D + (c + 1) * 128], xn[:, c * T:(c + 1) * T],
                                       start=(c == 0), stop=(c == NCH - 1))
                    ins.then_inc(SM["l1d"].h, 1)
                SM["l1d"].bump(1)

                @em("scalar")
                def _(e, slot=slot, w=SM["l1d"].c):
                    e.wait_ge(SM["l1d"].h, w)
                    e.copy(usb[slot][:], pp[2 + slot][:]).then_inc(SM["uev"].h, 1)
                SM["uev"].bump(1)

                @em("vector")
                def _(e, tau=tau, slot=slot, w=SM["uev"].c):
                    e.wait_ge(SM["uev"].h, w)
                    e.scalar_tensor_tensor(gu[:, tau * T:(tau + 1) * T], pp[slot][:], gsus[:, tau:tau + 1],
                                           usb[slot][:], ALU.mult, ALU.mult).then_inc(SM["gud"].h, 1)
                SM["gud"].bump(1)
            l1_all = SM["gud"].c

            for ht in range(NCH):
                for half in range(2):
                    hslot = (2 * ht + half) % 2
                    if conv:
                        @em("gpsimd")
                        def _(e, ht=ht, half=half, hslot=hslot, wfree=wv(SM["wdf"], -1), wwb=wv(SM["wwb"], 0)):
                            if wfree:
                                e.wait_ge(*wfree)
                            if wwb:
                                e.wait_ge(*wwb)
                            e.dma_start(out=wd[hslot].ap().rearrange("p (j m) -> p j m", m=128),
                                        in_=dv[:, half * 43:(half + 1) * 43, ht, :]).then_inc(SM[f"wdlg{hslot}"].h, 16)
                        wldd = (SM[f"wdlg{hslot}"].h, SM[f"wdlg{hslot}"].bump(16))

                        @em("sync")
                        def _(e, ht=ht, half=half, hslot=hslot, wldd=wldd):
                            e.wait_ge(*wldd)
                            e.dma_start(out=dB[ht].rearrange("p (hh x) -> p hh x", hh=2)[:, half, :],
                                        in_=wd[hslot][:]).then_inc(SM["wwb"].h, 16)
                        SM["wwb"].bump(16)
                    else:
                        @em("sync")
                        def _(e, ht=ht, half=half, hslot=hslot, wfree=wv(SM["wdf"], -1),
                              wwb=wv(SM["wwb"], 0) if (ht == 0 and half == 0 and first and tt == 1) else None):
                            if wwb:
                                e.wait_ge(*wwb)
                            if wfree:
                                e.wait_ge(*wfree)
                            e.dma_start(out=wd[hslot][:],
                                        in_=dB[ht].rearrange("p (hh x) -> p hh x", hh=2)[:, half, :]).then_inc(SM[f"wdl{hslot}"].h, 16)
                        wldd = (SM[f"wdl{hslot}"].h, SM[f"wdl{hslot}"].bump(16))

                    @em("tensor")
                    def _(e, ht=ht, half=half, hslot=hslot, wldd=wldd, wall=l1_all,
                          wde=wv(SM["dev"], 0) if half == 0 else None):
                        e.wait_ge(*wldd)
                        if ht == 0 and half == 0:
                            e.wait_ge(SM["gud"].h, wall)
                        if wde:
                            e.wait_ge(*wde)
                        last = None
                        for jj in range(43):
                            j = half * 43 + jj
                            last = e.matmul(pp[4][:], wd[hslot][:, jj * 128:(jj + 1) * 128],
                                            gu[:, j * T:(j + 1) * T], start=(j == 0), stop=(j == NI - 1))
                        last.then_inc(SM["wdf"].h, 1)
                    SM["wdf"].bump(1)
                w_dmm = SM["wdf"].c

                @em("scalar")
                def _(e, ht=ht, t0=t0, wfree=wv(SM["dev"], -1),
                      wxn=wv(SM["xnb"], 0) if ht == 0 else None):
                    if wfree:
                        e.wait_ge(*wfree)
                    if wxn:
                        e.wait_ge(*wxn)
                    e.dma_start(out=xfs[ht % 2][:], in_=src_fm3[ht][:, t0:t0 + T]).then_inc(SM[f"xfl{ht % 2}"].h, 16)
                wxl = (SM[f"xfl{ht % 2}"].h, SM[f"xfl{ht % 2}"].bump(16))

                @em("vector")
                def _(e, ht=ht, wmm=w_dmm, wxl=wxl, wowb=wv(SM["owb"], 0)):
                    e.wait_ge(SM["wdf"].h, wmm)
                    e.wait_ge(*wxl)
                    if wowb:
                        e.wait_ge(*wowb)
                    e.scalar_tensor_tensor(ogs[ht % 2][:], pp[4][:], dsch[:, ht:ht + 1],
                                           xfs[ht % 2][:], ALU.mult, ALU.add).then_inc(SM["dev"].h, 1)
                SM["dev"].bump(1)
                w_ev = SM["dev"].c

                if dst_fm3 is not None:
                    @em("gpsimd")
                    def _(e, ht=ht, t0=t0, w=w_ev):
                        e.wait_ge(SM["dev"].h, w)
                        e.dma_start(out=dst_fm3[ht][:, t0:t0 + T], in_=ogs[ht % 2][:]).then_inc(SM["owb"].h, 16)
                    SM["owb"].bump(16)
                else:
                    @em("tensor")
                    def _(e, ht=ht, w=w_ev, wpo=wv(SM["ppd"], 0)):
                        e.wait_ge(SM["dev"].h, w)
                        if wpo:
                            e.wait_ge(*wpo)
                        last = None
                        for j in range(4):
                            last = e.transpose(pp[ht % 2][:, j * 128:(j + 1) * 128],
                                               ogs[ht % 2][:, j * 128:(j + 1) * 128], idnf[:])
                        last.then_inc(SM["nmm"].h, 1)
                    w_tr = SM["nmm"].bump(1)

                    @em("scalar")
                    def _(e, ht=ht, w=w_tr, wwb=wv(SM["owb"], 0)):
                        e.wait_ge(SM["nmm"].h, w)
                        if wwb:
                            e.wait_ge(*wwb)
                        e.copy(otm[:], pp[ht % 2][:]).then_inc(SM["ppd"].h, 1)
                    w_ot = SM["ppd"].bump(1)

                    @em("gpsimd")
                    def _(e, ht=ht, tt=tt, w=w_ot):
                        e.wait_ge(SM["ppd"].h, w)
                        e.dma_start(out=ov[tt, :, :, ht, :],
                                    in_=otm.ap().rearrange("i (j m) -> i j m", m=128)).then_inc(SM["owb"].h, 16)
                    SM["owb"].bump(16)

    w_pre = emit_prepass()
    if "f1" in phases:
        stk = ExitStack()
        last_f1 = len(phases) == 1
        ffn_phase("a", xfmI, None if last_f1 else xfmA, out if last_f1 else None,
                  True, stk, (SM["owb"].h, w_pre))
        stk.close()
        phase_entry_waits()
        src_after_f1 = xfmA
    else:
        src_after_f1 = xfmI

    kv_stack = ExitStack()
    if "qkv" in phases:
        src_fm3 = src_after_f1
        kfm = kv_stack.enter_context(nc.sbuf_tensor("kfm", [128, NKV * S], BF16))
        vtm = kv_stack.enter_context(nc.sbuf_tensor("vtm", [128, (S // 128) * NKV * HD], BF16))
        qes = ExitStack()
        xn = qes.enter_context(nc.sbuf_tensor("xnq", [128, NCH * T], BF16))
        cosq = qes.enter_context(nc.sbuf_tensor("cosq_s", [64, S], F32))
        sinq = qes.enter_context(nc.sbuf_tensor("sinq_s", [64, S], F32))
        wqh = [qes.enter_context(nc.sbuf_tensor(f"wqh{i}", [128, 16 * 640], BF16)) for i in range(2)]
        qsb = [qes.enter_context(nc.sbuf_tensor(f"qsb{i}", [128, T], BF16)) for i in range(2)]
        rsc = [qes.enter_context(nc.sbuf_tensor(f"rsc{i}", [64, T], F32)) for i in range(4)]
        vst = [qes.enter_context(nc.sbuf_tensor(f"vst{i}", [128, T], BF16)) for i in range(2)]
        sqsc = qes.enter_context(nc.sbuf_tensor("sqsc", [128, T], BF16))
        nrm = qes.enter_context(nc.sbuf_tensor("nrm", [1, 1], F32))

        @em("sync")
        def _(e):
            e.dma_start(out=cosq[:], in_=cos_in).then_inc(SM["cld"].h, 16)
            e.dma_start(out=sinq[:], in_=sin_in).then_inc(SM["cld"].h, 16)
        w_rope = SM["cld"].bump(32)

        wqv4 = wqkv.rearrange("(cc ci p) col -> p cc ci col", p=128, ci=16)
        wqB4 = wqB.rearrange("(cc ci p) col -> p cc ci col", p=128, ci=16)
        qhalf_done = [0, 0]
        nqu = 0
        q_sq = []
        for tt in range(NTT):
            t0 = tt * T
            conv = tt == 0
            xn_tgt = ln_tile(src_fm3, tt, SM["wqf"],
                             wlist(SM["nrd"], SM["rcd"], SM["qev"], SM["exd"]),
                             (SM["owb"].h, SM["owb"].c) if (tt == 0 and "f1" not in phases) else None, xn)

            for b in range(10):
                nu = 5 if b < 9 else 3
                wq_base = SM["wqf"].c
                half_waits = []
                for hh in range(2):
                    par = (2 * b + hh) % 2
                    if conv:
                        @em("gpsimd")
                        def _(e, b=b, hh=hh, par=par, nu=nu, wfree=qhalf_done[par], wwb=wv(SM["wwb"], 0)):
                            if wfree >= 1:
                                e.wait_ge(SM["wqf"].h, wfree)
                            if wwb:
                                e.wait_ge(*wwb)
                            nc_ = 128 * nu
                            e.dma_start(out=wqh[par].ap().rearrange("p (ci col) -> p ci col", col=640)[:, :, 0:nc_],
                                        in_=wqv4[:, hh, :, b * 640:b * 640 + nc_]).then_inc(SM[f"wqlg{par}"].h, 16)
                        wldq = (SM[f"wqlg{par}"].h, SM[f"wqlg{par}"].bump(16))

                        @em("sync")
                        def _(e, b=b, hh=hh, par=par, nu=nu, wldq=wldq):
                            e.wait_ge(*wldq)
                            nc_ = 128 * nu
                            e.dma_start(out=wqB4[:, hh, :, b * 640:b * 640 + nc_],
                                        in_=wqh[par].ap().rearrange("p (ci col) -> p ci col", col=640)[:, :, 0:nc_]).then_inc(SM["wwb"].h, 16)
                        SM["wwb"].bump(16)
                    else:
                        @em("sync")
                        def _(e, b=b, hh=hh, par=par, nu=nu, wfree=qhalf_done[par],
                              wwb=wv(SM["wwb"], 0) if (b == 0 and hh == 0 and tt == 1) else None):
                            if wwb:
                                e.wait_ge(*wwb)
                            if wfree >= 1:
                                e.wait_ge(SM["wqf"].h, wfree)
                            nc_ = 128 * nu
                            e.dma_start(out=wqh[par].ap().rearrange("p (ci col) -> p ci col", col=640)[:, :, 0:nc_],
                                        in_=wqB4[:, hh, :, b * 640:b * 640 + nc_]).then_inc(SM[f"wql{par}"].h, 16)
                        wldq = (SM[f"wql{par}"].h, SM[f"wql{par}"].bump(16))
                    half_waits.append(wldq)
                    qhalf_done[par] = wq_base + 16 * (hh + 1)

                for c in range(NCH):
                    par = (2 * b + c // 16) % 2
                    @em("tensor")
                    def _(e, b=b, c=c, nu=nu, par=par,
                          wldq=half_waits[c // 16] if c % 16 == 0 else None,
                          xw=xn_tgt, wev=wv(SM["qev"], 0) if c == 0 else None):
                        if wldq:
                            e.wait_ge(*wldq)
                        if b == 0:
                            e.wait_ge(SM["xnb"].h, xw - NCH + c + 1)
                        if c == 0:
                            if wev:
                                e.wait_ge(*wev)
                        last = None
                        for u in range(nu):
                            last = e.matmul(pp[u][:], wqh[par][:, (c % 16) * 640 + u * 128:(c % 16) * 640 + (u + 1) * 128],
                                            xn[:, c * T:(c + 1) * T], start=(c == 0), stop=(c == NCH - 1))
                        last.then_inc(SM["wqf"].h, 1)
                    SM["wqf"].bump(1)
                w_batch = SM["wqf"].c

                for u in range(nu):
                    ot = b * 5 + u
                    if ot < NH:
                        slot = nqu % 2

                        @em("vector")
                        def _(e, u=u, slot=slot, t0=t0, wmm=w_batch, wrope=w_rope,
                              wqwb=wv(SM[f"qwb{nqu % 2}"], 0),
                              wsq=(q_sq[nqu - 2] if nqu >= 2 else 0)):
                            e.wait_ge(SM["wqf"].h, wmm)
                            e.wait_ge(SM["cld"].h, wrope)
                            if wqwb:
                                e.wait_ge(*wqwb)
                            if wsq >= 1:
                                e.wait_ge(SM["nsq"].h, wsq)
                            psrc = pp[u]
                            cs = cosq[:, t0:t0 + T]
                            sn = sinq[:, t0:t0 + T]
                            pe = psrc[0:64, :]
                            po = psrc[64:128, :]
                            e.tensor_tensor(rsc[0][:], pe, cs, ALU.mult)
                            e.tensor_tensor(rsc[1][:], po, sn, ALU.mult)
                            e.tensor_tensor(rsc[2][:], pe, sn, ALU.mult)
                            e.tensor_tensor(rsc[3][:], po, cs, ALU.mult)
                            e.drain()
                            e.tensor_sub(qsb[slot][0:64, :], rsc[0][:], rsc[1][:])
                            e.tensor_add(qsb[slot][64:128, :], rsc[2][:], rsc[3][:]).then_inc(SM["qev"].h, 1)
                        w_rp = SM["qev"].bump(1)

                        @em("gpsimd")
                        def _(e, ot=ot, slot=slot, t0=t0, w=w_rp):
                            e.wait_ge(SM["qev"].h, w)
                            e.dma_start(out=qscr[ot][:, t0:t0 + T], in_=qsb[slot][:]).then_inc(SM[f"qwb{slot}"].h, 16)
                        SM[f"qwb{slot}"].bump(16)

                        @em("scalar")
                        def _(e, slot=slot, w=w_rp, wn=wv(SM["nmm"], 0)):
                            e.wait_ge(SM["qev"].h, w)
                            if wn:
                                e.wait_ge(*wn)
                            e.activation(sqsc[:], qsb[slot][:], AF.Square).then_inc(SM["nsq"].h, 1)
                        w_sq = SM["nsq"].bump(1)
                        q_sq.append(w_sq)
                        nqu += 1

                        @em("tensor")
                        def _(e, w=w_sq, wr=wv(SM["nrd"], 0)):
                            e.wait_ge(SM["nsq"].h, w)
                            if wr:
                                e.wait_ge(*wr)
                            e.matmul(pS1[:], ones_cb[:], sqsc[:], start=True, stop=True).then_inc(SM["nmm"].h, 1)
                        w_nm = SM["nmm"].bump(1)

                        @em("vector")
                        def _(e, ot=ot, w=w_nm):
                            e.wait_ge(SM["nmm"].h, w)
                            e.reduce_max(nrm[:], pS1[0:1, 0:T], axis=AX.X)
                            e.drain()
                            e.tensor_max(qn[0:1, ot:ot + 1], qn[0:1, ot:ot + 1], nrm[:]).then_inc(SM["nrd"].h, 1)
                        SM["nrd"].bump(1)
                    elif ot < NH + NKV:
                        kh = ot - NH

                        @em("vector")
                        def _(e, kh=kh, u=u, t0=t0, wmm=w_batch, wrope=w_rope):
                            e.wait_ge(SM["wqf"].h, wmm)
                            e.wait_ge(SM["cld"].h, wrope)
                            psrc = pp[u]
                            cs = cosq[:, t0:t0 + T]
                            sn = sinq[:, t0:t0 + T]
                            pe = psrc[0:64, :]
                            po = psrc[64:128, :]
                            e.tensor_tensor(rsc[0][:], pe, cs, ALU.mult)
                            e.tensor_tensor(rsc[1][:], po, sn, ALU.mult)
                            e.tensor_tensor(rsc[2][:], pe, sn, ALU.mult)
                            e.tensor_tensor(rsc[3][:], po, cs, ALU.mult)
                            e.drain()
                            e.tensor_sub(kfm[0:64, kh * S + t0: kh * S + t0 + T], rsc[0][:], rsc[1][:])
                            e.tensor_add(kfm[64:128, kh * S + t0: kh * S + t0 + T], rsc[2][:], rsc[3][:]).then_inc(SM["qev"].h, 1)
                        w_rp = SM["qev"].bump(1)

                        @em("scalar")
                        def _(e, kh=kh, t0=t0, w=w_rp, wn=wv(SM["nmm"], 0)):
                            e.wait_ge(SM["qev"].h, w)
                            if wn:
                                e.wait_ge(*wn)
                            e.activation(sqsc[:], kfm[:, kh * S + t0: kh * S + t0 + T], AF.Square).then_inc(SM["nsq"].h, 1)
                        w_sq = SM["nsq"].bump(1)

                        @em("tensor")
                        def _(e, w=w_sq, wr=wv(SM["nrd"], 0)):
                            e.wait_ge(SM["nsq"].h, w)
                            if wr:
                                e.wait_ge(*wr)
                            e.matmul(pS1[:], ones_cb[:], sqsc[:], start=True, stop=True).then_inc(SM["nmm"].h, 1)
                        w_nm = SM["nmm"].bump(1)

                        @em("vector")
                        def _(e, kh=kh, w=w_nm):
                            e.wait_ge(SM["nmm"].h, w)
                            e.reduce_max(nrm[:], pS1[0:1, 0:T], axis=AX.X)
                            e.drain()
                            e.tensor_max(kn[0:1, kh:kh + 1], kn[0:1, kh:kh + 1], nrm[:]).then_inc(SM["nrd"].h, 1)
                        SM["nrd"].bump(1)
                    else:
                        vh = ot - NH - NKV
                        vslot = vh % 2

                        @em("scalar")
                        def _(e, u=u, vslot=vslot, wmm=w_batch, wtv=wv(SM["tvd"], -1)):
                            e.wait_ge(SM["wqf"].h, wmm)
                            if wtv:
                                e.wait_ge(*wtv)
                            e.copy(vst[vslot][:], pp[u][:]).then_inc(SM["qev"].h, 1)
                        w_vst = SM["qev"].bump(1)

                        @em("tensor")
                        def _(e, vslot=vslot, w=w_vst, wvt=wv(SM["vtd"], 0)):
                            e.wait_ge(SM["qev"].h, w)
                            if wvt:
                                e.wait_ge(*wvt)
                            last = None
                            for j in range(4):
                                last = e.transpose(pT[:, j * 128:(j + 1) * 128], vst[vslot][:, j * 128:(j + 1) * 128], idn[:])
                            last.then_inc(SM["tvd"].h, 1)
                        w_tv = SM["tvd"].bump(1)

                        @em("scalar")
                        def _(e, vh=vh, t0=t0, w=w_tv):
                            e.wait_ge(SM["tvd"].h, w)
                            ins = None
                            for j in range(4):
                                base = (t0 // 128 + j) * (NKV * HD) + vh * HD
                                ins = e.copy(vtm[:, base: base + HD], pT[:, j * 128:(j + 1) * 128])
                            ins.then_inc(SM["vtd"].h, 1)
                        SM["vtd"].bump(1)

        @em("vector")
        def _(e, w=SM["nrd"].c):
            e.wait_ge(SM["nrd"].h, w)
            for k in range(NKV):
                for r in range(4):
                    e.tensor_copy(knr[0:1, 4 * k + r:4 * k + r + 1], kn[0:1, k:k + 1])
            e.drain()
            e.tensor_tensor(sm1[:], qn[:], knr[:], ALU.mult).then_inc(SM["nsq"].h, 1)
        w1 = SM["nsq"].bump(1)

        @em("scalar")
        def _(e, w=w1):
            e.wait_ge(SM["nsq"].h, w)
            e.activation(sm1[:], sm1[:], AF.Sqrt).then_inc(SM["nsq"].h, 1)
        w2 = SM["nsq"].bump(1)

        @em("vector")
        def _(e, w=w2):
            e.wait_ge(SM["nsq"].h, w)
            e.tensor_scalar_mul(sm1[:], sm1[:], -RSCL).then_inc(SM["nsq"].h, 1)
        w3 = SM["nsq"].bump(1)

        @em("tensor")
        def _(e, w=w3):
            e.wait_ge(SM["nsq"].h, w)
            e.matmul(pp[0][:, 0:NH], ones_rf[:], sm1[:], start=True, stop=True).then_inc(SM["nmm"].h, 1)
        w4 = SM["nmm"].bump(1)

        @em("scalar")
        def _(e, w=w4):
            e.wait_ge(SM["nmm"].h, w)
            e.copy(negmbc[:], pp[0][:, 0:NH]).then_inc(SM["nsq"].h, 1)
        SM["nsq"].bump(1)
        qes.close()
        phase_entry_waits()

    if "attn" in phases:
        src_fm3 = src_after_f1
        dst_is_out = "f2" not in phases
        aes = ExitStack()
        qt = aes.enter_context(nc.sbuf_tensor("qt", [128, NH * T], BF16))
        ao = aes.enter_context(nc.sbuf_tensor("ao", [128, NH * T], BF16))
        msk = aes.enter_context(nc.sbuf_tensor("msk_s", [128, 4 * T], BF16))
        woh = [aes.enter_context(nc.sbuf_tensor(f"woh{i}", [128, 16 * 640], BF16)) for i in range(2)]
        pex = [aes.enter_context(nc.sbuf_tensor(f"pex{i}", [128, T], BF16)) for i in range(2)]
        rec = srow[0]
        rbc = a_bc
        otm2 = aes.enter_context(nc.sbuf_tensor("otm2", [128, T], F32)) if dst_is_out else None
        if dst_is_out:
            ov2 = out.rearrange("(q j i) (c m) -> q i j c m", i=128, j=4, m=128)

        @em("sync")
        def _(e):
            e.dma_start(out=msk[:], in_=msk_in).then_inc(SM["cld"].h, 16)
        w_msk = SM["cld"].bump(16)
        wov4 = wo_in.rearrange("(cc ci p) col -> p cc ci col", p=128, ci=16)
        woB4 = woB.rearrange("(cc ci p) col -> p cc ci col", p=128, ci=16)
        whalf_done = [0, 0]

        for tq in range(NTT):
            t0 = tq * T
            conv = tq == 0

            @em("gpsimd")
            def _(e, t0=t0, waod=wv(SM["aod"], 0)):
                if waod:
                    e.wait_ge(*waod)
                e.dma_start(out=qt.ap().rearrange("p (h t) -> p h t", t=T),
                            in_=qscr.rearrange("h p t -> p h t")[:, :, t0:t0 + T]).then_inc(SM["qld"].h, 16)
            w_ql = SM["qld"].bump(16)

            nck = (t0 + T) // 128
            for h in range(NH):
                kvh = h // 4
                po = pp[2 + h % 2]
                psr = pS1 if h % 2 == 0 else pS2
                for ck in range(nck):
                    diag = ck >= t0 // 128

                    @em("tensor")
                    def _(e, h=h, ck=ck, kvh=kvh, w_ql=w_ql, wexp=wv(SM["exd"], -1),
                          wqev=wv(SM["qev"], 0) if (h == 0 and ck == 0) else None):
                        if h == 0 and ck == 0:
                            e.wait_ge(SM["qld"].h, w_ql)
                            if wqev:
                                e.wait_ge(*wqev)
                        if wexp:
                            e.wait_ge(*wexp)
                        e.matmul(pp[ck % 2][:], kfm[:, kvh * S + ck * 128: kvh * S + (ck + 1) * 128],
                                 qt[:, h * T:(h + 1) * T], start=True, stop=True).then_inc(SM["smm"].h, 1)
                    SM["smm"].bump(1)

                    @em("scalar")
                    def _(e, h=h, ck=ck, wm=SM["smm"].c, wpv=wv(SM["pvd"], -1), w_msk=w_msk):
                        e.wait_ge(SM["smm"].h, wm)
                        if wpv:
                            e.wait_ge(*wpv)
                        e.wait_ge(SM["cld"].h, w_msk)
                        e.activation(pex[ck % 2][:], pp[ck % 2][:], AF.Exp,
                                     bias=negmbc[:, h:h + 1], scale=RSCL).then_inc(SM["exd"].h, 1)
                    w_ex = SM["exd"].bump(1)

                    if diag:
                        dd = ck - t0 // 128

                        @em("vector")
                        def _(e, ck=ck, dd=dd, w=w_ex):
                            e.wait_ge(SM["exd"].h, w)
                            e.tensor_tensor(pex[ck % 2][:], pex[ck % 2][:], msk[:, dd * T:(dd + 1) * T],
                                            ALU.mult).then_inc(SM["mkd"].h, 1)
                        SM["mkd"].bump(1)

                    @em("tensor")
                    def _(e, ck=ck, nck=nck, kvh=kvh, diag=diag, wex=w_ex, wmk=SM["mkd"].c,
                          po=po, psr=psr,
                          waod=wv(SM["aod"], -1) if ck == 0 else None,
                          wrcd=wv(SM["rcd"], -3) if ck == 0 else None):
                        if diag:
                            e.wait_ge(SM["mkd"].h, wmk)
                        else:
                            e.wait_ge(SM["exd"].h, wex)
                        if ck == 0:
                            if waod:
                                e.wait_ge(*waod)
                            if wrcd:
                                e.wait_ge(*wrcd)
                        e.matmul(po[:], vtm[:, ck * (NKV * HD) + kvh * HD: ck * (NKV * HD) + (kvh + 1) * HD],
                                 pex[ck % 2][:], start=(ck == 0), stop=(ck == nck - 1))
                        e.matmul(psr[:], ones_cb[:], pex[ck % 2][:],
                                 start=(ck == 0), stop=(ck == nck - 1)).then_inc(SM["pvd"].h, 1)
                    SM["pvd"].bump(1)
                w_pv = SM["pvd"].c

                @em("vector")
                def _(e, w=w_pv, psr=psr, wrb=wv(SM["rcd"], -1)):
                    e.wait_ge(SM["pvd"].h, w)
                    if wrb:
                        e.wait_ge(*wrb)
                    e.reciprocal(rec[:], psr[0:1, 0:T]).then_inc(SM["rcd"].h, 1)
                w_rc = SM["rcd"].bump(1)

                @em("tensor")
                def _(e, w=w_rc, wnm=wv(SM["rcd"], -1)):
                    e.wait_ge(SM["rcd"].h, w)
                    if wnm:
                        e.wait_ge(*wnm)
                    e.matmul(pp[4][:, 0:T], ones_rf[:], rec[:], start=True, stop=True).then_inc(SM["nmm"].h, 1)
                w_bc2 = SM["nmm"].bump(1)

                @em("scalar")
                def _(e, w=w_bc2, wao=wv(SM["aod"], 0)):
                    e.wait_ge(SM["nmm"].h, w)
                    if wao:
                        e.wait_ge(*wao)
                    e.copy(rbc[:], pp[4][:, 0:T]).then_inc(SM["rcd"].h, 1)
                w_rbc = SM["rcd"].bump(1)

                @em("vector")
                def _(e, h=h, w=w_rbc, po=po):
                    e.wait_ge(SM["rcd"].h, w)
                    e.tensor_tensor(ao[:, h * T:(h + 1) * T], po[:], rbc[:], ALU.mult).then_inc(SM["aod"].h, 1)
                SM["aod"].bump(1)
            all_heads = SM["aod"].c

            for b in range(7):
                units = list(range(5 if b < 6 else 2))
                wo_base = SM["wqf"].c
                half_waits = []
                for hh in range(2):
                    par = (2 * b + hh) % 2
                    if conv:
                        @em("gpsimd")
                        def _(e, b=b, hh=hh, par=par, nu=len(units), wfree=whalf_done[par], wwb=wv(SM["wwb"], 0)):
                            if wfree >= 1:
                                e.wait_ge(SM["wqf"].h, wfree)
                            if wwb:
                                e.wait_ge(*wwb)
                            nc_ = 128 * nu
                            e.dma_start(out=woh[par].ap().rearrange("p (ci col) -> p ci col", col=640)[:, :, 0:nc_],
                                        in_=wov4[:, hh, :, b * 640:b * 640 + nc_]).then_inc(SM[f"wqlg{par}"].h, 16)
                        wldw = (SM[f"wqlg{par}"].h, SM[f"wqlg{par}"].bump(16))

                        @em("sync")
                        def _(e, b=b, hh=hh, par=par, nu=len(units), wldw=wldw):
                            e.wait_ge(*wldw)
                            nc_ = 128 * nu
                            e.dma_start(out=woB4[:, hh, :, b * 640:b * 640 + nc_],
                                        in_=woh[par].ap().rearrange("p (ci col) -> p ci col", col=640)[:, :, 0:nc_]).then_inc(SM["wwb"].h, 16)
                        SM["wwb"].bump(16)
                    else:
                        @em("sync")
                        def _(e, b=b, hh=hh, par=par, nu=len(units), wfree=whalf_done[par],
                              wwb=wv(SM["wwb"], 0) if (b == 0 and hh == 0 and tq == 1) else None):
                            if wwb:
                                e.wait_ge(*wwb)
                            if wfree >= 1:
                                e.wait_ge(SM["wqf"].h, wfree)
                            nc_ = 128 * nu
                            e.dma_start(out=woh[par].ap().rearrange("p (ci col) -> p ci col", col=640)[:, :, 0:nc_],
                                        in_=woB4[:, hh, :, b * 640:b * 640 + nc_]).then_inc(SM[f"wql{par}"].h, 16)
                        wldw = (SM[f"wql{par}"].h, SM[f"wql{par}"].bump(16))
                    half_waits.append(wldw)
                    whalf_done[par] = wo_base + 16 * (hh + 1)

                for c in range(NCH):
                    par = (2 * b + c // 16) % 2
                    @em("tensor")
                    def _(e, c=c, par=par, units=tuple(units),
                          wldw=half_waits[c // 16] if c % 16 == 0 else None, wah=all_heads,
                          wev=wv(SM["qev"], 0) if c == 0 else None,
                          wex=wv(SM["exd"], 0) if c == 0 else None,
                          wrc=wv(SM["rcd"], 0) if c == 0 else None,
                          wpo=wv(SM["ppd"], 0) if c == 0 else None):
                        if wldw:
                            e.wait_ge(*wldw)
                        if c == 0:
                            e.wait_ge(SM["aod"].h, wah)
                            if wpo:
                                e.wait_ge(*wpo)
                            if wev:
                                e.wait_ge(*wev)
                            if wex:
                                e.wait_ge(*wex)
                            if wrc:
                                e.wait_ge(*wrc)
                        last = None
                        for u in units:
                            last = e.matmul(pp[u][:], woh[par][:, (c % 16) * 640 + u * 128:(c % 16) * 640 + (u + 1) * 128],
                                            ao[:, c * T:(c + 1) * T], start=(c == 0), stop=(c == NCH - 1))
                        last.then_inc(SM["wqf"].h, 1)
                    SM["wqf"].bump(1)
                w_wmm = SM["wqf"].c

                for u in units:
                    ot = b * 5 + u

                    @em("scalar")
                    def _(e, ot=ot, t0=t0, wfree=wv(SM["qev"], -1)):
                        if wfree:
                            e.wait_ge(*wfree)
                        e.dma_start(out=xfs[ot % 2][:], in_=src_fm3[ot][:, t0:t0 + T]).then_inc(SM[f"xfl{ot % 2}"].h, 16)
                    wxl = (SM[f"xfl{ot % 2}"].h, SM[f"xfl{ot % 2}"].bump(16))

                    @em("vector")
                    def _(e, ot=ot, u=u, wmm=w_wmm, wxl=wxl, wowb=wv(SM["owb"], 0)):
                        e.wait_ge(SM["wqf"].h, wmm)
                        e.wait_ge(*wxl)
                        if wowb:
                            e.wait_ge(*wowb)
                        e.tensor_add(ogs[ot % 2][:], pp[u][:], xfs[ot % 2][:]).then_inc(SM["qev"].h, 1)
                    w_ev2 = SM["qev"].bump(1)

                    if not dst_is_out:
                        @em("gpsimd")
                        def _(e, ot=ot, t0=t0, w=w_ev2):
                            e.wait_ge(SM["qev"].h, w)
                            e.dma_start(out=xfmB[ot][:, t0:t0 + T], in_=ogs[ot % 2][:]).then_inc(SM["owb"].h, 16)
                        SM["owb"].bump(16)
                    else:
                        @em("tensor")
                        def _(e, ot=ot, u=u, w=w_ev2, wpo=wv(SM["ppd"], 0)):
                            e.wait_ge(SM["qev"].h, w)
                            if wpo:
                                e.wait_ge(*wpo)
                            last = None
                            for j in range(4):
                                last = e.transpose(pp[u][:, j * 128:(j + 1) * 128],
                                                   ogs[ot % 2][:, j * 128:(j + 1) * 128], idnf[:])
                            last.then_inc(SM["nmm"].h, 1)
                        w_tr = SM["nmm"].bump(1)

                        @em("scalar")
                        def _(e, u=u, w=w_tr, wwb=wv(SM["owb"], 0)):
                            e.wait_ge(SM["nmm"].h, w)
                            if wwb:
                                e.wait_ge(*wwb)
                            e.copy(otm2[:], pp[u][:]).then_inc(SM["ppd"].h, 1)
                        w_ot2 = SM["ppd"].bump(1)

                        @em("gpsimd")
                        def _(e, ot=ot, tq=tq, w=w_ot2):
                            e.wait_ge(SM["ppd"].h, w)
                            e.dma_start(out=ov2[tq, :, :, ot, :],
                                        in_=otm2.ap().rearrange("i (j m) -> i j m", m=128)).then_inc(SM["owb"].h, 16)
                        SM["owb"].bump(16)
        aes.close()
        kv_stack.close()
        phase_entry_waits()

    if "f2" in phases:
        src3 = xfmB if "attn" in phases else xfmI
        stk = ExitStack()
        ffn_phase("b", src3, None, out, "f1" not in phases, stk,
                  (SM["owb"].h, SM["owb"].c) if "attn" not in phases else None)
        stk.close()

    @em("sync")
    def _(e, w=SM["owb"].c):
        e.wait_ge(SM["owb"].h, w)

    with nc.Block() as blk:
        @blk.tensor
        def _(e):
            for f in prog["tensor"]:
                f(e)

        @blk.scalar
        def _(e):
            for f in prog["scalar"]:
                f(e)

        @blk.vector
        def _(e):
            for f in prog["vector"]:
                f(e)

        @blk.gpsimd
        def _(e):
            for f in prog["gpsimd"]:
                f(e)

        @blk.sync
        def _(e):
            for f in prog["sync"]:
                f(e)

    ES.close()
    return nc


def host_inputs(inp):
    wqkv = np.asarray(inp["L0_attn_w_qkv"], dtype=np.float32)
    wo = np.ascontiguousarray(np.asarray(inp["L0_attn_w_o"], dtype=np.float32))
    perm = np.concatenate([np.arange(0, 128, 2), np.arange(1, 128, 2)])
    wq = wqkv.copy()
    for h in range(NH):
        wq[:, h * 128:(h + 1) * 128] = wqkv[:, h * 128 + perm]
    for kh in range(NKV):
        b0 = D + kh * 128
        wq[:, b0:b0 + 128] = wqkv[:, b0 + perm]
    wq = np.ascontiguousarray(wq)
    gT = np.ascontiguousarray(np.asarray(inp["L0_ffn_gate_wq"], dtype=np.int32).T)
    uT = np.ascontiguousarray(np.asarray(inp["L0_ffn_up_wq"], dtype=np.int32).T)
    dT = np.ascontiguousarray(np.asarray(inp["L0_ffn_down_wq"], dtype=np.int32).T)
    gs = np.ascontiguousarray(np.asarray(inp["L0_ffn_gate_scale"], dtype=np.float32).reshape(NI, 128).T)
    us = np.ascontiguousarray(np.asarray(inp["L0_ffn_up_scale"], dtype=np.float32).reshape(NI, 128).T)
    ds = np.ascontiguousarray(np.asarray(inp["L0_ffn_down_scale"], dtype=np.float32).reshape(NCH, 128).T)
    idn = np.eye(128, dtype=ml_dtypes.bfloat16)
    idnf = np.eye(128, dtype=np.float32)
    j = np.arange(64, dtype=np.float64)
    invf = 1.0 / (10000.0 ** (2.0 * j / 128.0))
    t = np.arange(S, dtype=np.float64)
    ang = invf[:, None] * t[None, :]
    cosq = np.cos(ang).astype(np.float32)
    sinq = np.sin(ang).astype(np.float32)
    ii = np.arange(128)[:, None]
    jj = np.arange(T)[None, :]
    msk = np.concatenate([(ii <= jj - dd * 128) for dd in range(4)], axis=1).astype(ml_dtypes.bfloat16)
    return {"gT": gT, "uT": uT, "dT": dT, "wqkv": wq, "wo": wo,
            "gs": gs, "us": us, "ds": ds, "idn": idn, "idnf": idnf,
            "cosq": cosq, "sinq": sinq, "msk": msk}


def kernel(**inp):
    nc = build_nc()
    base = host_inputs(inp)
    xall = np.asarray(inp["L0_attn_x"], dtype=np.float32)
    in_maps = [{**base, "x": np.ascontiguousarray(xall[c])} for c in range(8)]
    res = run_bass_kernel_spmd(nc, in_maps, core_ids=list(range(8)))
    return np.stack([np.asarray(res.results[c]["out"], dtype=np.float32) for c in range(8)], axis=0)


# version 07
# speedup vs baseline: 1.2260x; 1.0434x over previous
import numpy as np
import ml_dtypes
import concourse.bass as bass
import concourse.mybir as mybir
from concourse.bass_utils import run_bass_kernel_spmd
from contextlib import ExitStack

F32 = mybir.dt.float32
BF16 = mybir.dt.bfloat16
I32 = mybir.dt.int32
AF = mybir.ActivationFunctionType
ALU = mybir.AluOpType
AX = mybir.AxisListType

S = 2048
D = 4096
II = 11008
NI = II // 128
NCH = D // 128
T = 512
NTT = S // T
NH = 32
NKV = 8
HD = 128
QKVO = D + 2 * NKV * HD
RSCL = float(1.0 / np.sqrt(128.0))
ENGS = ("tensor", "scalar", "vector", "gpsimd", "sync")

PHASES = ("f1", "qkv", "attn", "f2")


class Sem:
    __slots__ = ("h", "c")

    def __init__(self, es, nc, name):
        self.h = es.enter_context(nc.semaphore(name))
        self.c = 0

    def bump(self, n=1):
        self.c += n
        return self.c


def build_nc(phases=PHASES):
    nc = bass.Bass("TRN2", target_bir_lowering=False)
    x_in = nc.dram_tensor("x", [S, D], F32, kind="ExternalInput").ap()
    gT = nc.dram_tensor("gT", [D, II], I32, kind="ExternalInput").ap()
    uT = nc.dram_tensor("uT", [D, II], I32, kind="ExternalInput").ap()
    dT = nc.dram_tensor("dT", [II, D], I32, kind="ExternalInput").ap()
    wqkv = nc.dram_tensor("wqkv", [D, QKVO], F32, kind="ExternalInput").ap()
    wo_in = nc.dram_tensor("wo", [D, D], F32, kind="ExternalInput").ap()
    gs_in = nc.dram_tensor("gs", [128, NI], F32, kind="ExternalInput").ap()
    us_in = nc.dram_tensor("us", [128, NI], F32, kind="ExternalInput").ap()
    ds_in = nc.dram_tensor("ds", [128, NCH], F32, kind="ExternalInput").ap()
    idn_in = nc.dram_tensor("idn", [128, 128], BF16, kind="ExternalInput").ap()
    idnf_in = nc.dram_tensor("idnf", [128, 128], F32, kind="ExternalInput").ap()
    cos_in = nc.dram_tensor("cosq", [64, S], F32, kind="ExternalInput").ap()
    sin_in = nc.dram_tensor("sinq", [64, S], F32, kind="ExternalInput").ap()
    msk_in = nc.dram_tensor("msk", [128, 4 * T], BF16, kind="ExternalInput").ap()
    out = nc.dram_tensor("out", [S, D], F32, kind="ExternalOutput").ap()

    guB = nc.dram_tensor("guB", [NI, 2, 128, D], BF16, kind="Internal").ap()
    dB = nc.dram_tensor("dB", [NCH, 128, II], BF16, kind="Internal").ap()
    wqB = nc.dram_tensor("wqB", [D, QKVO], BF16, kind="Internal").ap()
    woB = nc.dram_tensor("woB", [D, D], BF16, kind="Internal").ap()
    xfmI = nc.dram_tensor("xfmI", [NCH, 128, S], F32, kind="Internal").ap()
    xfmA = nc.dram_tensor("xfmA", [NCH, 128, S], F32, kind="Internal").ap()
    xfmB = nc.dram_tensor("xfmB", [NCH, 128, S], F32, kind="Internal").ap()
    qscr = nc.dram_tensor("qscr", [NH, 128, S], BF16, kind="Internal").ap()

    ES = ExitStack()
    E = ES.enter_context

    idn = E(nc.sbuf_tensor("idn_s", [128, 128], BF16))
    idnf = E(nc.sbuf_tensor("idnf_s", [128, 128], F32))
    ones_cb = E(nc.sbuf_tensor("ones_cb", [128, 1], BF16))
    ones_rf = E(nc.sbuf_tensor("ones_rf", [1, 128], F32))
    gsus = E(nc.sbuf_tensor("gsus", [128, NI], F32))
    us_s = E(nc.sbuf_tensor("us_s", [128, NI], F32))
    dsch = E(nc.sbuf_tensor("dsch", [128, NCH], F32))
    negmbc = E(nc.sbuf_tensor("negmbc", [128, NH], F32))
    qn = E(nc.sbuf_tensor("qn", [1, NH], F32))
    kn = E(nc.sbuf_tensor("kn", [1, NKV], F32))
    knr = E(nc.sbuf_tensor("knr", [1, NH], F32))
    sm1 = E(nc.sbuf_tensor("sm1", [1, NH], F32))
    srow = [E(nc.sbuf_tensor(f"srow{i}", [1, T], F32)) for i in range(3)]
    a_bc = E(nc.sbuf_tensor("a_bc", [128, T], F32))
    b_bc = E(nc.sbuf_tensor("b_bc", [128, T], F32))
    cnv = [E(nc.sbuf_tensor(f"cnv{i}", [128, T], BF16)) for i in range(4)]
    lnt = E(nc.sbuf_tensor("lnt", [128, T], F32))
    xfs = [E(nc.sbuf_tensor(f"xf{i}", [128, T], F32)) for i in range(2)]
    sfs = [E(nc.sbuf_tensor(f"sf{i}", [128, T], F32)) for i in range(2)]
    ogs = [E(nc.sbuf_tensor(f"og{i}", [128, T], F32)) for i in range(2)]

    pp = [E(nc.psum_tensor(f"pp{i}", [128, 512], F32)) for i in range(5)]
    pS1 = E(nc.psum_tensor("pS1", [1, 512], F32))
    pS2 = E(nc.psum_tensor("pS2", [1, 512], F32))
    pT = E(nc.psum_tensor("pT", [128, 512], BF16))

    sem_names = ["cld", "wgl0", "wgl1", "wglg0", "wglg1", "wdlg0", "wdlg1", "wqlg0", "wqlg1",
                 "l1d", "uev", "gud", "wwb", "wdl0", "wdl1", "wdf",
                 "dev", "xfl0", "xfl1", "sfl0", "sfl1", "xnb", "owb", "nsq", "nmm", "nrd",
                 "wql0", "wql1", "wqf", "qev", "qwb0", "qwb1", "tvd", "vtd",
                 "qld", "smm", "exd", "mkd", "pvd", "aod", "rcd", "std", "ppd"]
    SM = {n: Sem(ES, nc, n) for n in sem_names}

    prog = {e: [] for e in ENGS}

    def em(eng):
        def deco(f):
            prog[eng].append(f)
        return deco

    def wv(sem, delta=0):
        v = sem.c + delta
        return (sem.h, v) if v >= 1 else None

    def wlist(*sems):
        return tuple(x for x in (wv(s) for s in sems) if x)

    def phase_entry_waits():
        caps = [(s.h, s.c) for s in SM.values() if s.c >= 1]
        for eng in ENGS:
            @em(eng)
            def _(e, caps=tuple(caps)):
                for h, v in caps:
                    e.wait_ge(h, v)

    @em("sync")
    def _(e):
        e.dma_start(out=idn[:], in_=idn_in).then_inc(SM["cld"].h, 16)
        e.dma_start(out=idnf[:], in_=idnf_in).then_inc(SM["cld"].h, 16)
        e.dma_start(out=gsus[:], in_=gs_in).then_inc(SM["cld"].h, 16)
        e.dma_start(out=us_s[:], in_=us_in).then_inc(SM["cld"].h, 16)
        e.dma_start(out=dsch[:], in_=ds_in).then_inc(SM["cld"].h, 16)
    w_cld0 = SM["cld"].bump(5 * 16)

    @em("vector")
    def _(e, w=w_cld0):
        e.memset(ones_cb[:], 1.0)
        e.memset(ones_rf[:], 1.0)
        e.memset(qn[:], 0.0)
        e.memset(kn[:], 0.0)
        e.wait_ge(SM["cld"].h, w)
        e.tensor_mul(gsus[:], gsus[:], us_s[:])
        e.tensor_scalar_mul(dsch[:], dsch[:], 0.5).then_inc(SM["std"].h, 1)
    SM["std"].bump(1)
    w_consts = SM["std"].c

    def emit_prepass():
        xv = x_in.rearrange("(q j i) (c m) -> i j q c m", i=128, j=4, m=128)
        k = 0
        for c in range(NCH):
            for q in range(NTT):
                slot = k % 2

                @em("sync")
                def _(e, c=c, q=q, slot=slot, wfree=wv(SM["nmm"], -1)):
                    if wfree:
                        e.wait_ge(*wfree)
                    e.dma_start(out=xfs[slot].ap().rearrange("i (j m) -> i j m", m=128),
                                in_=xv[:, :, q, c, :]).then_inc(SM[f"xfl{slot}"].h, 16)
                w_ld = (SM[f"xfl{slot}"].h, SM[f"xfl{slot}"].bump(16))

                @em("tensor")
                def _(e, slot=slot, w=w_ld, wst=w_consts, wev=wv(SM["nsq"], -1)):
                    e.wait_ge(*w)
                    e.wait_ge(SM["std"].h, wst)
                    if wev:
                        e.wait_ge(*wev)
                    last = None
                    for j in range(4):
                        last = e.transpose(pp[slot][:, j * 128:(j + 1) * 128],
                                           xfs[slot][:, j * 128:(j + 1) * 128], idnf[:])
                    last.then_inc(SM["nmm"].h, 1)
                w_tr = SM["nmm"].bump(1)

                @em("scalar")
                def _(e, slot=slot, w=w_tr, wwb=wv(SM["owb"], 0)):
                    e.wait_ge(SM["nmm"].h, w)
                    if wwb:
                        e.wait_ge(*wwb)
                    e.copy(ogs[slot][:], pp[slot][:]).then_inc(SM["nsq"].h, 1)
                w_ev = SM["nsq"].bump(1)

                @em("gpsimd")
                def _(e, c=c, q=q, slot=slot, w=w_ev):
                    e.wait_ge(SM["nsq"].h, w)
                    e.dma_start(out=xfmI[c][:, q * T:(q + 1) * T], in_=ogs[slot][:]).then_inc(SM["owb"].h, 16)
                SM["owb"].bump(16)
                k += 1
        return SM["owb"].c

    def ln_tile(src_fm3, tt, xn_guard_sem, tens_guards, src_wait, xn):
        t0 = tt * T
        for c in range(NCH):
            @em("scalar")
            def _(e, c=c, t0=t0, wfree=wv(SM["nmm"], -1), sw=src_wait if c == 0 else None):
                if sw:
                    e.wait_ge(*sw)
                if wfree:
                    e.wait_ge(*wfree)
                e.dma_start(out=sfs[c % 2][:], in_=src_fm3[c][:, t0:t0 + T]).then_inc(SM[f"sfl{c % 2}"].h, 16)
            w_ld = (SM[f"sfl{c % 2}"].h, SM[f"sfl{c % 2}"].bump(16))

            @em("scalar")
            def _(e, c=c, w=w_ld, wmm=wv(SM["nmm"], -1)):
                e.wait_ge(*w)
                if wmm:
                    e.wait_ge(*wmm)
                e.copy(cnv[2 * (c % 2)][:], sfs[c % 2][:])
                e.activation(cnv[2 * (c % 2) + 1][:], sfs[c % 2][:], AF.Square).then_inc(SM["nsq"].h, 1)
            w_cv = SM["nsq"].bump(1)

            @em("tensor")
            def _(e, c=c, w=w_cv, tg=tens_guards if c == 0 else ()):
                e.wait_ge(SM["nsq"].h, w)
                for hvv in tg:
                    e.wait_ge(*hvv)
                e.matmul(pS1[:], ones_cb[:], cnv[2 * (c % 2)][:], start=(c == 0), stop=(c == NCH - 1))
                e.matmul(pS2[:], ones_cb[:], cnv[2 * (c % 2) + 1][:], start=(c == 0), stop=(c == NCH - 1)).then_inc(SM["nmm"].h, 1)
            SM["nmm"].bump(1)
        w_stat = SM["nmm"].c

        @em("vector")
        def _(e, w=w_stat):
            e.wait_ge(SM["nmm"].h, w)
            m, r, t2 = srow
            e.tensor_scalar_mul(m[:], pS1[0:1, 0:T], 1.0 / D)
            e.tensor_scalar_mul(r[:], pS2[0:1, 0:T], 1.0 / D)
            e.drain()
            e.tensor_tensor(t2[:], m[:], m[:], ALU.mult)
            e.drain()
            e.tensor_sub(r[:], r[:], t2[:])
            e.drain()
            e.tensor_scalar_add(r[:], r[:], 1e-5)
            e.drain()
            e.tensor_copy(t2[:], r[:])
            e.drain()
            e.reciprocal(r[:], r[:]).then_inc(SM["nsq"].h, 1)
        w1 = SM["nsq"].bump(1)

        @em("scalar")
        def _(e, w=w1):
            e.wait_ge(SM["nsq"].h, w)
            e.activation(srow[1][:], srow[1][:], AF.Sqrt).then_inc(SM["nsq"].h, 1)
        w2 = SM["nsq"].bump(1)

        @em("vector")
        def _(e, w=w2):
            e.wait_ge(SM["nsq"].h, w)
            m, r, t2 = srow
            e.tensor_tensor(lnt[0:1, :], r[:], r[:], ALU.mult)
            e.drain()
            e.tensor_tensor(lnt[0:1, :], lnt[0:1, :], t2[:], ALU.mult)
            e.drain()
            e.tensor_scalar(lnt[0:1, :], lnt[0:1, :], -0.5, 1.5, ALU.mult, ALU.add)
            e.drain()
            e.tensor_tensor(r[:], r[:], lnt[0:1, :], ALU.mult)
            e.drain()
            e.scalar_tensor_tensor(m[:], m[:], -1.0, r[:], ALU.mult, ALU.mult).then_inc(SM["nsq"].h, 1)
        w3 = SM["nsq"].bump(1)

        @em("tensor")
        def _(e, w=w3, wab=wv(SM["xnb"], 0), wpo=wv(SM["ppd"], 0)):
            e.wait_ge(SM["nsq"].h, w)
            if wab:
                e.wait_ge(*wab)
            if wpo:
                e.wait_ge(*wpo)
            e.matmul(pp[0][:, 0:T], ones_rf[:], srow[1][:], start=True, stop=True)
            e.matmul(pp[1][:, 0:T], ones_rf[:], srow[0][:], start=True, stop=True).then_inc(SM["nmm"].h, 1)
        w4 = SM["nmm"].bump(1)

        @em("scalar")
        def _(e, w=w4):
            e.wait_ge(SM["nmm"].h, w)
            e.copy(a_bc[:], pp[0][:, 0:T])
            e.copy(b_bc[:], pp[1][:, 0:T]).then_inc(SM["nsq"].h, 1)
        w_abc = SM["nsq"].bump(1)

        xn_w = (xn_guard_sem.h, xn_guard_sem.c) if xn_guard_sem.c >= 1 else None
        for c2 in range(NCH // 2):
            for c in (2 * c2, 2 * c2 + 1):
                @em("scalar")
                def _(e, c=c, t0=t0, wfree=wv(SM["xnb"], -1 if c % 2 == 0 else 0), wab=w_abc,
                      wdev=wlist(SM["dev"], SM["qev"], SM["nmm"]) if c < 2 else ()):
                    e.wait_ge(SM["nsq"].h, wab)
                    for hvv in wdev:
                        e.wait_ge(*hvv)
                    if wfree:
                        e.wait_ge(*wfree)
                    e.dma_start(out=xfs[c % 2][:], in_=src_fm3[c][:, t0:t0 + T]).then_inc(SM[f"xfl{c % 2}"].h, 16)
                SM[f"xfl{c % 2}"].bump(16)
            wld = ((SM["xfl0"].h, SM["xfl0"].c), (SM["xfl1"].h, SM["xfl1"].c))

            @em("vector")
            def _(e, c2=c2, wld=wld, wab=w_abc, xw=xn_w if c2 == 0 else None):
                e.wait_ge(*wld[0])
                e.wait_ge(*wld[1])
                e.wait_ge(SM["nsq"].h, wab)
                if xw:
                    e.wait_ge(*xw)
                ca, cb = 2 * c2, 2 * c2 + 1
                e.tensor_tensor(lnt[:], xfs[0][:], a_bc[:], ALU.mult)
                e.tensor_tensor(ogs[0][:], xfs[1][:], a_bc[:], ALU.mult)
                e.drain()
                e.tensor_add(xn[:, ca * T:(ca + 1) * T], lnt[:], b_bc[:]).then_inc(SM["xnb"].h, 1)
                e.tensor_add(xn[:, cb * T:(cb + 1) * T], ogs[0][:], b_bc[:]).then_inc(SM["xnb"].h, 1)
            SM["xnb"].bump(2)
        return SM["xnb"].c

    def ffn_phase(tag, src_fm3, dst_fm3, dst_tm, first, stk, src_wait):
        xn = stk.enter_context(nc.sbuf_tensor(f"xn{tag}", [128, NCH * T], BF16))
        gu = stk.enter_context(nc.sbuf_tensor(f"gu_{tag}", [128, NI * T], BF16))
        wgu = [stk.enter_context(nc.sbuf_tensor(f"wgu{tag}{i}", [128, 2 * D], BF16)) for i in range(2)]
        wd = [stk.enter_context(nc.sbuf_tensor(f"wd{tag}{i}", [128, II // 2], BF16)) for i in range(2)]
        usb = [stk.enter_context(nc.sbuf_tensor(f"us{tag}{i}", [128, T], BF16)) for i in range(2)]
        otm = stk.enter_context(nc.sbuf_tensor(f"otm{tag}", [128, T], F32)) if dst_tm is not None else None
        gv = gT.rearrange("(c p) (t m) -> p c t m", p=128, m=128)
        uv = uT.rearrange("(c p) (t m) -> p c t m", p=128, m=128)
        dv = dT.rearrange("(j p) (h m) -> p j h m", p=128, m=128)
        if dst_tm is not None:
            ov = dst_tm.rearrange("(q j i) (c m) -> q i j c m", i=128, j=4, m=128)

        for tt in range(NTT):
            t0 = tt * T
            conv = first and tt == 0
            xn_tgt = ln_tile(src_fm3, tt, SM["l1d"],
                             wlist(SM["nrd"], SM["rcd"], SM["gud"], SM["uev"], SM["qev"], SM["exd"], SM["dev"]),
                             src_wait if tt == 0 else None, xn)

            for tau in range(NI):
                slot = tau % 2
                if conv:
                    @em("gpsimd")
                    def _(e, tau=tau, slot=slot, wfree=wv(SM["l1d"], -1), wwb=wv(SM["wwb"], 0)):
                        if wfree:
                            e.wait_ge(*wfree)
                        if wwb:
                            e.wait_ge(*wwb)
                        e.dma_start(out=wgu[slot].ap().rearrange("p (c m) -> p c m", m=128)[:, 0:NCH, :],
                                    in_=gv[:, :, tau, :]).then_inc(SM[f"wglg{slot}"].h, 16)
                        e.dma_start(out=wgu[slot].ap().rearrange("p (c m) -> p c m", m=128)[:, NCH:2 * NCH, :],
                                    in_=uv[:, :, tau, :]).then_inc(SM[f"wglg{slot}"].h, 16)
                    wld = (SM[f"wglg{slot}"].h, SM[f"wglg{slot}"].bump(32))

                    @em("sync")
                    def _(e, tau=tau, slot=slot, wld=wld):
                        e.wait_ge(*wld)
                        e.dma_start(out=guB[tau].rearrange("t p f -> p t f"),
                                    in_=wgu[slot].ap().rearrange("p (t f) -> p t f", f=D)).then_inc(SM["wwb"].h, 16)
                    SM["wwb"].bump(16)
                else:
                    @em("sync")
                    def _(e, tau=tau, slot=slot, wfree=wv(SM["l1d"], -1),
                          wwb=wv(SM["wwb"], 0) if (tau == 0 and first and tt == 1) else None):
                        if wwb:
                            e.wait_ge(*wwb)
                        if wfree:
                            e.wait_ge(*wfree)
                        e.dma_start(out=wgu[slot].ap().rearrange("p (t f) -> p t f", f=D),
                                    in_=guB[tau].rearrange("t p f -> p t f")).then_inc(SM[f"wgl{slot}"].h, 16)
                    wld = (SM[f"wgl{slot}"].h, SM[f"wgl{slot}"].bump(16))

                @em("tensor")
                def _(e, tau=tau, slot=slot, wld=wld, xw=xn_tgt,
                      wgud=wv(SM["gud"], -1), wue=wv(SM["uev"], -1),
                      wpo=wv(SM["ppd"], 0) if tau < 2 else None):
                    e.wait_ge(*wld)
                    if wpo:
                        e.wait_ge(*wpo)
                    if wgud:
                        e.wait_ge(*wgud)
                    if wue:
                        e.wait_ge(*wue)
                    for c in range(NCH):
                        if tau == 0:
                            e.wait_ge(SM["xnb"].h, xw - NCH + c + 1)
                        e.matmul(pp[slot][:], wgu[slot][:, c * 128:(c + 1) * 128], xn[:, c * T:(c + 1) * T],
                                 start=(c == 0), stop=(c == NCH - 1))
                    for c in range(NCH):
                        ins = e.matmul(pp[2 + slot][:], wgu[slot][:, D + c * 128:D + (c + 1) * 128], xn[:, c * T:(c + 1) * T],
                                       start=(c == 0), stop=(c == NCH - 1))
                    ins.then_inc(SM["l1d"].h, 1)
                SM["l1d"].bump(1)

                @em("scalar")
                def _(e, slot=slot, w=SM["l1d"].c):
                    e.wait_ge(SM["l1d"].h, w)
                    e.copy(usb[slot][:], pp[2 + slot][:]).then_inc(SM["uev"].h, 1)
                SM["uev"].bump(1)

                @em("vector")
                def _(e, tau=tau, slot=slot, w=SM["uev"].c):
                    e.wait_ge(SM["uev"].h, w)
                    e.scalar_tensor_tensor(gu[:, tau * T:(tau + 1) * T], pp[slot][:], gsus[:, tau:tau + 1],
                                           usb[slot][:], ALU.mult, ALU.mult).then_inc(SM["gud"].h, 1)
                SM["gud"].bump(1)
            l1_all = SM["gud"].c

            for ht in range(NCH):
                for half in range(2):
                    hslot = (2 * ht + half) % 2
                    if conv:
                        @em("gpsimd")
                        def _(e, ht=ht, half=half, hslot=hslot, wfree=wv(SM["wdf"], -1), wwb=wv(SM["wwb"], 0)):
                            if wfree:
                                e.wait_ge(*wfree)
                            if wwb:
                                e.wait_ge(*wwb)
                            e.dma_start(out=wd[hslot].ap().rearrange("p (j m) -> p j m", m=128),
                                        in_=dv[:, half * 43:(half + 1) * 43, ht, :]).then_inc(SM[f"wdlg{hslot}"].h, 16)
                        wldd = (SM[f"wdlg{hslot}"].h, SM[f"wdlg{hslot}"].bump(16))

                        @em("sync")
                        def _(e, ht=ht, half=half, hslot=hslot, wldd=wldd):
                            e.wait_ge(*wldd)
                            e.dma_start(out=dB[ht].rearrange("p (hh x) -> p hh x", hh=2)[:, half, :],
                                        in_=wd[hslot][:]).then_inc(SM["wwb"].h, 16)
                        SM["wwb"].bump(16)
                    else:
                        @em("sync")
                        def _(e, ht=ht, half=half, hslot=hslot, wfree=wv(SM["wdf"], -1),
                              wwb=wv(SM["wwb"], 0) if (ht == 0 and half == 0 and first and tt == 1) else None):
                            if wwb:
                                e.wait_ge(*wwb)
                            if wfree:
                                e.wait_ge(*wfree)
                            e.dma_start(out=wd[hslot][:],
                                        in_=dB[ht].rearrange("p (hh x) -> p hh x", hh=2)[:, half, :]).then_inc(SM[f"wdl{hslot}"].h, 16)
                        wldd = (SM[f"wdl{hslot}"].h, SM[f"wdl{hslot}"].bump(16))

                    @em("tensor")
                    def _(e, ht=ht, half=half, hslot=hslot, wldd=wldd, wall=l1_all,
                          wde=wv(SM["dev"], 0) if half == 0 else None):
                        e.wait_ge(*wldd)
                        if ht == 0 and half == 0:
                            e.wait_ge(SM["gud"].h, wall)
                        if wde:
                            e.wait_ge(*wde)
                        last = None
                        for jj in range(43):
                            j = half * 43 + jj
                            last = e.matmul(pp[4][:], wd[hslot][:, jj * 128:(jj + 1) * 128],
                                            gu[:, j * T:(j + 1) * T], start=(j == 0), stop=(j == NI - 1))
                        last.then_inc(SM["wdf"].h, 1)
                    SM["wdf"].bump(1)
                w_dmm = SM["wdf"].c

                @em("scalar")
                def _(e, ht=ht, t0=t0, wfree=wv(SM["dev"], -1),
                      wxn=wv(SM["xnb"], 0) if ht == 0 else None):
                    if wfree:
                        e.wait_ge(*wfree)
                    if wxn:
                        e.wait_ge(*wxn)
                    e.dma_start(out=xfs[ht % 2][:], in_=src_fm3[ht][:, t0:t0 + T]).then_inc(SM[f"xfl{ht % 2}"].h, 16)
                wxl = (SM[f"xfl{ht % 2}"].h, SM[f"xfl{ht % 2}"].bump(16))

                @em("vector")
                def _(e, ht=ht, wmm=w_dmm, wxl=wxl, wowb=wv(SM["owb"], 0)):
                    e.wait_ge(SM["wdf"].h, wmm)
                    e.wait_ge(*wxl)
                    if wowb:
                        e.wait_ge(*wowb)
                    e.scalar_tensor_tensor(ogs[ht % 2][:], pp[4][:], dsch[:, ht:ht + 1],
                                           xfs[ht % 2][:], ALU.mult, ALU.add).then_inc(SM["dev"].h, 1)
                SM["dev"].bump(1)
                w_ev = SM["dev"].c

                if dst_fm3 is not None:
                    @em("gpsimd")
                    def _(e, ht=ht, t0=t0, w=w_ev):
                        e.wait_ge(SM["dev"].h, w)
                        e.dma_start(out=dst_fm3[ht][:, t0:t0 + T], in_=ogs[ht % 2][:]).then_inc(SM["owb"].h, 16)
                    SM["owb"].bump(16)
                else:
                    @em("tensor")
                    def _(e, ht=ht, w=w_ev, wpo=wv(SM["ppd"], 0)):
                        e.wait_ge(SM["dev"].h, w)
                        if wpo:
                            e.wait_ge(*wpo)
                        last = None
                        for j in range(4):
                            last = e.transpose(pp[ht % 2][:, j * 128:(j + 1) * 128],
                                               ogs[ht % 2][:, j * 128:(j + 1) * 128], idnf[:])
                        last.then_inc(SM["nmm"].h, 1)
                    w_tr = SM["nmm"].bump(1)

                    @em("scalar")
                    def _(e, ht=ht, w=w_tr, wwb=wv(SM["owb"], 0)):
                        e.wait_ge(SM["nmm"].h, w)
                        if wwb:
                            e.wait_ge(*wwb)
                        e.copy(otm[:], pp[ht % 2][:]).then_inc(SM["ppd"].h, 1)
                    w_ot = SM["ppd"].bump(1)

                    @em("gpsimd")
                    def _(e, ht=ht, tt=tt, w=w_ot):
                        e.wait_ge(SM["ppd"].h, w)
                        e.dma_start(out=ov[tt, :, :, ht, :],
                                    in_=otm.ap().rearrange("i (j m) -> i j m", m=128)).then_inc(SM["owb"].h, 16)
                    SM["owb"].bump(16)

    w_pre = emit_prepass()
    if "f1" in phases:
        stk = ExitStack()
        last_f1 = len(phases) == 1
        ffn_phase("a", xfmI, None if last_f1 else xfmA, out if last_f1 else None,
                  True, stk, (SM["owb"].h, w_pre))
        stk.close()
        phase_entry_waits()
        src_after_f1 = xfmA
    else:
        src_after_f1 = xfmI

    kv_stack = ExitStack()
    if "qkv" in phases:
        src_fm3 = src_after_f1
        kfm = kv_stack.enter_context(nc.sbuf_tensor("kfm", [128, NKV * S], BF16))
        vtm = kv_stack.enter_context(nc.sbuf_tensor("vtm", [128, (S // 128) * NKV * HD], BF16))
        qes = ExitStack()
        xn = qes.enter_context(nc.sbuf_tensor("xnq", [128, NCH * T], BF16))
        cosq = qes.enter_context(nc.sbuf_tensor("cosq_s", [64, S], F32))
        sinq = qes.enter_context(nc.sbuf_tensor("sinq_s", [64, S], F32))
        wqh = [qes.enter_context(nc.sbuf_tensor(f"wqh{i}", [128, 16 * 640], BF16)) for i in range(2)]
        qsb = [qes.enter_context(nc.sbuf_tensor(f"qsb{i}", [128, T], BF16)) for i in range(2)]
        rsc = [qes.enter_context(nc.sbuf_tensor(f"rsc{i}", [64, T], F32)) for i in range(4)]
        vst = [qes.enter_context(nc.sbuf_tensor(f"vst{i}", [128, T], BF16)) for i in range(2)]
        sqsc = qes.enter_context(nc.sbuf_tensor("sqsc", [128, T], BF16))
        nrm = qes.enter_context(nc.sbuf_tensor("nrm", [1, 1], F32))

        @em("sync")
        def _(e):
            e.dma_start(out=cosq[:], in_=cos_in).then_inc(SM["cld"].h, 16)
            e.dma_start(out=sinq[:], in_=sin_in).then_inc(SM["cld"].h, 16)
        w_rope = SM["cld"].bump(32)

        wqv4 = wqkv.rearrange("(cc ci p) col -> p cc ci col", p=128, ci=16)
        wqB4 = wqB.rearrange("(cc ci p) col -> p cc ci col", p=128, ci=16)
        qhalf_done = [0, 0]
        nqu = 0
        q_sq = []
        for tt in range(NTT):
            t0 = tt * T
            conv = tt == 0
            xn_tgt = ln_tile(src_fm3, tt, SM["wqf"],
                             wlist(SM["nrd"], SM["rcd"], SM["qev"], SM["exd"]),
                             (SM["owb"].h, SM["owb"].c) if (tt == 0 and "f1" not in phases) else None, xn)

            for b in range(10):
                nu = 5 if b < 9 else 3
                wq_base = SM["wqf"].c
                half_waits = []
                for hh in range(2):
                    par = (2 * b + hh) % 2
                    if conv:
                        @em("gpsimd")
                        def _(e, b=b, hh=hh, par=par, nu=nu, wfree=qhalf_done[par], wwb=wv(SM["wwb"], 0)):
                            if wfree >= 1:
                                e.wait_ge(SM["wqf"].h, wfree)
                            if wwb:
                                e.wait_ge(*wwb)
                            nc_ = 128 * nu
                            e.dma_start(out=wqh[par].ap().rearrange("p (ci col) -> p ci col", col=640)[:, :, 0:nc_],
                                        in_=wqv4[:, hh, :, b * 640:b * 640 + nc_]).then_inc(SM[f"wqlg{par}"].h, 16)
                        wldq = (SM[f"wqlg{par}"].h, SM[f"wqlg{par}"].bump(16))

                        @em("sync")
                        def _(e, b=b, hh=hh, par=par, nu=nu, wldq=wldq):
                            e.wait_ge(*wldq)
                            nc_ = 128 * nu
                            e.dma_start(out=wqB4[:, hh, :, b * 640:b * 640 + nc_],
                                        in_=wqh[par].ap().rearrange("p (ci col) -> p ci col", col=640)[:, :, 0:nc_]).then_inc(SM["wwb"].h, 16)
                        SM["wwb"].bump(16)
                    else:
                        @em("sync")
                        def _(e, b=b, hh=hh, par=par, nu=nu, wfree=qhalf_done[par],
                              wwb=wv(SM["wwb"], 0) if (b == 0 and hh == 0 and tt == 1) else None):
                            if wwb:
                                e.wait_ge(*wwb)
                            if wfree >= 1:
                                e.wait_ge(SM["wqf"].h, wfree)
                            nc_ = 128 * nu
                            e.dma_start(out=wqh[par].ap().rearrange("p (ci col) -> p ci col", col=640)[:, :, 0:nc_],
                                        in_=wqB4[:, hh, :, b * 640:b * 640 + nc_]).then_inc(SM[f"wql{par}"].h, 16)
                        wldq = (SM[f"wql{par}"].h, SM[f"wql{par}"].bump(16))
                    half_waits.append(wldq)
                    qhalf_done[par] = wq_base + 16 * (hh + 1)

                for c in range(NCH):
                    par = (2 * b + c // 16) % 2
                    @em("tensor")
                    def _(e, b=b, c=c, nu=nu, par=par,
                          wldq=half_waits[c // 16] if c % 16 == 0 else None,
                          xw=xn_tgt, wev=wv(SM["qev"], 0) if c == 0 else None):
                        if wldq:
                            e.wait_ge(*wldq)
                        if b == 0:
                            e.wait_ge(SM["xnb"].h, xw - NCH + c + 1)
                        if c == 0:
                            if wev:
                                e.wait_ge(*wev)
                        last = None
                        for u in range(nu):
                            last = e.matmul(pp[u][:], wqh[par][:, (c % 16) * 640 + u * 128:(c % 16) * 640 + (u + 1) * 128],
                                            xn[:, c * T:(c + 1) * T], start=(c == 0), stop=(c == NCH - 1))
                        last.then_inc(SM["wqf"].h, 1)
                    SM["wqf"].bump(1)
                w_batch = SM["wqf"].c

                for u in range(nu):
                    ot = b * 5 + u
                    if ot < NH:
                        slot = nqu % 2

                        @em("vector")
                        def _(e, u=u, slot=slot, t0=t0, wmm=w_batch, wrope=w_rope,
                              wqwb=wv(SM[f"qwb{nqu % 2}"], 0)):
                            e.wait_ge(SM["wqf"].h, wmm)
                            e.wait_ge(SM["cld"].h, wrope)
                            if wqwb:
                                e.wait_ge(*wqwb)
                            e.drain()
                            psrc = pp[u]
                            cs = cosq[:, t0:t0 + T]
                            sn = sinq[:, t0:t0 + T]
                            pe = psrc[0:64, :]
                            po = psrc[64:128, :]
                            e.tensor_tensor(rsc[0][:], pe, cs, ALU.mult)
                            e.tensor_tensor(rsc[1][:], po, sn, ALU.mult)
                            e.tensor_tensor(rsc[2][:], pe, sn, ALU.mult)
                            e.tensor_tensor(rsc[3][:], po, cs, ALU.mult)
                            e.drain()
                            e.tensor_sub(qsb[slot][0:64, :], rsc[0][:], rsc[1][:])
                            e.tensor_add(qsb[slot][64:128, :], rsc[2][:], rsc[3][:]).then_inc(SM["qev"].h, 1)
                        w_rp = SM["qev"].bump(1)

                        @em("gpsimd")
                        def _(e, ot=ot, slot=slot, t0=t0, w=w_rp):
                            e.wait_ge(SM["qev"].h, w)
                            e.dma_start(out=qscr[ot][:, t0:t0 + T], in_=qsb[slot][:]).then_inc(SM[f"qwb{slot}"].h, 16)
                        SM[f"qwb{slot}"].bump(16)

                        nqu += 1
                    elif ot < NH + NKV:
                        kh = ot - NH

                        @em("vector")
                        def _(e, kh=kh, u=u, t0=t0, wmm=w_batch, wrope=w_rope):
                            e.wait_ge(SM["wqf"].h, wmm)
                            e.wait_ge(SM["cld"].h, wrope)
                            e.drain()
                            psrc = pp[u]
                            cs = cosq[:, t0:t0 + T]
                            sn = sinq[:, t0:t0 + T]
                            pe = psrc[0:64, :]
                            po = psrc[64:128, :]
                            e.tensor_tensor(rsc[0][:], pe, cs, ALU.mult)
                            e.tensor_tensor(rsc[1][:], po, sn, ALU.mult)
                            e.tensor_tensor(rsc[2][:], pe, sn, ALU.mult)
                            e.tensor_tensor(rsc[3][:], po, cs, ALU.mult)
                            e.drain()
                            e.tensor_sub(kfm[0:64, kh * S + t0: kh * S + t0 + T], rsc[0][:], rsc[1][:])
                            e.tensor_add(kfm[64:128, kh * S + t0: kh * S + t0 + T], rsc[2][:], rsc[3][:]).then_inc(SM["qev"].h, 1)
                        w_rp = SM["qev"].bump(1)


                    else:
                        vh = ot - NH - NKV
                        vslot = vh % 2

                        @em("scalar")
                        def _(e, u=u, vslot=vslot, wmm=w_batch, wtv=wv(SM["tvd"], -1)):
                            e.wait_ge(SM["wqf"].h, wmm)
                            if wtv:
                                e.wait_ge(*wtv)
                            e.copy(vst[vslot][:], pp[u][:]).then_inc(SM["qev"].h, 1)
                        w_vst = SM["qev"].bump(1)

                        @em("tensor")
                        def _(e, vslot=vslot, w=w_vst, wvt=wv(SM["vtd"], 0)):
                            e.wait_ge(SM["qev"].h, w)
                            if wvt:
                                e.wait_ge(*wvt)
                            last = None
                            for j in range(4):
                                last = e.transpose(pT[:, j * 128:(j + 1) * 128], vst[vslot][:, j * 128:(j + 1) * 128], idn[:])
                            last.then_inc(SM["tvd"].h, 1)
                        w_tv = SM["tvd"].bump(1)

                        @em("scalar")
                        def _(e, vh=vh, t0=t0, w=w_tv):
                            e.wait_ge(SM["tvd"].h, w)
                            ins = None
                            for j in range(4):
                                base = (t0 // 128 + j) * (NKV * HD) + vh * HD
                                ins = e.copy(vtm[:, base: base + HD], pT[:, j * 128:(j + 1) * 128])
                            ins.then_inc(SM["vtd"].h, 1)
                        SM["vtd"].bump(1)

        w_allq = SM["wqf"].c
        for h in range(NH + NKV):
            par = h % 2
            if h < NH:
                @em("gpsimd")
                def _(e, h=h, par=par, wq=w_allq, wsq=wv(SM["nsq"], -1),
                      wqw=wlist(SM["qwb0"], SM["qwb1"]) if h < 2 else ()):
                    if h < 2:
                        e.wait_ge(SM["wqf"].h, wq)
                    for hvv in wqw:
                        e.wait_ge(*hvv)
                    if wsq:
                        e.wait_ge(*wsq)
                    e.dma_start(out=wqh[par][:, 0:S], in_=qscr[h]).then_inc(SM[f"wqlg{par}"].h, 16)
                w_nl = (SM[f"wqlg{par}"].h, SM[f"wqlg{par}"].bump(16))
            else:
                w_nl = None

            @em("scalar")
            def _(e, h=h, par=par, w_nl=w_nl, wmm=wv(SM["nmm"], -3)):
                if w_nl:
                    e.wait_ge(*w_nl)
                if wmm:
                    e.wait_ge(*wmm)
                src = wqh[par][:, 0:S] if h < NH else kfm[:, (h - NH) * S:(h - NH + 1) * S]
                e.activation(wqh[par][:, S:2 * S], src, AF.Square).then_inc(SM["nsq"].h, 1)
            w_sq = SM["nsq"].bump(1)

            for g in range(4):
                @em("tensor")
                def _(e, par=par, g=g, w=w_sq, wr=wv(SM["nrd"], 0)):
                    if g == 0:
                        e.wait_ge(SM["nsq"].h, w)
                    if wr:
                        e.wait_ge(*wr)
                    e.matmul(pS1[:], ones_cb[:], wqh[par][:, S + g * 512:S + (g + 1) * 512],
                             start=True, stop=True).then_inc(SM["nmm"].h, 1)
                w_nm = SM["nmm"].bump(1)

                @em("vector")
                def _(e, h=h, w=w_nm):
                    e.wait_ge(SM["nmm"].h, w)
                    e.reduce_max(nrm[:], pS1[0:1, 0:T], axis=AX.X)
                    e.drain()
                    if h < NH:
                        e.tensor_max(qn[0:1, h:h + 1], qn[0:1, h:h + 1], nrm[:]).then_inc(SM["nrd"].h, 1)
                    else:
                        e.tensor_max(kn[0:1, h - NH:h - NH + 1], kn[0:1, h - NH:h - NH + 1], nrm[:]).then_inc(SM["nrd"].h, 1)
                SM["nrd"].bump(1)

        @em("vector")
        def _(e, w=SM["nrd"].c):
            e.wait_ge(SM["nrd"].h, w)
            for k in range(NKV):
                for r in range(4):
                    e.tensor_copy(knr[0:1, 4 * k + r:4 * k + r + 1], kn[0:1, k:k + 1])
            e.drain()
            e.tensor_tensor(sm1[:], qn[:], knr[:], ALU.mult).then_inc(SM["nsq"].h, 1)
        w1 = SM["nsq"].bump(1)

        @em("scalar")
        def _(e, w=w1):
            e.wait_ge(SM["nsq"].h, w)
            e.activation(sm1[:], sm1[:], AF.Sqrt).then_inc(SM["nsq"].h, 1)
        w2 = SM["nsq"].bump(1)

        @em("vector")
        def _(e, w=w2):
            e.wait_ge(SM["nsq"].h, w)
            e.tensor_scalar_mul(sm1[:], sm1[:], -RSCL).then_inc(SM["nsq"].h, 1)
        w3 = SM["nsq"].bump(1)

        @em("tensor")
        def _(e, w=w3):
            e.wait_ge(SM["nsq"].h, w)
            e.matmul(pp[0][:, 0:NH], ones_rf[:], sm1[:], start=True, stop=True).then_inc(SM["nmm"].h, 1)
        w4 = SM["nmm"].bump(1)

        @em("scalar")
        def _(e, w=w4):
            e.wait_ge(SM["nmm"].h, w)
            e.copy(negmbc[:], pp[0][:, 0:NH]).then_inc(SM["nsq"].h, 1)
        SM["nsq"].bump(1)
        qes.close()
        phase_entry_waits()

    if "attn" in phases:
        src_fm3 = src_after_f1
        dst_is_out = "f2" not in phases
        aes = ExitStack()
        qt = aes.enter_context(nc.sbuf_tensor("qt", [128, NH * T], BF16))
        ao = aes.enter_context(nc.sbuf_tensor("ao", [128, NH * T], BF16))
        msk = aes.enter_context(nc.sbuf_tensor("msk_s", [128, 4 * T], BF16))
        woh = [aes.enter_context(nc.sbuf_tensor(f"woh{i}", [128, 16 * 640], BF16)) for i in range(2)]
        pex = [aes.enter_context(nc.sbuf_tensor(f"pex{i}", [128, T], BF16)) for i in range(2)]
        rec = srow[0]
        rbc = a_bc
        otm2 = aes.enter_context(nc.sbuf_tensor("otm2", [128, T], F32)) if dst_is_out else None
        if dst_is_out:
            ov2 = out.rearrange("(q j i) (c m) -> q i j c m", i=128, j=4, m=128)

        @em("sync")
        def _(e):
            e.dma_start(out=msk[:], in_=msk_in).then_inc(SM["cld"].h, 16)
        w_msk = SM["cld"].bump(16)
        wov4 = wo_in.rearrange("(cc ci p) col -> p cc ci col", p=128, ci=16)
        woB4 = woB.rearrange("(cc ci p) col -> p cc ci col", p=128, ci=16)
        whalf_done = [0, 0]

        for tq in range(NTT):
            t0 = tq * T
            conv = tq == 0

            @em("gpsimd")
            def _(e, t0=t0, waod=wv(SM["aod"], 0)):
                if waod:
                    e.wait_ge(*waod)
                e.dma_start(out=qt.ap().rearrange("p (h t) -> p h t", t=T),
                            in_=qscr.rearrange("h p t -> p h t")[:, :, t0:t0 + T]).then_inc(SM["qld"].h, 16)
            w_ql = SM["qld"].bump(16)

            nck = (t0 + T) // 128
            for h in range(NH):
                kvh = h // 4
                po = pp[2 + h % 2]
                psr = pS1 if h % 2 == 0 else pS2
                for ck in range(nck):
                    diag = ck >= t0 // 128

                    @em("tensor")
                    def _(e, h=h, ck=ck, kvh=kvh, w_ql=w_ql, wexp=wv(SM["exd"], -1),
                          wqev=wv(SM["qev"], 0) if (h == 0 and ck == 0) else None):
                        if h == 0 and ck == 0:
                            e.wait_ge(SM["qld"].h, w_ql)
                            if wqev:
                                e.wait_ge(*wqev)
                        if wexp:
                            e.wait_ge(*wexp)
                        e.matmul(pp[ck % 2][:], kfm[:, kvh * S + ck * 128: kvh * S + (ck + 1) * 128],
                                 qt[:, h * T:(h + 1) * T], start=True, stop=True).then_inc(SM["smm"].h, 1)
                    SM["smm"].bump(1)

                    @em("scalar")
                    def _(e, h=h, ck=ck, wm=SM["smm"].c, wpv=wv(SM["pvd"], -1), w_msk=w_msk):
                        e.wait_ge(SM["smm"].h, wm)
                        if wpv:
                            e.wait_ge(*wpv)
                        e.wait_ge(SM["cld"].h, w_msk)
                        e.activation(pex[ck % 2][:], pp[ck % 2][:], AF.Exp,
                                     bias=negmbc[:, h:h + 1], scale=RSCL).then_inc(SM["exd"].h, 1)
                    w_ex = SM["exd"].bump(1)

                    if diag:
                        dd = ck - t0 // 128

                        @em("vector")
                        def _(e, ck=ck, dd=dd, w=w_ex):
                            e.wait_ge(SM["exd"].h, w)
                            e.tensor_tensor(pex[ck % 2][:], pex[ck % 2][:], msk[:, dd * T:(dd + 1) * T],
                                            ALU.mult).then_inc(SM["mkd"].h, 1)
                        SM["mkd"].bump(1)

                    @em("tensor")
                    def _(e, ck=ck, nck=nck, kvh=kvh, diag=diag, wex=w_ex, wmk=SM["mkd"].c,
                          po=po, psr=psr,
                          waod=wv(SM["aod"], -1) if ck == 0 else None,
                          wrcd=wv(SM["rcd"], -3) if ck == 0 else None):
                        if diag:
                            e.wait_ge(SM["mkd"].h, wmk)
                        else:
                            e.wait_ge(SM["exd"].h, wex)
                        if ck == 0:
                            if waod:
                                e.wait_ge(*waod)
                            if wrcd:
                                e.wait_ge(*wrcd)
                        e.matmul(po[:], vtm[:, ck * (NKV * HD) + kvh * HD: ck * (NKV * HD) + (kvh + 1) * HD],
                                 pex[ck % 2][:], start=(ck == 0), stop=(ck == nck - 1))
                        e.matmul(psr[:], ones_cb[:], pex[ck % 2][:],
                                 start=(ck == 0), stop=(ck == nck - 1)).then_inc(SM["pvd"].h, 1)
                    SM["pvd"].bump(1)
                w_pv = SM["pvd"].c

                @em("vector")
                def _(e, w=w_pv, psr=psr, wrb=wv(SM["rcd"], -1)):
                    e.wait_ge(SM["pvd"].h, w)
                    if wrb:
                        e.wait_ge(*wrb)
                    e.reciprocal(rec[:], psr[0:1, 0:T]).then_inc(SM["rcd"].h, 1)
                w_rc = SM["rcd"].bump(1)

                @em("tensor")
                def _(e, w=w_rc, wnm=wv(SM["rcd"], -1)):
                    e.wait_ge(SM["rcd"].h, w)
                    if wnm:
                        e.wait_ge(*wnm)
                    e.matmul(pp[4][:, 0:T], ones_rf[:], rec[:], start=True, stop=True).then_inc(SM["nmm"].h, 1)
                w_bc2 = SM["nmm"].bump(1)

                @em("scalar")
                def _(e, w=w_bc2, wao=wv(SM["aod"], 0)):
                    e.wait_ge(SM["nmm"].h, w)
                    if wao:
                        e.wait_ge(*wao)
                    e.copy(rbc[:], pp[4][:, 0:T]).then_inc(SM["rcd"].h, 1)
                w_rbc = SM["rcd"].bump(1)

                @em("vector")
                def _(e, h=h, w=w_rbc, po=po):
                    e.wait_ge(SM["rcd"].h, w)
                    e.tensor_tensor(ao[:, h * T:(h + 1) * T], po[:], rbc[:], ALU.mult).then_inc(SM["aod"].h, 1)
                SM["aod"].bump(1)
            all_heads = SM["aod"].c

            for b in range(7):
                units = list(range(5 if b < 6 else 2))
                wo_base = SM["wqf"].c
                half_waits = []
                for hh in range(2):
                    par = (2 * b + hh) % 2
                    if conv:
                        @em("gpsimd")
                        def _(e, b=b, hh=hh, par=par, nu=len(units), wfree=whalf_done[par], wwb=wv(SM["wwb"], 0)):
                            if wfree >= 1:
                                e.wait_ge(SM["wqf"].h, wfree)
                            if wwb:
                                e.wait_ge(*wwb)
                            nc_ = 128 * nu
                            e.dma_start(out=woh[par].ap().rearrange("p (ci col) -> p ci col", col=640)[:, :, 0:nc_],
                                        in_=wov4[:, hh, :, b * 640:b * 640 + nc_]).then_inc(SM[f"wqlg{par}"].h, 16)
                        wldw = (SM[f"wqlg{par}"].h, SM[f"wqlg{par}"].bump(16))

                        @em("sync")
                        def _(e, b=b, hh=hh, par=par, nu=len(units), wldw=wldw):
                            e.wait_ge(*wldw)
                            nc_ = 128 * nu
                            e.dma_start(out=woB4[:, hh, :, b * 640:b * 640 + nc_],
                                        in_=woh[par].ap().rearrange("p (ci col) -> p ci col", col=640)[:, :, 0:nc_]).then_inc(SM["wwb"].h, 16)
                        SM["wwb"].bump(16)
                    else:
                        @em("sync")
                        def _(e, b=b, hh=hh, par=par, nu=len(units), wfree=whalf_done[par],
                              wwb=wv(SM["wwb"], 0) if (b == 0 and hh == 0 and tq == 1) else None):
                            if wwb:
                                e.wait_ge(*wwb)
                            if wfree >= 1:
                                e.wait_ge(SM["wqf"].h, wfree)
                            nc_ = 128 * nu
                            e.dma_start(out=woh[par].ap().rearrange("p (ci col) -> p ci col", col=640)[:, :, 0:nc_],
                                        in_=woB4[:, hh, :, b * 640:b * 640 + nc_]).then_inc(SM[f"wql{par}"].h, 16)
                        wldw = (SM[f"wql{par}"].h, SM[f"wql{par}"].bump(16))
                    half_waits.append(wldw)
                    whalf_done[par] = wo_base + 16 * (hh + 1)

                for c in range(NCH):
                    par = (2 * b + c // 16) % 2
                    @em("tensor")
                    def _(e, c=c, par=par, units=tuple(units),
                          wldw=half_waits[c // 16] if c % 16 == 0 else None, wah=all_heads,
                          wev=wv(SM["qev"], 0) if c == 0 else None,
                          wex=wv(SM["exd"], 0) if c == 0 else None,
                          wrc=wv(SM["rcd"], 0) if c == 0 else None,
                          wpo=wv(SM["ppd"], 0) if c == 0 else None):
                        if wldw:
                            e.wait_ge(*wldw)
                        if c == 0:
                            e.wait_ge(SM["aod"].h, wah)
                            if wpo:
                                e.wait_ge(*wpo)
                            if wev:
                                e.wait_ge(*wev)
                            if wex:
                                e.wait_ge(*wex)
                            if wrc:
                                e.wait_ge(*wrc)
                        last = None
                        for u in units:
                            last = e.matmul(pp[u][:], woh[par][:, (c % 16) * 640 + u * 128:(c % 16) * 640 + (u + 1) * 128],
                                            ao[:, c * T:(c + 1) * T], start=(c == 0), stop=(c == NCH - 1))
                        last.then_inc(SM["wqf"].h, 1)
                    SM["wqf"].bump(1)
                w_wmm = SM["wqf"].c

                for u in units:
                    ot = b * 5 + u

                    @em("scalar")
                    def _(e, ot=ot, t0=t0, wfree=wv(SM["qev"], -1)):
                        if wfree:
                            e.wait_ge(*wfree)
                        e.dma_start(out=xfs[ot % 2][:], in_=src_fm3[ot][:, t0:t0 + T]).then_inc(SM[f"xfl{ot % 2}"].h, 16)
                    wxl = (SM[f"xfl{ot % 2}"].h, SM[f"xfl{ot % 2}"].bump(16))

                    @em("vector")
                    def _(e, ot=ot, u=u, wmm=w_wmm, wxl=wxl, wowb=wv(SM["owb"], 0)):
                        e.wait_ge(SM["wqf"].h, wmm)
                        e.wait_ge(*wxl)
                        if wowb:
                            e.wait_ge(*wowb)
                        e.tensor_add(ogs[ot % 2][:], pp[u][:], xfs[ot % 2][:]).then_inc(SM["qev"].h, 1)
                    w_ev2 = SM["qev"].bump(1)

                    if not dst_is_out:
                        @em("gpsimd")
                        def _(e, ot=ot, t0=t0, w=w_ev2):
                            e.wait_ge(SM["qev"].h, w)
                            e.dma_start(out=xfmB[ot][:, t0:t0 + T], in_=ogs[ot % 2][:]).then_inc(SM["owb"].h, 16)
                        SM["owb"].bump(16)
                    else:
                        @em("tensor")
                        def _(e, ot=ot, u=u, w=w_ev2, wpo=wv(SM["ppd"], 0)):
                            e.wait_ge(SM["qev"].h, w)
                            if wpo:
                                e.wait_ge(*wpo)
                            last = None
                            for j in range(4):
                                last = e.transpose(pp[u][:, j * 128:(j + 1) * 128],
                                                   ogs[ot % 2][:, j * 128:(j + 1) * 128], idnf[:])
                            last.then_inc(SM["nmm"].h, 1)
                        w_tr = SM["nmm"].bump(1)

                        @em("scalar")
                        def _(e, u=u, w=w_tr, wwb=wv(SM["owb"], 0)):
                            e.wait_ge(SM["nmm"].h, w)
                            if wwb:
                                e.wait_ge(*wwb)
                            e.copy(otm2[:], pp[u][:]).then_inc(SM["ppd"].h, 1)
                        w_ot2 = SM["ppd"].bump(1)

                        @em("gpsimd")
                        def _(e, ot=ot, tq=tq, w=w_ot2):
                            e.wait_ge(SM["ppd"].h, w)
                            e.dma_start(out=ov2[tq, :, :, ot, :],
                                        in_=otm2.ap().rearrange("i (j m) -> i j m", m=128)).then_inc(SM["owb"].h, 16)
                        SM["owb"].bump(16)
        aes.close()
        kv_stack.close()
        phase_entry_waits()

    if "f2" in phases:
        src3 = xfmB if "attn" in phases else xfmI
        stk = ExitStack()
        ffn_phase("b", src3, None, out, "f1" not in phases, stk,
                  (SM["owb"].h, SM["owb"].c) if "attn" not in phases else None)
        stk.close()

    @em("sync")
    def _(e, w=SM["owb"].c):
        e.wait_ge(SM["owb"].h, w)

    with nc.Block() as blk:
        @blk.tensor
        def _(e):
            for f in prog["tensor"]:
                f(e)

        @blk.scalar
        def _(e):
            for f in prog["scalar"]:
                f(e)

        @blk.vector
        def _(e):
            for f in prog["vector"]:
                f(e)

        @blk.gpsimd
        def _(e):
            for f in prog["gpsimd"]:
                f(e)

        @blk.sync
        def _(e):
            for f in prog["sync"]:
                f(e)

    ES.close()
    return nc


def host_inputs(inp):
    wqkv = np.asarray(inp["L0_attn_w_qkv"], dtype=np.float32)
    wo = np.ascontiguousarray(np.asarray(inp["L0_attn_w_o"], dtype=np.float32))
    perm = np.concatenate([np.arange(0, 128, 2), np.arange(1, 128, 2)])
    wq = wqkv.copy()
    for h in range(NH):
        wq[:, h * 128:(h + 1) * 128] = wqkv[:, h * 128 + perm]
    for kh in range(NKV):
        b0 = D + kh * 128
        wq[:, b0:b0 + 128] = wqkv[:, b0 + perm]
    wq = np.ascontiguousarray(wq)
    gT = np.ascontiguousarray(np.asarray(inp["L0_ffn_gate_wq"], dtype=np.int32).T)
    uT = np.ascontiguousarray(np.asarray(inp["L0_ffn_up_wq"], dtype=np.int32).T)
    dT = np.ascontiguousarray(np.asarray(inp["L0_ffn_down_wq"], dtype=np.int32).T)
    gs = np.ascontiguousarray(np.asarray(inp["L0_ffn_gate_scale"], dtype=np.float32).reshape(NI, 128).T)
    us = np.ascontiguousarray(np.asarray(inp["L0_ffn_up_scale"], dtype=np.float32).reshape(NI, 128).T)
    ds = np.ascontiguousarray(np.asarray(inp["L0_ffn_down_scale"], dtype=np.float32).reshape(NCH, 128).T)
    idn = np.eye(128, dtype=ml_dtypes.bfloat16)
    idnf = np.eye(128, dtype=np.float32)
    j = np.arange(64, dtype=np.float64)
    invf = 1.0 / (10000.0 ** (2.0 * j / 128.0))
    t = np.arange(S, dtype=np.float64)
    ang = invf[:, None] * t[None, :]
    cosq = np.cos(ang).astype(np.float32)
    sinq = np.sin(ang).astype(np.float32)
    ii = np.arange(128)[:, None]
    jj = np.arange(T)[None, :]
    msk = np.concatenate([(ii <= jj - dd * 128) for dd in range(4)], axis=1).astype(ml_dtypes.bfloat16)
    return {"gT": gT, "uT": uT, "dT": dT, "wqkv": wq, "wo": wo,
            "gs": gs, "us": us, "ds": ds, "idn": idn, "idnf": idnf,
            "cosq": cosq, "sinq": sinq, "msk": msk}


def kernel(**inp):
    nc = build_nc()
    base = host_inputs(inp)
    xall = np.asarray(inp["L0_attn_x"], dtype=np.float32)
    in_maps = [{**base, "x": np.ascontiguousarray(xall[c])} for c in range(8)]
    res = run_bass_kernel_spmd(nc, in_maps, core_ids=list(range(8)))
    return np.stack([np.asarray(res.results[c]["out"], dtype=np.float32) for c in range(8)], axis=0)
